# version 11
# baseline (speedup 1.0000x reference)
import numpy as np
import concourse.bass as bass
import concourse.mybir as mybir
from concourse.bass_utils import run_bass_kernel_spmd

F32 = mybir.dt.float32
BF16 = mybir.dt.bfloat16
OP = mybir.AluOpType
AF = mybir.ActivationFunctionType
AX = mybir.AxisListType
AP = bass.AP

D = 1024
H = 16
FF = 4096
FI = 2816
NCORES = 8
T = 4096
SLAB = 512
NS = T // SLAB
NCT = SLAB // 128
KC = D // 128
EPS_LN = 1e-5
EPS_Z = 1e-6

ENGS = ("sync", "scalar", "vector", "tensor", "gpsimd")

C_BQ, C_BO, C_B2, C_G1, C_BE1, C_G2, C_B1 = 0, 8, 16, 24, 32, 40, 48
M_SA, M_CA, M_GA, M_SB, M_CB, M_GB, M_CA1, M_CB1 = 0, 8, 16, 24, 32, 40, 48, 56
R_BK, R_BV = 0, D
R_GA, R_GB, R_CB1, R_SB = 0, D, 2 * D, 3 * D

W_COLS = 8192
SL = NCT * D


class Buf:
    __slots__ = ("name", "wr", "rds")

    def __init__(self, name):
        self.name = name
        self.wr = None
        self.rds = []


NPOOL = 12


class CB:

    def __init__(self):
        self.nc = bass.Bass("TRN2", target_bir_lowering=False)
        self.q = {e: [] for e in ENGS}
        self.sem = {e: self.nc.alloc_semaphore(f"S{e}") for e in ENGS}
        for e in ("sync", "gpsimd"):
            for j in range(NPOOL):
                self.sem[f"{e}_d{j}"] = self.nc.alloc_semaphore(f"S{e}d{j}")
        self.sem["cc"] = self.nc.alloc_semaphore("Scc")
        self.cnt = {k: 0 for k in self.sem}
        self.waited = {e: {} for e in ENGS}
        self.pool_i = {"sync": 0, "gpsimd": 0}
        self.last_drain = {}

    def _hazard_waits(self, eng, reads, writes, extra=()):
        waits = list(extra)
        for b in reads:
            if b.wr is not None:
                waits.append(b.wr)
        for b in writes:
            if b.wr is not None:
                waits.append(b.wr)
            waits.extend(b.rds)
        wmap = {}
        for (key, val) in waits:
            if key == eng:
                continue
            if wmap.get(key, -1) < val:
                wmap[key] = val
        final = []
        for key, val in wmap.items():
            if self.waited[eng].get(key, -1) < val:
                self.waited[eng][key] = val
                final.append((self.sem[key], val))
        return final

    def _finish(self, eng, tok, reads, writes):
        for b in writes:
            b.wr = tok
            b.rds = []
        for b in reads:
            b.rds.append(tok)
        return tok

    def _emit(self, eng, f, reads, writes, inc, drain, semkey=None):
        semkey = semkey or eng
        need_drain = False
        if drain:
            last = self.last_drain.get(eng, 0)
            dep = 0
            for b in reads:
                if b.wr is not None and b.wr[0] == semkey:
                    dep = max(dep, b.wr[1])
            for b in writes:
                if b.wr is not None and b.wr[0] == semkey:
                    dep = max(dep, b.wr[1])
                for t in b.rds:
                    if t[0] == semkey:
                        dep = max(dep, t[1])
            if dep > last:
                need_drain = True
        final = self._hazard_waits(eng, reads, writes)
        sem = self.sem[semkey]

        def g(e, final=final, f=f, inc=inc, nd=need_drain, sem=sem):
            if nd:
                e.drain()
            for s, v in final:
                e.wait_ge(s, v)
            r = f(e)
            if inc:
                r.then_inc(sem, inc)

        self.q[eng].append(g)
        self.cnt[semkey] += inc
        if need_drain:
            self.last_drain[eng] = self.cnt[semkey] - inc
        return self._finish(eng, (semkey, self.cnt[semkey]), reads, writes)

    def _dma(self, eng, f, reads, writes):
        j = self.pool_i[eng]
        self.pool_i[eng] = (j + 1) % NPOOL
        key = f"{eng}_d{j}"
        guard = [(key, self.cnt[key])] if self.cnt[key] else []
        final = self._hazard_waits(eng, reads, writes, extra=guard)
        sem = self.sem[key]

        def g(e, final=final, f=f, sem=sem):
            for s, v in final:
                e.wait_ge(s, v)
            f(e).then_inc(sem, 16)

        self.q[eng].append(g)
        self.cnt[key] += 16
        return self._finish(eng, (key, self.cnt[key]), reads, writes)

    def v(self, f, rd=(), wr=()):
        return self._emit("vector", f, rd, wr, 1, True)

    def p(self, f, rd=(), wr=()):
        return self._emit("gpsimd", f, rd, wr, 1, True)

    def a(self, f, rd=(), wr=()):
        return self._emit("scalar", f, rd, wr, 1, True)

    def t(self, f, rd=(), wr=()):
        return self._emit("tensor", f, rd, wr, 1, False)

    def dml(self, f, rd=(), wr=()):
        return self._dma("sync", f, rd, wr)

    def dms(self, f, rd=(), wr=()):
        return self._dma("gpsimd", f, rd, wr)


def ap3(handle, offset, dims):
    return AP(handle, offset, [list(d) for d in dims])


def build_program(debug_names=(), skip_collective=False, bc2_stripes=8, stop_after=None):
    cb = CB()
    nc = cb.nc
    dbg = set(debug_names)

    def dram(name, shape, dt=F32):
        kind = "ExternalOutput" if name in dbg else "Internal"
        return nc.dram_tensor(name, list(shape), dt, kind=kind)

    xin = nc.dram_tensor("xin", [T, D], F32, kind="ExternalInput")
    cfm_in = nc.dram_tensor("cfm", [128, 8], F32, kind="ExternalInput")
    wmod_in = nc.dram_tensor("wmod", [D, 6 * D], F32, kind="ExternalInput")
    wq_in = nc.dram_tensor("wq", [D, D], F32, kind="ExternalInput")
    wk_in = nc.dram_tensor("wk", [D, D], F32, kind="ExternalInput")
    wv_in = nc.dram_tensor("wv", [D, D], F32, kind="ExternalInput")
    wo_in = nc.dram_tensor("wo", [D, D], F32, kind="ExternalInput")
    w1_in = nc.dram_tensor("w1", [D, FF], F32, kind="ExternalInput")
    w2_in = nc.dram_tensor("w2", [FF, D], F32, kind="ExternalInput")
    wg_in = nc.dram_tensor("wg", [FI, D], F32, kind="ExternalInput")
    wu_in = nc.dram_tensor("wu", [FI, D], F32, kind="ExternalInput")
    wd_in = nc.dram_tensor("wd", [D, FI], F32, kind="ExternalInput")
    bq_fm = nc.dram_tensor("bq_fm", [128, 8], F32, kind="ExternalInput")
    bk_row = nc.dram_tensor("bk_row", [1, D], F32, kind="ExternalInput")
    bv_row = nc.dram_tensor("bv_row", [1, D], F32, kind="ExternalInput")
    bo_fm = nc.dram_tensor("bo_fm", [128, 8], F32, kind="ExternalInput")
    b1_fm = nc.dram_tensor("b1_fm", [128, 32], F32, kind="ExternalInput")
    b2_fm = nc.dram_tensor("b2_fm", [128, 8], F32, kind="ExternalInput")
    g1_fm = nc.dram_tensor("g1_fm", [128, 8], F32, kind="ExternalInput")
    be1_fm = nc.dram_tensor("be1_fm", [128, 8], F32, kind="ExternalInput")
    g2_fm = nc.dram_tensor("g2_fm", [128, 8], F32, kind="ExternalInput")
    be2_fm = nc.dram_tensor("be2_fm", [128, 8], F32, kind="ExternalInput")
    g2_row = nc.dram_tensor("g2_row", [1, D], F32, kind="ExternalInput")
    be2_row = nc.dram_tensor("be2_row", [1, D], F32, kind="ExternalInput")
    ident_in = nc.dram_tensor("ident", [128, 128], F32, kind="ExternalInput")
    out = nc.dram_tensor("out", [T, D], F32, kind="ExternalOutput")

    h_hi_d = dram("h_hi_d", [NS, 128, SL], BF16)
    h_lo_d = dram("h_lo_d", [NS, 128, SL], BF16)
    q_d = dram("q_d", [NS, 128, SL], F32)
    x1_hi_d = dram("x1_hi_d", [NS, 128, SL], BF16)
    x1_lo_d = dram("x1_lo_d", [NS, 128, SL], BF16)
    xq_hi_d = dram("xq_hi_d", [NS, 128, SL], BF16)
    xq_lo_d = dram("xq_lo_d", [NS, 128, SL], BF16)
    ya_d = dram("ya_d", [NS, 128, SL], F32)
    yb_d = dram("yb_d", [NS, 128, SL], F32)
    x2_d = dram("x2_d", [NS, 128, SL], F32)
    y2a_d = dram("y2a_d", [NS, 128, SL], F32)
    y2b_d = dram("y2b_d", [NS, 128, SL], F32)
    garow_d = dram("garow_d", [4, D], F32)
    w4_d = dram("w4_d", [3, 6, 128, 4096], F32)
    w4b_d = dram("w4b_d", [36, 128, 4096], BF16)
    w12_d = dram("w12_d", [8, 2, 128, W_COLS], BF16)
    cc_in = dram("cc_in", [128, 8 * 65], F32)
    cc_out = dram("cc_out", [128, 8 * 65], F32)

    B = {}

    def sb(name, cols, dt=F32, parts=128):
        h = nc.alloc_sbuf_tensor(name, [parts, cols], dt)
        B[name] = Buf(name)
        return h

    ident = sb("ident_s", 128)
    onesr = sb("onesr", 128, parts=1)
    onescol = sb("onescol", 1)
    cfm = sb("cfm_s", 8)
    silc = sb("silc", 8)
    modv = sb("modv", 64)
    sgn8 = sb("sgn8", 16)
    vecs = sb("vecs", 80)
    be2s = sb("be2s", 8)
    r6c = sb("r6c", 2)
    m1b = sb("m1b", 32)
    r1b = sb("r1b", 32)
    m4b = sb("m4b", 4)
    r4b = sb("r4b", 4)
    minis = sb("minis", 32)
    zb = sb("zb", 64)
    kvexp = sb("kvexp", 8 * 130)
    rep = sb("rep", 4 * D)
    xb = sb("xb", SL)
    tb = sb("tb", SL)
    kb = sb("kb", SL)
    qb = sb("qb", SL)
    a2b = sb("a2b", SL)
    x1b = sb("x1b", SL)
    hhi = sb("hhi", SL, BF16)
    hlo = sb("hlo", SL, BF16)
    wa_hi = sb("wa_hi", W_COLS, BF16)
    wa_lo = sb("wa_lo", W_COLS, BF16)
    wb_hi = sb("wb_hi", W_COLS, BF16)
    wb_lo = sb("wb_lo", W_COLS, BF16)
    ub = sb("ub", SLAB)
    t1s = sb("t1s", SLAB)
    t2s = sb("t2s", SLAB)

    pb = [nc.alloc_psum_tensor(f"pbk{i}", [128, 512], F32) for i in range(8)]
    PB = [Buf(f"pb{i}") for i in range(8)]
    Bcc = Buf("cc")
    Bw4 = Buf("w4")
    WBH = [Buf(f"wbh{i}") for i in range(3)]
    WBL = [Buf(f"wbl{i}") for i in range(3)]
    WAH = [Buf(f"wah{i}") for i in range(2)]
    WAL = [Buf(f"wal{i}") for i in range(2)]
    Bw12 = [Buf(f"w12_{i}") for i in range(8)]
    Bw4b = [Buf(f"w4b_{i}") for i in range(18)]
    Bgarow = Buf("garow")
    SP = {k: Buf("sp_" + k) for k in
          ("h", "q", "x1", "xq", "ya", "yb", "x2", "y2a", "y2b", "out")}

    def vcol(i):
        return vecs[:, i:i + 1]

    cb.dml(lambda e: e.dma_start(out=ident[:, :], in_=ident_in.ap()[:, :]),
           wr=[B["ident_s"]])
    cb.dml(lambda e: e.dma_start(out=cfm[:, :], in_=cfm_in.ap()[:, :]),
           wr=[B["cfm_s"]])
    for i, src in enumerate((bq_fm, bo_fm, b2_fm, g1_fm, be1_fm, g2_fm)):
        cb.dml(lambda e, i=i, src=src: e.dma_start(
            out=vecs[:, i * 8:(i + 1) * 8], in_=src.ap()[:, :]),
            wr=[B["vecs"]])
    cb.dml(lambda e: e.dma_start(out=vecs[:, 48:80], in_=b1_fm.ap()[:, :]),
           wr=[B["vecs"]])
    cb.dml(lambda e: e.dma_start(out=be2s[:, :], in_=be2_fm.ap()[:, :]),
           wr=[B["be2s"]])
    cb.v(lambda e: e.memset(r6c[:, 0:1], 6.0), wr=[B["r6c"]])
    cb.v(lambda e: e.reciprocal(r6c[:, 0:1], r6c[:, 0:1]), rd=[B["r6c"]],
         wr=[B["r6c"]])
    cb.v(lambda e: e.memset(r6c[:, 1:2], EPS_LN), wr=[B["r6c"]])
    cb.v(lambda e: e.memset(onesr[:, :], 1.0), wr=[B["onesr"]])
    cb.v(lambda e: e.memset(onescol[:, :], 1.0), wr=[B["onescol"]])

    cb.a(lambda e: e.activation(silc[:, :], cfm[:, :], AF.Silu),
         rd=[B["cfm_s"]], wr=[B["silc"]])
    for g in range(12):
        def ldw(e, g=g):
            o = ap3(xb, 0, [[SL, 128], [512, 8], [1, 512]])
            i = ap3(wmod_in, g * 512,
                    [[6 * D, 128], [128 * 6 * D, 8], [1, 512]])
            return e.dma_start(out=o, in_=i)
        cb.dml(ldw, wr=[B["xb"]])

        def mm(e, g=g):
            r = None
            for mc in range(4):
                for kc in range(KC):
                    r = e.matmul(
                        pb[0][:, g * 4 + mc:g * 4 + mc + 1],
                        xb[:, kc * 512 + mc * 128:kc * 512 + (mc + 1) * 128],
                        silc[:, kc:kc + 1],
                        start=(kc == 0), stop=(kc == KC - 1))
            return r
        cb.t(mm, rd=[B["xb"], B["silc"]], wr=[PB[0]])
    cb.v(lambda e: e.tensor_copy(modv[:, 0:48], pb[0][:, 0:48]),
         rd=[PB[0]], wr=[B["modv"]])
    cb.v(lambda e: e.tensor_scalar_add(modv[:, M_CA1:M_CA1 + 8],
                                       modv[:, M_CA:M_CA + 8], 1.0),
         rd=[B["modv"]], wr=[B["modv"]])
    cb.v(lambda e: e.tensor_scalar_add(modv[:, M_CB1:M_CB1 + 8],
                                       modv[:, M_CB:M_CB + 8], 1.0),
         rd=[B["modv"]], wr=[B["modv"]])
    for j, col in enumerate((M_GA, M_GB)):
        cb.v(lambda e, j=j, col=col: e.tensor_scalar(
            sgn8[:, j * 8:(j + 1) * 8], modv[:, col:col + 8], 0.0, 2.0,
            OP.is_ge, OP.mult), rd=[B["modv"]], wr=[B["sgn8"]])
        cb.v(lambda e, j=j: e.tensor_scalar_sub(
            sgn8[:, j * 8:(j + 1) * 8], sgn8[:, j * 8:(j + 1) * 8], 1.0),
            rd=[B["sgn8"]], wr=[B["sgn8"]])
        cb.a(lambda e, col=col: e.activation(
            modv[:, col:col + 8], modv[:, col:col + 8], AF.Abs),
            rd=[B["modv"]], wr=[B["modv"]])
        cb.v(lambda e, col=col: e.tensor_scalar_max(
            modv[:, col:col + 8], modv[:, col:col + 8], 0.25),
            rd=[B["modv"]], wr=[B["modv"]])
        cb.v(lambda e, j=j, col=col: e.tensor_mul(
            modv[:, col:col + 8], modv[:, col:col + 8],
            sgn8[:, j * 8:(j + 1) * 8]),
            rd=[B["modv"], B["sgn8"]], wr=[B["modv"]])

    for j, col in enumerate((M_GA, M_GB, M_CB1, M_SB)):
        def st(e, j=j, col=col):
            o = ap3(garow_d, j * D, [[1, 128], [128, 8]])
            return e.dma_start(out=o, in_=modv[:, col:col + 8],
                               allow_slow_non_contiguous=True)
        cb.dms(st, rd=[B["modv"]], wr=[Bgarow])

    def replicate(dst_col, src_dram, src_off):
        def f(e):
            o = ap3(rep, dst_col, [[4 * D, 128], [1, D]])
            i = ap3(src_dram, src_off, [[0, 128], [1, D]])
            return e.dma_start(out=o, in_=i)
        cb.dml(f, rd=[Bgarow], wr=[B["rep"]])

    replicate(R_BK, bk_row, 0)
    replicate(R_BV, bv_row, 0)

    def _bl(b):
        return list(b) if isinstance(b, (list, tuple)) else [b]

    def load_split(w_dram, wcols, row0, col0, nrows, ncols, hi, lo, hib, lob,
                   dcol0=0):
        hib, lob = _bl(hib), _bl(lob)
        nkc = nrows // 128

        def cast(e):
            o = ap3(hi, dcol0, [[W_COLS, 128], [ncols, nkc], [1, ncols]])
            i = ap3(w_dram, row0 * wcols + col0,
                    [[wcols, 128], [128 * wcols, nkc], [1, ncols]])
            return e.dma_start(out=o, in_=i)
        cb.dms(cast, wr=hib)
        hkc = max(1, nkc // 2)
        nhalf = (nkc + hkc - 1) // hkc
        for hh in range(nhalf):
            def full(e, hh=hh):
                o = ap3(xb, 0, [[SL, 128], [ncols, hkc], [1, ncols]])
                i = ap3(w_dram, (row0 + hh * hkc * 128) * wcols + col0,
                        [[wcols, 128], [128 * wcols, hkc], [1, ncols]])
                return e.dma_start(out=o, in_=i)
            cb.dml(full, wr=[B["xb"]])
            cb.v(lambda e, hh=hh: e.tensor_sub(
                ap3(lo, dcol0 + hh * hkc * ncols,
                    [[W_COLS, 128], [1, hkc * ncols]]),
                ap3(xb, 0, [[SL, 128], [1, hkc * ncols]]),
                ap3(hi, dcol0 + hh * hkc * ncols,
                    [[W_COLS, 128], [1, hkc * ncols]])),
                rd=[B["xb"]] + hib, wr=lob)

    load_split(wq_in, D, 0, 0, D, 1024, wa_hi, wa_lo, B["wa_hi"], B["wa_lo"])

    cb.v(lambda e: e.memset(qb[:, 0:520], 0.0), wr=[B["qb"]])
    cb.v(lambda e: e.memset(kvexp[:, :], 0.0), wr=[B["kvexp"]])

    def ln_stats_tm(src, src_b, sqout, sqout_b, mdst, mdst_b, rdst, rdst_b,
                    col0, eps):
        for c in range(NCT):
            cb.v(lambda e, c=c: e.tensor_reduce(
                minis[:, c:c + 1], src[:, c * D:(c + 1) * D], AX.X, OP.add),
                rd=[src_b], wr=[B["minis"]])
            cb.a(lambda e, c=c: e.activation(
                sqout[:, c * D:(c + 1) * D], src[:, c * D:(c + 1) * D],
                AF.Square, accum_out=minis[:, 4 + c:5 + c]),
                rd=[src_b], wr=[sqout_b, B["minis"]])
        cb.v(lambda e: e.tensor_scalar_mul(
            mdst[:, col0:col0 + NCT], minis[:, 0:NCT], 1.0 / D),
            rd=[B["minis"]], wr=[mdst_b])
        cb.v(lambda e: e.tensor_mul(
            minis[:, 8:8 + NCT], mdst[:, col0:col0 + NCT],
            mdst[:, col0:col0 + NCT]), rd=[mdst_b], wr=[B["minis"]])
        cb.v(lambda e: e.scalar_tensor_tensor(
            minis[:, 12:12 + NCT], minis[:, 4:4 + NCT], 1.0 / D,
            minis[:, 8:8 + NCT], OP.mult, OP.subtract),
            rd=[B["minis"]], wr=[B["minis"]])
        cb.a(lambda e: e.activation(
            minis[:, 16:16 + NCT], minis[:, 12:12 + NCT], AF.Sqrt,
            bias=r6c[:, 1:2]),
            rd=[B["minis"], B["r6c"]], wr=[B["minis"]])
        cb.v(lambda e: e.reciprocal(
            rdst[:, col0:col0 + NCT], minis[:, 16:16 + NCT]),
            rd=[B["minis"]], wr=[rdst_b])

    def ln_apply_tm(src, src_b, dst, dst_b, mdst, mdst_b, rdst, rdst_b, col0,
                    rw):
        for c in range(NCT):
            def f(e, c=c):
                r_b = ap3(rdst, col0 + c, [[rw, 128], [0, D]])
                return e.scalar_tensor_tensor(
                    dst[:, c * D:(c + 1) * D], src[:, c * D:(c + 1) * D],
                    mdst[:, col0 + c:col0 + c + 1], r_b, OP.subtract, OP.mult)
            cb.v(f, rd=[src_b, mdst_b, rdst_b], wr=[dst_b])

    def tr_tm_to_fm(src, src_b, dsthi, dsthi_b, dstlo, dstlo_b, banks,
                    affine=None, aff_tmp=None, aff_tmp_b=None, dcol0=0,
                    extra_wr_hi=(), extra_wr_lo=()):
        for f in range(KC):
            bk = banks[f % 2]
            pbk = pb[bk]

            def tr(e, f=f, pbk=pbk):
                r = None
                for c in range(NCT):
                    r = e.transpose(
                        pbk[:, c * 128:(c + 1) * 128],
                        src[:, c * D + f * 128:c * D + (f + 1) * 128],
                        ident[:, :])
                return r
            cb.t(tr, rd=[src_b, B["ident_s"]], wr=[PB[bk]])
            if affine is not None:
                acol, scol = affine

                def ev(e, f=f, pbk=pbk, acol=acol, scol=scol):
                    sa_b = ap3(modv, scol + f, [[64, 128], [0, SLAB]])
                    return e.scalar_tensor_tensor(
                        aff_tmp[:, f * SLAB:(f + 1) * SLAB], pbk[:, 0:SLAB],
                        modv[:, acol + f:acol + f + 1], sa_b, OP.mult, OP.add)
                cb.v(ev, rd=[PB[bk], B["modv"]], wr=[aff_tmp_b])
                cb.a(lambda e, f=f: e.activation(
                    dsthi[:, dcol0 + f * SLAB:dcol0 + (f + 1) * SLAB],
                    aff_tmp[:, f * SLAB:(f + 1) * SLAB], AF.Identity),
                    rd=[aff_tmp_b], wr=[dsthi_b])
                cb.v(lambda e, f=f: e.tensor_sub(
                    dstlo[:, dcol0 + f * SLAB:dcol0 + (f + 1) * SLAB],
                    aff_tmp[:, f * SLAB:(f + 1) * SLAB],
                    dsthi[:, dcol0 + f * SLAB:dcol0 + (f + 1) * SLAB]),
                    rd=[aff_tmp_b, dsthi_b], wr=[dstlo_b])
            else:
                cb.a(lambda e, f=f, pbk=pbk: e.activation(
                    dsthi[:, dcol0 + f * SLAB:dcol0 + (f + 1) * SLAB],
                    pbk[:, 0:SLAB],
                    AF.Identity), rd=[PB[bk]],
                    wr=[dsthi_b] + list(extra_wr_hi))
                cb.v(lambda e, f=f, pbk=pbk: e.tensor_sub(
                    dstlo[:, dcol0 + f * SLAB:dcol0 + (f + 1) * SLAB],
                    pbk[:, 0:SLAB],
                    dsthi[:, dcol0 + f * SLAB:dcol0 + (f + 1) * SLAB]),
                    rd=[PB[bk], dsthi_b], wr=[dstlo_b] + list(extra_wr_lo))

    def mm3_fm(bank, lhs_hi, lhs_lo, lhsb, rhs_hi, rhs_lo, rhsb, fcol, wcol0,
               wcols, nkc=KC, rcol0=0, rstep=SLAB):
        pbk = pb[bank]

        def mm(e):
            r = None
            n = 0
            for (wsrc, xsrc) in ((lhs_hi, rhs_hi), (lhs_hi, rhs_lo),
                                 (lhs_lo, rhs_hi)):
                for kc in range(nkc):
                    r = e.matmul(
                        pbk[:, 0:SLAB],
                        wsrc[:, wcol0 + kc * wcols + fcol * 128:
                             wcol0 + kc * wcols + (fcol + 1) * 128],
                        xsrc[:, rcol0 + kc * rstep:rcol0 + kc * rstep + SLAB],
                        start=(n == 0), stop=(n == 3 * nkc - 1))
                    n += 1
            return r
        rds = []
        for b in (*lhsb, *rhsb):
            rds.extend(_bl(b))
        return cb.t(mm, rd=rds, wr=[PB[bank]])

    def fm_stats(srcbuf, src_b, tmp, tmp_b, eps):
        def f1(e):
            r = None
            for kc in range(KC):
                r = e.matmul(pb[7][0:1, 0:SLAB], onescol[:, :],
                             srcbuf[:, kc * SLAB:(kc + 1) * SLAB],
                             start=(kc == 0), stop=(kc == KC - 1))
            return r
        cb.t(f1, rd=[src_b, B["onescol"]], wr=[PB[7]])
        cb.a(lambda e: e.activation(tmp[:, :], srcbuf[:, :], AF.Square),
             rd=[src_b], wr=[tmp_b])

        def f2(e):
            r = None
            for kc in range(KC):
                r = e.matmul(pb[6][0:1, 0:SLAB], onescol[:, :],
                             tmp[:, kc * SLAB:(kc + 1) * SLAB],
                             start=(kc == 0), stop=(kc == KC - 1))
            return r
        cb.t(f2, rd=[tmp_b, B["onescol"]], wr=[PB[6]])
        cb.v(lambda e: e.tensor_scalar_mul(ub[0:1, 0:SLAB],
                                           pb[7][0:1, 0:SLAB], 1.0 / D),
             rd=[PB[7]], wr=[B["ub"]])
        cb.v(lambda e: e.tensor_mul(t1s[0:1, 0:SLAB], ub[0:1, 0:SLAB],
                                    ub[0:1, 0:SLAB]),
             rd=[B["ub"]], wr=[B["t1s"]])
        cb.v(lambda e: e.scalar_tensor_tensor(
            t2s[0:1, 0:SLAB], pb[6][0:1, 0:SLAB], 1.0 / D, t1s[0:1, 0:SLAB],
            OP.mult, OP.subtract), rd=[PB[6], B["t1s"]], wr=[B["t2s"]])
        cb.a(lambda e: e.activation(t1s[0:1, 0:SLAB], t2s[0:1, 0:SLAB],
                                    AF.Sqrt, bias=r6c[0:1, 1:2]),
             rd=[B["t2s"], B["r6c"]], wr=[B["t1s"]])
        cb.v(lambda e: e.reciprocal(t2s[0:1, 0:SLAB], t1s[0:1, 0:SLAB]),
             rd=[B["t1s"]], wr=[B["t2s"]])
        cb.v(lambda e: e.tensor_mul(t1s[0:1, 0:SLAB], ub[0:1, 0:SLAB],
                                    t2s[0:1, 0:SLAB]),
             rd=[B["ub"], B["t2s"]], wr=[B["t1s"]])

    def fm_apply(srcbuf, src_b, gcol, bvec, bvec_b, bvw, bcol0, dst, dst_b,
                 tmp, tmp_b):
        cb.t(lambda e: e.matmul(pb[7][:, 0:SLAB], onesr[:, :],
                                t2s[0:1, 0:SLAB], start=True, stop=True),
             rd=[B["t2s"], B["onesr"]], wr=[PB[7]])
        cb.t(lambda e: e.matmul(pb[6][:, 0:SLAB], onesr[:, :],
                                t1s[0:1, 0:SLAB], start=True, stop=True),
             rd=[B["t1s"], B["onesr"]], wr=[PB[6]])
        for f in range(KC):
            cb.v(lambda e, f=f: e.tensor_mul(
                tmp[:, f * SLAB:(f + 1) * SLAB],
                srcbuf[:, f * SLAB:(f + 1) * SLAB], pb[7][:, 0:SLAB]),
                rd=[src_b, PB[7]], wr=[tmp_b])
            cb.v(lambda e, f=f: e.tensor_sub(
                tmp[:, f * SLAB:(f + 1) * SLAB],
                tmp[:, f * SLAB:(f + 1) * SLAB], pb[6][:, 0:SLAB]),
                rd=[tmp_b, PB[6]], wr=[tmp_b])

            def f3(e, f=f):
                b_b = ap3(bvec, bcol0 + f, [[bvw, 128], [0, SLAB]])
                return e.scalar_tensor_tensor(
                    dst[:, f * SLAB:(f + 1) * SLAB],
                    tmp[:, f * SLAB:(f + 1) * SLAB],
                    vcol(gcol + f), b_b, OP.mult, OP.add)
            cb.v(f3, rd=[tmp_b, B["vecs"], bvec_b], wr=[dst_b])

    def fm_ln(srcbuf, src_b, gcol, bvec, bvec_b, bvw, bcol0, dst, dst_b, eps,
              tmp, tmp_b):
        fm_stats(srcbuf, src_b, tmp, tmp_b, eps)
        fm_apply(srcbuf, src_b, gcol, bvec, bvec_b, bvw, bcol0, dst, dst_b,
                 tmp, tmp_b)

    def quant_tm(src, src_b, sw, dst, dst_b, dw, col0, F,
                 s1, s1b, s1w, c1, s2, s2b, s2w, c2, s3, s3b, s3w, c3,
                 sc_off=0, pool=False):
        V = cb.p if pool else cb.v
        NB = F // 16
        sA = ap3(src, col0, [[sw, 128], [16, NB], [1, 16]])
        cb.v(lambda e: e.tensor_reduce(ub[:, sc_off:sc_off + NB], sA, AX.X,
                                       OP.max, apply_absolute_value=True),
             rd=[src_b], wr=[B["ub"]])
        cb.v(lambda e: e.tensor_scalar(ub[:, 256 + sc_off:256 + sc_off + NB],
                                       ub[:, sc_off:sc_off + NB],
                                       r6c[:, 0:1], 1e-12, OP.mult, OP.max),
             rd=[B["ub"], B["r6c"]], wr=[B["ub"]])
        cb.v(lambda e: e.reciprocal(ub[:, sc_off:sc_off + NB],
                                    ub[:, 256 + sc_off:256 + sc_off + NB]),
             rd=[B["ub"]], wr=[B["ub"]])
        vf = ap3(s1, c1, [[s1w, 128], [1, F]])
        v3 = ap3(s1, c1, [[s1w, 128], [16, NB], [1, 16]])
        if pool:
            V(lambda e: e.tensor_single_scalar(
                vf, ap3(src, col0, [[sw, 128], [1, F]]), 0.0, OP.abs_max),
              rd=[src_b], wr=[s1b])
        else:
            cb.a(lambda e: e.activation(
                vf, ap3(src, col0, [[sw, 128], [1, F]]), AF.Abs),
                rd=[src_b], wr=[s1b])
        rsb = ap3(ub, sc_off, [[SLAB, 128], [1, NB], [0, 16]])
        V(lambda e: e.tensor_tensor(v3, v3, rsb, OP.mult),
          rd=[s1b, B["ub"]], wr=[s1b])
        tf = ap3(s2, c2, [[s2w, 128], [1, F]])
        kf = ap3(s3, c3, [[s3w, 128], [1, F]])
        V(lambda e: e.tensor_single_scalar(tf, vf, 0.25, OP.is_gt),
          rd=[s1b], wr=[s2b])
        for thr in (0.75, 1.25, 1.75):
            V(lambda e, thr=thr: e.scalar_tensor_tensor(
                tf, vf, thr, tf, OP.is_gt, OP.add), rd=[s1b, s2b], wr=[s2b])
        V(lambda e: e.tensor_single_scalar(kf, vf, 2.5, OP.is_gt),
          rd=[s1b], wr=[s3b])
        for thr in (3.5, 5.0, 5.0):
            V(lambda e, thr=thr: e.scalar_tensor_tensor(
                kf, vf, thr, kf, OP.is_gt, OP.add), rd=[s1b, s3b], wr=[s3b])
        V(lambda e: e.scalar_tensor_tensor(tf, tf, 0.5, kf, OP.mult, OP.add),
          rd=[s2b, s3b], wr=[s2b])
        if pool:
            V(lambda e: e.tensor_single_scalar(
                kf, ap3(src, col0, [[sw, 128], [1, F]]), 0.0, OP.is_lt),
              rd=[src_b], wr=[s3b])
            V(lambda e: e.tensor_single_scalar(
                vf, ap3(src, col0, [[sw, 128], [1, F]]), 0.0, OP.is_gt),
              rd=[src_b], wr=[s1b])
            V(lambda e: e.tensor_tensor(vf, vf, kf, OP.subtract),
              rd=[s1b, s3b], wr=[s1b])
        else:
            cb.a(lambda e: e.activation(
                vf, ap3(src, col0, [[sw, 128], [1, F]]), AF.Sign),
                rd=[src_b], wr=[s1b])
        V(lambda e: e.tensor_tensor(tf, tf, vf, OP.mult),
          rd=[s2b, s1b], wr=[s2b])
        t3 = ap3(s2, c2, [[s2w, 128], [16, NB], [1, 16]])
        d3 = ap3(dst, col0, [[dw, 128], [16, NB], [1, 16]])
        ssb = ap3(ub, 256 + sc_off, [[SLAB, 128], [1, NB], [0, 16]])
        V(lambda e: e.tensor_tensor(d3, t3, ssb, OP.mult),
          rd=[s2b, B["ub"]], wr=[dst_b])

    def prep_fp4_weight(w_dram, wcols, i0, isz, along_rows, mat, stripe,
                        thunks):
        if along_rows:
            nrc, ncols = isz // 128, wcols
            off = i0 * wcols
        else:
            nrc, ncols = D // 128, isz
            off = i0

        def ld(e):
            o = ap3(x1b, 0, [[SL, 128], [ncols, nrc], [1, ncols]])
            i = ap3(w_dram, off, [[wcols, 128], [128 * wcols, nrc],
                                  [1, ncols]])
            return e.dma_start(out=o, in_=i)
        thunks.append(lambda: cb.dml(ld, wr=[B["x1b"]]))
        total = nrc * ncols
        for p0 in range(0, total, 1024):
            pf = min(1024, total - p0)
            thunks.append(lambda p0=p0, pf=pf: quant_tm(
                x1b, B["x1b"], SL, x1b, B["x1b"], SL, p0, pf,
                a2b, B["a2b"], SL, 0, a2b, B["a2b"], SL, 1024,
                a2b, B["a2b"], SL, 2048, sc_off=64))
        nkc = ncols // 128
        for kc in range(nkc):
            for grp in range((nrc + 3) // 4):
                bank = 5 + grp % 2
                pbk = pb[bank]
                cnt = min(4, nrc - grp * 4)

                def tr(e, kc=kc, grp=grp, cnt=cnt, pbk=pbk):
                    r = None
                    for j in range(cnt):
                        rc = grp * 4 + j
                        r = e.transpose(
                            pbk[:, j * 128:(j + 1) * 128],
                            x1b[:, rc * ncols + kc * 128:
                                rc * ncols + (kc + 1) * 128],
                            ident[:, :])
                    return r

                def group(tr=tr, bank=bank, pbk=pbk, cnt=cnt,
                          kc=kc, grp=grp, mat=mat, stripe=stripe, nrc=nrc):
                    cb.t(tr, rd=[B["x1b"], B["ident_s"]], wr=[PB[bank]])
                    cb.v(lambda e, cnt=cnt, pbk=pbk: e.tensor_copy(
                        a2b[:, 3072:3072 + cnt * 128], pbk[:, 0:cnt * 128]),
                        rd=[PB[bank]], wr=[B["a2b"]])
                    dcol = kc * (nrc * 128) + grp * 512

                    def stw(e, mat=mat, stripe=stripe, dcol=dcol, cnt=cnt):
                        o = ap3(w4_d, (mat * 6 + stripe) * 128 * 4096 + dcol,
                                [[4096, 128], [1, cnt * 128]])
                        return e.dma_start(out=o,
                                           in_=a2b[:, 3072:3072 + cnt * 128])
                    cb.dms(stw, rd=[B["a2b"]], wr=[Bw4])
                thunks.append(group)

    STRIPES4 = [(i, min(512, FI - i)) for i in range(0, FI, 512)]
    prep_thunks = []
    for st4, (i0_, isz_) in enumerate(STRIPES4):
        prep_fp4_weight(wg_in, D, i0_, isz_, True, 0, st4, prep_thunks)
        prep_fp4_weight(wu_in, D, i0_, isz_, True, 1, st4, prep_thunks)
        prep_fp4_weight(wd_in, FI, i0_, isz_, False, 2, st4, prep_thunks)

    def drain_prep(n):
        for _ in range(n):
            if prep_thunks:
                prep_thunks.pop(0)()

    for s in range(NS):
        def ldx(e, s=s):
            o = ap3(xb, 0, [[SL, 128], [D, NCT], [1, D]])
            i = ap3(xin, s * SLAB * D, [[D, 128], [128 * D, NCT], [1, D]])
            return e.dma_start(out=o, in_=i)
        cb.dml(ldx, wr=[B["xb"]])
        ln_stats_tm(xb, B["xb"], tb, B["tb"], m1b, B["m1b"], r1b, B["r1b"],
                    s * NCT, EPS_LN)
        ln_apply_tm(xb, B["xb"], tb, B["tb"], m1b, B["m1b"], r1b, B["r1b"],
                    s * NCT, 32)
        tr_tm_to_fm(tb, B["tb"], hhi, B["hhi"], hlo, B["hlo"], (1, 2),
                    affine=(M_CA1, M_SA), aff_tmp=kb, aff_tmp_b=B["kb"])
        cb.dms(lambda e, s=s: e.dma_start(out=h_hi_d.ap()[s], in_=hhi[:, :]),
               rd=[B["hhi"]], wr=[SP["h"]])
        cb.dms(lambda e, s=s: e.dma_start(out=h_lo_d.ap()[s], in_=hlo[:, :]),
               rd=[B["hlo"]], wr=[SP["h"]])
        for f in range(KC):
            bank = 3 + f % 2
            mm3_fm(bank, wa_hi, wa_lo, (B["wa_hi"], B["wa_lo"]),
                   hhi, hlo, (B["hhi"], B["hlo"]), f, 0, 1024)
            cb.a(lambda e, f=f, bank=bank: e.activation(
                ub[:, :], pb[bank][:, 0:SLAB], AF.Identity,
                bias=vcol(C_BQ + f)), rd=[PB[bank], B["vecs"]], wr=[B["ub"]])
            cb.v(lambda e: e.tensor_scalar_min(t1s[:, :], ub[:, :], 0.0),
                 rd=[B["ub"]], wr=[B["t1s"]])
            cb.a(lambda e: e.activation(t2s[:, :], t1s[:, :], AF.Exp),
                 rd=[B["t1s"]], wr=[B["t2s"]])
            cb.v(lambda e, f=f: e.scalar_tensor_tensor(
                qb[:, f * SLAB:(f + 1) * SLAB], ub[:, :], 0.0, t2s[:, :],
                OP.max, OP.add), rd=[B["ub"], B["t2s"]], wr=[B["qb"]])
        cb.dms(lambda e, s=s: e.dma_start(out=q_d.ap()[s], in_=qb[:, :]),
               rd=[B["qb"]], wr=[SP["q"]])

    if stop_after == "a1":
        return _finish_queues(cb, nc)
    load_split(wk_in, D, 0, 0, D, 1024, wa_hi, wa_lo, B["wa_hi"], B["wa_lo"])
    load_split(wv_in, D, 0, 0, D, 1024, wb_hi, wb_lo, WBH[0], WBL[0])
    for s in range(NS):
        cb.dml(lambda e, s=s: e.dma_start(out=hhi[:, :], in_=h_hi_d.ap()[s]),
               rd=[SP["h"]], wr=[B["hhi"]])
        cb.dml(lambda e, s=s: e.dma_start(out=hlo[:, :], in_=h_lo_d.ap()[s]),
               rd=[SP["h"]], wr=[B["hlo"]])
        for proj in ("k", "v"):
            whi_, wlo_ = (wa_hi, wa_lo) if proj == "k" else (wb_hi, wb_lo)
            whib = B["wa_hi"] if proj == "k" else WBH[0]
            wlob = B["wa_lo"] if proj == "k" else WBL[0]
            for c in range(NCT):
                for nh in range(2):
                    bank = 3 + (c * 2 + nh) % 2
                    pbk = pb[bank]

                    def mm(e, c=c, nh=nh, pbk=pbk, whi_=whi_, wlo_=wlo_):
                        r = None
                        n = 0
                        for (wsrc, xsrc) in ((whi_, hhi), (wlo_, hhi),
                                             (whi_, hlo)):
                            for kc in range(KC):
                                r = e.matmul(
                                    pbk[:, 0:512],
                                    xsrc[:, kc * SLAB + c * 128:
                                         kc * SLAB + (c + 1) * 128],
                                    wsrc[:, kc * 1024 + nh * 512:
                                         kc * 1024 + (nh + 1) * 512],
                                    start=(n == 0), stop=(n == 3 * KC - 1))
                                n += 1
                        return r
                    cb.t(mm, rd=[whib, wlob, B["hhi"], B["hlo"]],
                         wr=[PB[bank]])
                    if proj == "k":
                        cb.v(lambda e, nh=nh, pbk=pbk: e.tensor_add(
                            ub[:, :], pbk[:, 0:512],
                            rep[:, R_BK + nh * 512:R_BK + (nh + 1) * 512]),
                            rd=[PB[bank], B["rep"]], wr=[B["ub"]])
                        cb.v(lambda e: e.tensor_scalar_min(
                            t1s[:, :], ub[:, :], 0.0),
                            rd=[B["ub"]], wr=[B["t1s"]])
                        cb.a(lambda e: e.activation(t2s[:, :], t1s[:, :],
                                                    AF.Exp),
                             rd=[B["t1s"]], wr=[B["t2s"]])
                        cb.v(lambda e, c=c, nh=nh: e.scalar_tensor_tensor(
                            kb[:, c * D + nh * 512:c * D + (nh + 1) * 512],
                            ub[:, :], 0.0, t2s[:, :], OP.max, OP.add),
                            rd=[B["ub"], B["t2s"]], wr=[B["kb"]])
                    else:
                        cb.v(lambda e, c=c, nh=nh, pbk=pbk: e.tensor_add(
                            tb[:, c * D + nh * 512:c * D + (nh + 1) * 512],
                            pbk[:, 0:512],
                            rep[:, R_BV + nh * 512:R_BV + (nh + 1) * 512]),
                            rd=[PB[bank], B["rep"]], wr=[B["tb"]])
        for c in range(NCT):
            bank = 5 + c % 2
            pbk = pb[bank]
            p7off = (c % 2) * 65

            def mmkv(e, c=c, pbk=pbk, p7off=p7off):
                r = None
                for hh in range(H):
                    hp, half = hh // 2, hh % 2
                    dst = (pbk[half * 64:(half + 1) * 64,
                               hp * 65:hp * 65 + 64] if hp < 7 else
                           pb[7][half * 64:(half + 1) * 64,
                                 p7off:p7off + 64])
                    r = e.matmul(
                        dst,
                        kb[:, c * D + hh * 64:c * D + (hh + 1) * 64],
                        tb[:, c * D + hh * 64:c * D + (hh + 1) * 64],
                        start=True, stop=True)
                for hh in range(H):
                    hp, half = hh // 2, hh % 2
                    dst = (pbk[half * 64:(half + 1) * 64,
                               hp * 65 + 64:hp * 65 + 65] if hp < 7 else
                           pb[7][half * 64:(half + 1) * 64,
                                 p7off + 64:p7off + 65])
                    r = e.matmul(
                        dst,
                        kb[:, c * D + hh * 64:c * D + (hh + 1) * 64],
                        onescol[:, :],
                        start=True, stop=True)
                return r
            cb.t(mmkv, rd=[B["kb"], B["tb"], B["onescol"]],
                 wr=[PB[bank], PB[7]])
            cb.v(lambda e, pbk=pbk: e.tensor_add(
                qb[:, 0:455], qb[:, 0:455], pbk[:, 0:455]),
                rd=[PB[bank], B["qb"]], wr=[B["qb"]])
            cb.v(lambda e, p7off=p7off: e.tensor_add(
                qb[:, 455:520], qb[:, 455:520],
                pb[7][:, p7off:p7off + 65]),
                rd=[PB[7], B["qb"]], wr=[B["qb"]])
        drain_prep(4)

    if stop_after == "a2":
        return _finish_queues(cb, nc)
    drain_prep(len(prep_thunks))

    cb.dms(lambda e: e.dma_start(out=cc_in.ap()[:, :], in_=qb[:, 0:520]),
           rd=[B["qb"]], wr=[Bcc])

    if skip_collective:
        cb.dms(lambda e: e.dma_start(out=cc_out.ap()[:, :],
                                     in_=cc_in.ap()[:, :]),
               rd=[Bcc], wr=[Bcc])
    else:
        def ccf(e):
            return e.collective_compute(
                "AllReduce", OP.add,
                replica_groups=[[0, 1, 2, 3], [4, 5, 6, 7]],
                ins=[cc_in.ap().opt()], outs=[cc_out.ap().opt()])
        cb._emit("gpsimd", ccf, [Bcc], [Bcc], 1, False, semkey="cc")
    cb.dml(lambda e: e.dma_start(out=qb[:, 0:520], in_=cc_out.ap()[:, :]),
           rd=[Bcc], wr=[B["qb"]])

    def expand_kv(half):
        def f(e):
            o = ap3(kvexp, half * (64 * 8 * 130 + 65),
                    [[8 * 130, 64], [130, 8], [1, 65]])
            i = ap3(qb, half * 64 * SL, [[SL, 64], [65, 8], [1, 65]])
            return e.dma_start(out=o, in_=i)
        cb.dms(f, rd=[B["qb"]], wr=[B["kvexp"]])
    expand_kv(0)
    expand_kv(1)

    for j, rc in enumerate((R_GA, R_GB, R_CB1, R_SB)):
        replicate(rc, garow_d, j * D)

    if stop_after == "cc":
        return _finish_queues(cb, nc)
    load_split(wo_in, D, 0, 0, D, 1024, wa_hi, wa_lo, B["wa_hi"], B["wa_lo"])
    x1hi = wb_hi
    x1lo = wb_lo

    def bc1_A(s, dst, dst_b):
        cb.dml(lambda e, s=s: e.dma_start(out=qb[:, :], in_=q_d.ap()[s]),
               rd=[SP["q"]], wr=[B["qb"]])
        cb.dml(lambda e, s=s: e.dma_start(out=hhi[:, :], in_=h_hi_d.ap()[s]),
               rd=[SP["h"]], wr=[B["hhi"]])
        cb.dml(lambda e, s=s: e.dma_start(out=hlo[:, :], in_=h_lo_d.ap()[s]),
               rd=[SP["h"]], wr=[B["hlo"]])
        for c in range(NCT):
            def mmo(e, c=c):
                r = None
                for hp in range(8):
                    b_i = hp // 3
                    off = (hp % 3) * 130
                    r = e.matmul(
                        pb[b_i][:, off:off + 130],
                        qb[:, hp * SLAB + c * 128:hp * SLAB + (c + 1) * 128],
                        kvexp[:, hp * 130:(hp + 1) * 130],
                        start=True, stop=True)
                return r
            cb.t(mmo, rd=[B["qb"], B["kvexp"]], wr=[PB[0], PB[1], PB[2]])
            for b_i, nh in ((0, 6), (1, 6), (2, 4)):
                cb.v(lambda e, b_i=b_i, nh=nh: e.tensor_scalar_add(
                    zb[:, b_i * 6:b_i * 6 + nh],
                    ap3(pb[b_i], 64, [[512, 128], [65, nh]]), EPS_Z),
                    rd=[PB[b_i]], wr=[B["zb"]])
            cb.v(lambda e: e.reciprocal(zb[:, 32:48], zb[:, 0:16]),
                 rd=[B["zb"]], wr=[B["zb"]])
            for b_i, nh in ((0, 6), (1, 6), (2, 4)):
                def f(e, c=c, b_i=b_i, nh=nh):
                    o = ap3(xb, c * D + b_i * 6 * 64,
                            [[SL, 128], [64, nh], [1, 64]])
                    i = ap3(pb[b_i], 0, [[512, 128], [65, nh], [1, 64]])
                    i2 = ap3(zb, 32 + b_i * 6, [[64, 128], [1, nh], [0, 64]])
                    return e.tensor_tensor(o, i, i2, OP.mult)
                cb.v(f, rd=[PB[b_i], B["zb"]], wr=[B["xb"]])
        tr_tm_to_fm(xb, B["xb"], wb_hi, WBH[2], wb_lo, WBL[2],
                    (3, 4), dcol0=4096)
        for f in range(KC):
            bank = 5 + f % 2
            mm3_fm(bank, wa_hi, wa_lo, (B["wa_hi"], B["wa_lo"]),
                   wb_hi, wb_lo, (WBH[2], WBL[2]), f, 0, 1024,
                   rcol0=4096)
            cb.a(lambda e, f=f, bank=bank: e.activation(
                xb[:, 0:SLAB], pb[bank][:, 0:SLAB],
                AF.Identity, bias=vcol(C_BO + f)),
                rd=[PB[bank], B["vecs"]], wr=[B["xb"]])
            cb.v(lambda e, f=f: e.tensor_add(
                xb[:, SLAB:2 * SLAB], xb[:, 0:SLAB],
                hhi[:, f * SLAB:(f + 1) * SLAB]),
                rd=[B["xb"], B["hhi"]], wr=[B["xb"]])
            cb.v(lambda e, f=f: e.tensor_add(
                dst[:, f * SLAB:(f + 1) * SLAB], xb[:, SLAB:2 * SLAB],
                hlo[:, f * SLAB:(f + 1) * SLAB]),
                rd=[B["xb"], B["hlo"]], wr=[dst_b])

    def bc1_P(s, srcbuf, src_b):
        fm_apply(srcbuf, src_b, C_G1, vecs, B["vecs"], 80, C_BE1,
                 x1b, B["x1b"], tb, B["tb"])
        cb.a(lambda e: e.activation(x1hi[:, 0:SL], x1b[:, :], AF.Identity),
             rd=[B["x1b"]], wr=[WBH[0], WBH[1]])
        cb.v(lambda e: e.tensor_sub(x1lo[:, 0:SL], x1b[:, :],
                                    x1hi[:, 0:SL]),
             rd=[B["x1b"], WBH[0], WBH[1]], wr=[WBL[0], WBL[1]])
        cb.dms(lambda e, s=s: e.dma_start(out=x1_hi_d.ap()[s],
                                          in_=x1hi[:, 0:SL]),
               rd=[WBH[0], WBH[1]], wr=[SP["x1"]])
        cb.dms(lambda e, s=s: e.dma_start(out=x1_lo_d.ap()[s],
                                          in_=x1lo[:, 0:SL]),
               rd=[WBL[0], WBL[1]], wr=[SP["x1"]])

    BC1_PIPE = True
    if BC1_PIPE:
        prev = None
        for s in range(NS):
            dst, dst_b = (a2b, B["a2b"]) if s % 2 == 0 else (kb, B["kb"])
            bc1_A(s, dst, dst_b)
            if prev is not None:
                bc1_P(prev[0], prev[1], prev[2])
            fm_stats(dst, dst_b, tb, B["tb"], EPS_LN)
            prev = (s, dst, dst_b)
        bc1_P(prev[0], prev[1], prev[2])
    else:
        for s in range(NS):
            dst, dst_b = (a2b, B["a2b"]) if s % 2 == 0 else (kb, B["kb"])
            bc1_A(s, dst, dst_b)
            fm_stats(dst, dst_b, tb, B["tb"], EPS_LN)
            bc1_P(s, dst, dst_b)

    if stop_after == "bc1":
        return _finish_queues(cb, nc)

    cb.dml(lambda e: e.dma_start(out=tb[0:1, 0:D], in_=g2_row.ap()[:, :]),
           wr=[B["tb"]])
    cb.dml(lambda e: e.dma_start(out=tb[0:1, D:2 * D],
                                 in_=be2_row.ap()[:, :]), wr=[B["tb"]])
    for j in range(2):
        cb.t(lambda e, j=j: e.matmul(
            pb[4][:, 0:SLAB], onesr[:, :],
            tb[0:1, D + j * SLAB:D + (j + 1) * SLAB],
            start=True, stop=True), rd=[B["tb"], B["onesr"]], wr=[PB[4]])
        cb.v(lambda e, j=j: e.tensor_mul(
            kvexp[:, j * SLAB:(j + 1) * SLAB], pb[4][:, 0:SLAB],
            rep[:, R_GA + j * SLAB:R_GA + (j + 1) * SLAB]),
            rd=[PB[4], B["rep"]], wr=[B["kvexp"]])
    for j in range(2):
        cb.t(lambda e, j=j: e.matmul(
            pb[5][:, 0:SLAB], onesr[:, :], tb[0:1, j * SLAB:(j + 1) * SLAB],
            start=True, stop=True), rd=[B["tb"], B["onesr"]], wr=[PB[5]])
        cb.v(lambda e, j=j: e.tensor_mul(
            rep[:, R_GA + j * SLAB:R_GA + (j + 1) * SLAB],
            rep[:, R_GA + j * SLAB:R_GA + (j + 1) * SLAB],
            pb[5][:, 0:SLAB]), rd=[PB[5], B["rep"]], wr=[B["rep"]])

    def w12_off(st, part, col):
        return (st * 2 + part) * 128 * W_COLS + col

    def emit_wload(s, st):
        if s == 0:
            first = st == 0
            load_split(w1_in, FF, 0, st * 512, D, 512, wa_hi, wa_lo,
                       [WAH[0]] + ([B["wa_hi"]] if first else []),
                       [WAL[0]] + ([B["wa_lo"]] if first else []), dcol0=0)
            load_split(w2_in, D, st * 512, 0, 512, 1024, wa_hi, wa_lo,
                       [WAH[1]] + ([B["wa_hi"]] if first else []),
                       [WAL[1]] + ([B["wa_lo"]] if first else []),
                       dcol0=4096)
            for part, src, srcb in ((0, wa_hi, [WAH[0]]),
                                    (1, wa_lo, [WAL[0]])):
                def sp1(e, st=st, part=part, src=src):
                    o = ap3(w12_d, w12_off(st, part, 0),
                            [[W_COLS, 128], [1, 4096]])
                    return e.dma_start(out=o, in_=src[:, 0:4096])
                cb.dms(sp1, rd=srcb, wr=[Bw12[st]])
            for part, src, srcb in ((0, wa_hi, [WAH[1]]),
                                    (1, wa_lo, [WAL[1]])):
                def sp2(e, st=st, part=part, src=src):
                    o = ap3(w12_d, w12_off(st, part, 4096),
                            [[W_COLS, 128], [1, 4096]])
                    return e.dma_start(out=o, in_=src[:, 4096:8192])
                cb.dms(sp2, rd=srcb, wr=[Bw12[st]])
        else:
            for part, dst, dstb in ((0, wa_hi, [WAH[0]]),
                                    (1, wa_lo, [WAL[0]])):
                def ld1(e, st=st, part=part, dst=dst):
                    i = ap3(w12_d, w12_off(st, part, 0),
                            [[W_COLS, 128], [1, 4096]])
                    return e.dma_start(out=dst[:, 0:4096], in_=i)
                cb.dml(ld1, rd=[Bw12[st]], wr=dstb)
            for part, dst, dstb in ((0, wa_hi, [WAH[1]]),
                                    (1, wa_lo, [WAL[1]])):
                def ld2(e, st=st, part=part, dst=dst):
                    i = ap3(w12_d, w12_off(st, part, 4096),
                            [[W_COLS, 128], [1, 4096]])
                    return e.dma_start(out=dst[:, 4096:8192], in_=i)
                cb.dml(ld2, rd=[Bw12[st]], wr=dstb)

    def slab_bufs(s):
        p = s % 2
        Y, BY = (qb, B["qb"]) if p == 0 else (a2b, B["a2b"])
        if p == 0:
            return Y, BY, hhi, hlo, [B["hhi"]], [B["hlo"]]
        return Y, BY, wb_hi, wb_lo, [WBH[0], WBH[1]], [WBL[0], WBL[1]]

    def emit_stripe(s, st):
        Y, BY, xh, xl, Bxh, Bxl = slab_bufs(s)
        for gf in range(4):
            bank = gf % 2
            mm3_fm(bank, wa_hi, wa_lo, ([WAH[0]], [WAL[0]]), xh, xl,
                   (Bxh, Bxl), gf, 0, 512, nkc=8)
            cb.a(lambda e, gf=gf, bank=bank, st=st: e.activation(
                tb[:, gf * SLAB:(gf + 1) * SLAB], pb[bank][:, 0:SLAB],
                AF.Gelu, bias=vcol(C_B1 + st * 4 + gf)),
                rd=[PB[bank], B["vecs"]], wr=[B["tb"]])
            cb.a(lambda e, gf=gf: e.activation(
                wb_hi[:, 4096 + gf * SLAB:4096 + (gf + 1) * SLAB],
                tb[:, gf * SLAB:(gf + 1) * SLAB], AF.Identity),
                rd=[B["tb"]], wr=[WBH[2]])
            cb.v(lambda e, gf=gf: e.tensor_sub(
                wb_lo[:, 4096 + gf * SLAB:4096 + (gf + 1) * SLAB],
                tb[:, gf * SLAB:(gf + 1) * SLAB],
                wb_hi[:, 4096 + gf * SLAB:4096 + (gf + 1) * SLAB]),
                rd=[B["tb"], WBH[2]], wr=[WBL[2]])
        for yo in range(KC):
            bank = (2, 3, 6, 7)[yo % 4]
            mm3_fm(bank, wa_hi, wa_lo, ([WAH[1]], [WAL[1]]),
                   wb_hi, wb_lo, (WBH[2], WBL[2]), yo, 4096, 1024,
                   nkc=4, rcol0=4096, rstep=SLAB)
            if st == 0:
                cb.a(lambda e, yo=yo, bank=bank, Y=Y, BY=BY: e.activation(
                    Y[:, yo * SLAB:(yo + 1) * SLAB], pb[bank][:, 0:SLAB],
                    AF.Identity, bias=vcol(C_B2 + yo)),
                    rd=[PB[bank], B["vecs"]], wr=[BY])
            else:
                cb.v(lambda e, yo=yo, bank=bank, Y=Y, BY=BY: e.tensor_add(
                    Y[:, yo * SLAB:(yo + 1) * SLAB], pb[bank][:, 0:SLAB],
                    Y[:, yo * SLAB:(yo + 1) * SLAB]),
                    rd=[PB[bank], BY], wr=[BY])

    def bc3_piece(s, pid):
        Y, BY, xh, xl, Bxh, Bxl = slab_bufs(s)
        if pid == 0:
            cb.v(lambda e, Y=Y, xh=xh: e.tensor_add(x1b[:, :], Y[:, :],
                                                    xh[:, 0:SL]),
                 rd=[BY] + Bxh, wr=[B["x1b"]])
            cb.v(lambda e, xl=xl: e.tensor_add(x1b[:, :], x1b[:, :],
                                               xl[:, 0:SL]),
                 rd=[B["x1b"]] + Bxl, wr=[B["x1b"]])
        elif pid == 1:
            for f in range(KC):
                bk = 4 + f % 2
                pbk = pb[bk]

                def tr(e, f=f, pbk=pbk):
                    r = None
                    for c in range(NCT):
                        r = e.transpose(
                            pbk[:, c * 128:(c + 1) * 128],
                            x1b[:, f * SLAB + c * 128:
                                f * SLAB + (c + 1) * 128],
                            ident[:, :])
                    return r
                cb.t(tr, rd=[B["x1b"], B["ident_s"]], wr=[PB[bk]])

                def ev(e, f=f, pbk=pbk):
                    o = ap3(kb, f * 128, [[SL, 128], [D, NCT], [1, 128]])
                    i = ap3(pbk, 0, [[512, 128], [128, NCT], [1, 128]])
                    return e.activation(o, i, AF.Identity)
                cb.a(ev, rd=[PB[bk]], wr=[B["kb"]])
        elif pid == 2:
            ln_stats_tm(kb, B["kb"], x1b, B["x1b"], m4b, B["m4b"],
                        r4b, B["r4b"], 0, EPS_LN)
        elif pid == 3:
            ln_apply_tm(kb, B["kb"], kb, B["kb"], m4b, B["m4b"],
                        r4b, B["r4b"], 0, 4)

            def gm(e):
                k3 = ap3(kb, 0, [[SL, 128], [D, NCT], [1, D]])
                r3 = ap3(rep, R_GA, [[4 * D, 128], [0, NCT], [1, D]])
                return e.tensor_tensor(k3, k3, r3, OP.mult)
            cb.v(gm, rd=[B["kb"], B["rep"]], wr=[B["kb"]])
        elif pid == 4:
            def gab(e):
                k3 = ap3(kb, 0, [[SL, 128], [D, NCT], [1, D]])
                g3 = ap3(kvexp, 0, [[8 * 130, 128], [0, NCT], [1, D]])
                return e.tensor_tensor(k3, k3, g3, OP.add)
            cb.v(gab, rd=[B["kb"], B["kvexp"]], wr=[B["kb"]])
            cb.v(lambda e: e.tensor_add(kb[:, :], kb[:, :], xb[:, :]),
                 rd=[B["kb"], B["xb"]], wr=[B["kb"]])
            cb.dms(lambda e, s=s: e.dma_start(out=x2_d.ap()[s],
                                              in_=kb[:, :]),
                   rd=[B["kb"]], wr=[SP["x2"]])
        elif pid == 5:
            ln_stats_tm(kb, B["kb"], x1b, B["x1b"], m1b, B["m1b"],
                        r1b, B["r1b"], 0, EPS_LN)
        elif pid == 6:
            ln_apply_tm(kb, B["kb"], kb, B["kb"], m1b, B["m1b"],
                        r1b, B["r1b"], 0, 32)

            def cm(e):
                k3 = ap3(kb, 0, [[SL, 128], [D, NCT], [1, D]])
                r3 = ap3(rep, R_CB1, [[4 * D, 128], [0, NCT], [1, D]])
                return e.tensor_tensor(k3, k3, r3, OP.mult)
            cb.v(cm, rd=[B["kb"], B["rep"]], wr=[B["kb"]])

            def sm(e):
                k3 = ap3(kb, 0, [[SL, 128], [D, NCT], [1, D]])
                r3 = ap3(rep, R_SB, [[4 * D, 128], [0, NCT], [1, D]])
                return e.tensor_tensor(k3, k3, r3, OP.add)
            cb.v(sm, rd=[B["kb"], B["rep"]], wr=[B["kb"]])
        elif pid == 7:
            sA = ap3(kb, 0, [[SL, 128], [16, 256], [1, 16]])
            cb.v(lambda e, sA=sA: e.tensor_reduce(
                ub[:, 0:256], sA, AX.X, OP.max, apply_absolute_value=True),
                rd=[B["kb"]], wr=[B["ub"]])
            cb.v(lambda e: e.tensor_scalar(ub[:, 256:512], ub[:, 0:256],
                                           r6c[:, 0:1], 1e-12, OP.mult,
                                           OP.max),
                 rd=[B["ub"], B["r6c"]], wr=[B["ub"]])
            cb.v(lambda e: e.reciprocal(ub[:, 0:256], ub[:, 256:512]),
                 rd=[B["ub"]], wr=[B["ub"]])
            cb.a(lambda e, Y=Y: e.activation(Y[:, :], kb[:, :], AF.Abs),
                 rd=[B["kb"]], wr=[BY])
            v3 = ap3(Y, 0, [[SL, 128], [16, 256], [1, 16]])
            rsb = ap3(ub, 0, [[SLAB, 128], [1, 256], [0, 16]])
            cb.v(lambda e, v3=v3, rsb=rsb: e.tensor_tensor(v3, v3, rsb,
                                                           OP.mult),
                 rd=[BY, B["ub"]], wr=[BY])
        elif pid == 8:
            vf = ap3(Y, 0, [[SL, 128], [1, SL]])
            tf = ap3(x1b, 0, [[SL, 128], [1, SL]])
            cb.v(lambda e, vf=vf, tf=tf: e.tensor_single_scalar(
                tf, vf, 0.25, OP.is_gt), rd=[BY], wr=[B["x1b"]])
            for thr in (0.75, 1.25, 1.75):
                cb.v(lambda e, vf=vf, tf=tf, thr=thr: e.scalar_tensor_tensor(
                    tf, vf, thr, tf, OP.is_gt, OP.add),
                    rd=[BY, B["x1b"]], wr=[B["x1b"]])
        elif pid == 9:
            vf = ap3(Y, 0, [[SL, 128], [1, SL]])
            kf = ap3(xb, 0, [[SL, 128], [1, SL]])
            cb.v(lambda e, vf=vf, kf=kf: e.tensor_single_scalar(
                kf, vf, 2.5, OP.is_gt), rd=[BY], wr=[B["xb"]])
            for thr in (3.5, 5.0, 5.0):
                cb.v(lambda e, vf=vf, kf=kf, thr=thr: e.scalar_tensor_tensor(
                    kf, vf, thr, kf, OP.is_gt, OP.add),
                    rd=[BY, B["xb"]], wr=[B["xb"]])
        elif pid == 10:
            vf = ap3(Y, 0, [[SL, 128], [1, SL]])
            tf = ap3(x1b, 0, [[SL, 128], [1, SL]])
            kf = ap3(xb, 0, [[SL, 128], [1, SL]])
            cb.v(lambda e, tf=tf, kf=kf: e.scalar_tensor_tensor(
                tf, tf, 0.5, kf, OP.mult, OP.add),
                rd=[B["x1b"], B["xb"]], wr=[B["x1b"]])
            cb.a(lambda e, Y=Y: e.activation(Y[:, :], kb[:, :], AF.Sign),
                 rd=[B["kb"]], wr=[BY])
            cb.v(lambda e, tf=tf, vf=vf: e.tensor_tensor(tf, tf, vf,
                                                         OP.mult),
                 rd=[B["x1b"], BY], wr=[B["x1b"]])
            t3 = ap3(x1b, 0, [[SL, 128], [16, 256], [1, 16]])
            d3 = ap3(kb, 0, [[SL, 128], [16, 256], [1, 16]])
            ssb = ap3(ub, 256, [[SLAB, 128], [1, 256], [0, 16]])
            cb.v(lambda e, t3=t3, d3=d3, ssb=ssb: e.tensor_tensor(
                d3, t3, ssb, OP.mult), rd=[B["x1b"], B["ub"]], wr=[B["kb"]])
        elif pid == 11:
            tr_tm_to_fm(kb, B["kb"], wa_hi, WAH[1], wa_lo, WAL[1], (4, 5),
                        dcol0=4096)
            cb.dms(lambda e, s=s: e.dma_start(out=xq_hi_d.ap()[s],
                                              in_=wa_hi[:, 4096:4096 + SL]),
                   rd=[WAH[1]], wr=[SP["xq"]])
            cb.dms(lambda e, s=s: e.dma_start(out=xq_lo_d.ap()[s],
                                              in_=wa_lo[:, 4096:4096 + SL]),
                   rd=[WAL[1]], wr=[SP["xq"]])

    PIECE_MAP = {0: (0, 1, 2), 1: (3,), 2: (4,), 3: (5, 6), 4: (7,),
                 5: (8,), 6: (9, 10), 7: (11,)}

    cb.dml(lambda e: e.dma_start(out=hhi[:, :], in_=x1_hi_d.ap()[0]),
           rd=[SP["x1"]], wr=[B["hhi"]])
    cb.dml(lambda e: e.dma_start(out=hlo[:, :], in_=x1_lo_d.ap()[0]),
           rd=[SP["x1"]], wr=[B["hlo"]])
    emit_wload(0, 0)
    for s in range(NS):
        for st in range(8):
            emit_stripe(s, st)
            if s >= 1:
                for pid in PIECE_MAP[st]:
                    bc3_piece(s - 1, pid)
            if st < 7:
                emit_wload(s, st + 1)
            elif s + 1 < NS:
                emit_wload(s + 1, 0)
            if st == 1 and s + 1 < NS:
                _, _, nh, nl, nhb, nlb = slab_bufs(s + 1)
                cb.dml(lambda e, s=s, nh=nh: e.dma_start(
                    out=nh[:, 0:SL], in_=x1_hi_d.ap()[s + 1]),
                    rd=[SP["x1"]], wr=nhb)
                cb.dml(lambda e, s=s, nl=nl: e.dma_start(
                    out=nl[:, 0:SL], in_=x1_lo_d.ap()[s + 1]),
                    rd=[SP["x1"]], wr=nlb)
            if st == 6:
                def ldx2(e, s=s):
                    o = ap3(xb, 0, [[SL, 128], [D, NCT], [1, D]])
                    i = ap3(xin, s * SLAB * D,
                            [[D, 128], [128 * D, NCT], [1, D]])
                    return e.dma_start(out=o, in_=i)
                cb.dml(ldx2, wr=[B["xb"]])
    for pid in range(12):
        bc3_piece(NS - 1, pid)

    if stop_after in ("bc2", "bc3"):
        return _finish_queues(cb, nc)

    STRIPES = STRIPES4
    NST4 = len(STRIPES)

    def w4_bufs(mat):
        if mat == 0:
            return wa_hi, wa_lo, 0, [WAH[0]], [WAL[0]]
        if mat == 1:
            return wa_hi, wa_lo, 4096, [WAH[1]], [WAL[1]]
        return wb_hi, wb_lo, 0, [WBH[0], WBH[1]], [WBL[0], WBL[1]]

    def emit_w4load(s, sti):
        i0, isz = STRIPES[sti]
        nic = isz // 128
        for mat, ncols in ((0, KC * isz), (1, KC * isz), (2, nic * 1024)):
            hi_t, lo_t, dc0, hib, lob = w4_bufs(mat)
            first = s == 0 and sti == 0 and mat == 0
            wh = hib + ([B["wa_hi"]] if first else [])
            wl = lob + ([B["wa_lo"]] if first else [])
            if s == 0:
                def ldc(e, mat=mat, dc0=dc0, ncols=ncols, sti=sti,
                        hi_t=hi_t):
                    o = ap3(hi_t, dc0, [[W_COLS, 128], [1, ncols]])
                    i = ap3(w4_d, (mat * 6 + sti) * 128 * 4096,
                            [[4096, 128], [1, ncols]])
                    return e.dma_start(out=o, in_=i)
                cb.dms(ldc, rd=[Bw4], wr=wh)

                def ldf(e, mat=mat, ncols=ncols, sti=sti):
                    o = ap3(xb, 0, [[SL, 128], [1, ncols]])
                    i = ap3(w4_d, (mat * 6 + sti) * 128 * 4096,
                            [[4096, 128], [1, ncols]])
                    return e.dma_start(out=o, in_=i)
                cb.dml(ldf, rd=[Bw4], wr=[B["xb"]])
                cb.v(lambda e, dc0=dc0, ncols=ncols, hi_t=hi_t, lo_t=lo_t:
                     e.tensor_sub(
                         ap3(lo_t, dc0, [[W_COLS, 128], [1, ncols]]),
                         ap3(xb, 0, [[SL, 128], [1, ncols]]),
                         ap3(hi_t, dc0, [[W_COLS, 128], [1, ncols]])),
                     rd=[B["xb"]] + hib, wr=wl)
                for part, src, srcb in ((0, hi_t, hib), (1, lo_t, lob)):
                    def sp(e, mat=mat, sti=sti, part=part, src=src,
                           dc0=dc0, ncols=ncols):
                        o = ap3(w4b_d,
                                ((mat * 6 + sti) * 2 + part) * 128 * 4096,
                                [[4096, 128], [1, ncols]])
                        return e.dma_start(
                            out=o, in_=ap3(src, dc0,
                                           [[W_COLS, 128], [1, ncols]]))
                    cb.dms(sp, rd=srcb, wr=[Bw4b[mat * 6 + sti]])
            else:
                for part, dst, dstb in ((0, hi_t, wh), (1, lo_t, wl)):
                    def ldb(e, mat=mat, sti=sti, part=part, dst=dst,
                            dc0=dc0, ncols=ncols):
                        o = ap3(dst, dc0, [[W_COLS, 128], [1, ncols]])
                        i = ap3(w4b_d,
                                ((mat * 6 + sti) * 2 + part) * 128 * 4096,
                                [[4096, 128], [1, ncols]])
                        return e.dma_start(out=o, in_=i)
                    cb.dml(ldb, rd=[Bw4b[mat * 6 + sti]], wr=dstb)

    def emit_bc4_stripe(s, sti):
        i0, isz = STRIPES[sti]
        nic = isz // 128
        last = sti == NST4 - 1

        def phase1(c):
            p = c % 2
            for bank, wc0 in ((0 + p, 0), (2 + p, 4096)):
                def mm(e, c=c, bank=bank, wc0=wc0, isz=isz):
                    r = None
                    n = 0
                    for (wsrc, xsrc) in ((wa_hi, hhi), (wa_lo, hhi),
                                         (wa_hi, hlo)):
                        for kc in range(KC):
                            r = e.matmul(
                                pb[bank][:, 0:isz],
                                xsrc[:, kc * SLAB + c * 128:
                                     kc * SLAB + (c + 1) * 128],
                                wsrc[:, wc0 + kc * isz:
                                     wc0 + (kc + 1) * isz],
                                start=(n == 0), stop=(n == 3 * KC - 1))
                            n += 1
                    return r
                wbs = ([WAH[0], WAL[0]] if wc0 == 0 else [WAH[1], WAL[1]])
                cb.t(mm, rd=wbs + [B["hhi"], B["hlo"]], wr=[PB[bank]])
            cb.a(lambda e, p=p, isz=isz: e.activation(
                t2s[:, 0:isz] if p else t1s[:, 0:isz],
                pb[0 + p][:, 0:isz], AF.Silu),
                rd=[PB[0 + p]], wr=[B["t2s"] if p else B["t1s"]])
            cb.v(lambda e, p=p, c=c, isz=isz: e.tensor_mul(
                a2b[:, p * 2048:p * 2048 + isz],
                (t2s if p else t1s)[:, 0:isz], pb[2 + p][:, 0:isz]),
                rd=[B["t2s"] if p else B["t1s"], PB[2 + p]],
                wr=[B["a2b"]])
            quant_tm(a2b, B["a2b"], SL, a2b, B["a2b"], SL, p * 2048, isz,
                     tb, B["tb"], SL, p * 1536, tb, B["tb"], SL,
                     p * 1536 + 512, tb, B["tb"], SL, p * 1536 + 1024,
                     sc_off=p * 64)

        def phase2(c):
            p = c % 2
            pbk = pb[4 + p]

            def trh(e, p=p, nic=nic, pbk=pbk):
                r = None
                for ic in range(nic):
                    r = e.transpose(
                        pbk[:, ic * 128:(ic + 1) * 128],
                        a2b[:, p * 2048 + ic * 128:
                            p * 2048 + (ic + 1) * 128],
                        ident[:, :])
                return r
            cb.t(trh, rd=[B["a2b"], B["ident_s"]], wr=[PB[4 + p]])
            hqc = 4096 + p * 512
            cb.a(lambda e, hqc=hqc, nic=nic, pbk=pbk: e.activation(
                wb_hi[:, hqc:hqc + nic * 128], pbk[:, 0:nic * 128],
                AF.Identity), rd=[PB[4 + p]], wr=[WBH[2]])
            cb.v(lambda e, hqc=hqc, nic=nic, pbk=pbk: e.tensor_sub(
                wb_lo[:, hqc:hqc + nic * 128], pbk[:, 0:nic * 128],
                wb_hi[:, hqc:hqc + nic * 128]),
                rd=[PB[4 + p], WBH[2]], wr=[WBL[2]])
            for no in range(2):
                bank = 6 + no

                def mmd(e, c=c, no=no, bank=bank, nic=nic, hqc=hqc):
                    r = None
                    n = 0
                    for (wsrc, xsrc) in ((wb_hi, wb_hi), (wb_lo, wb_hi),
                                         (wb_hi, wb_lo)):
                        for ic in range(nic):
                            r = e.matmul(
                                pb[bank][:, 0:512],
                                xsrc[:, hqc + ic * 128:
                                     hqc + (ic + 1) * 128],
                                wsrc[:, ic * 1024 + no * 512:
                                     ic * 1024 + (no + 1) * 512],
                                start=(n == 0), stop=(n == 3 * nic - 1))
                            n += 1
                    return r
                cb.t(mmd, rd=[WBH[0], WBH[1], WBL[0], WBL[1],
                              WBH[2], WBL[2]], wr=[PB[bank]])
                if sti == 0:
                    cb.v(lambda e, c=c, no=no, bank=bank: e.tensor_copy(
                        x1b[:, c * D + no * 512:c * D + (no + 1) * 512],
                        pb[bank][:, 0:512]),
                        rd=[PB[bank]], wr=[B["x1b"]])
                elif not last:
                    cb.v(lambda e, c=c, no=no, bank=bank: e.tensor_add(
                        x1b[:, c * D + no * 512:c * D + (no + 1) * 512],
                        pb[bank][:, 0:512],
                        x1b[:, c * D + no * 512:c * D + (no + 1) * 512]),
                        rd=[PB[bank], B["x1b"]], wr=[B["x1b"]])
                else:
                    cb.v(lambda e, c=c, no=no, bank=bank: e.tensor_add(
                        t1s[:, :], pb[bank][:, 0:512],
                        x1b[:, c * D + no * 512:c * D + (no + 1) * 512]),
                        rd=[PB[bank], B["x1b"]], wr=[B["t1s"]])
                    cb.v(lambda e, no=no: e.tensor_mul(
                        t2s[:, :], t1s[:, :],
                        rep[:, R_GB + no * 512:R_GB + (no + 1) * 512]),
                        rd=[B["t1s"], B["rep"]], wr=[B["t2s"]])
                    cb.v(lambda e, c=c, no=no: e.tensor_add(
                        x1b[:, c * D + no * 512:c * D + (no + 1) * 512],
                        t2s[:, :],
                        xb[:, c * D + no * 512:c * D + (no + 1) * 512]),
                        rd=[B["t2s"], B["xb"]], wr=[B["x1b"]])

        if last:
            for c in range(NCT):
                phase1(c)
                phase2(c)
        else:
            phase1(0)
            for c in range(1, NCT):
                phase1(c)
                phase2(c - 1)
            phase2(NCT - 1)

    emit_w4load(0, 0)
    for s in range(NS):
        cb.dml(lambda e, s=s: e.dma_start(out=hhi[:, :],
                                          in_=xq_hi_d.ap()[s]),
               rd=[SP["xq"]], wr=[B["hhi"]])
        cb.dml(lambda e, s=s: e.dma_start(out=hlo[:, :],
                                          in_=xq_lo_d.ap()[s]),
               rd=[SP["xq"]], wr=[B["hlo"]])
        for sti in range(NST4):
            if sti == NST4 - 1:
                cb.dml(lambda e, s=s: e.dma_start(out=xb[:, :],
                                                  in_=x2_d.ap()[s]),
                       rd=[SP["x2"]], wr=[B["xb"]])
            emit_bc4_stripe(s, sti)
            if sti + 1 < NST4:
                emit_w4load(s, sti + 1)
            elif s + 1 < NS:
                emit_w4load(s + 1, 0)

        def stout(e, s=s):
            oo = ap3(out, s * SLAB * D,
                     [[D, 128], [128 * D, NCT], [1, D]])
            i = ap3(x1b, 0, [[SL, 128], [D, NCT], [1, D]])
            return e.dma_start(out=oo, in_=i)
        cb.dms(stout, rd=[B["x1b"]], wr=[SP["out"]])

    return _finish_queues(cb, nc)

def _finish_queues(cb, nc):
    fin = [(cb.sem[f"gpsimd_d{j}"], cb.cnt[f"gpsimd_d{j}"])
           for j in range(NPOOL) if cb.cnt[f"gpsimd_d{j}"]]

    def _final(e, fin=fin):
        for s, v in fin:
            e.wait_ge(s, v)
    cb.q["gpsimd"].append(_final)

    from contextlib import ExitStack
    with ExitStack() as ctx:
        block = ctx.enter_context(nc.Block())

        @block.sync
        def _(sync):
            for g in cb.q["sync"]:
                g(sync)

        @block.scalar
        def _(scalar):
            for g in cb.q["scalar"]:
                g(scalar)

        @block.vector
        def _(vector):
            for g in cb.q["vector"]:
                g(vector)

        @block.tensor
        def _(tensor):
            for g in cb.q["tensor"]:
                g(tensor)

        @block.gpsimd
        def _(gpsimd):
            for g in cb.q["gpsimd"]:
                g(gpsimd)

    return nc


def make_in_maps(inputs):
    x = np.ascontiguousarray(np.asarray(inputs["L0_attn_src"],
                                        dtype=np.float32))
    c = np.asarray(inputs["L0_c"], dtype=np.float32)
    ident = np.eye(128, dtype=np.float32)

    def fm(v, k):
        return np.ascontiguousarray(
            np.asarray(v, dtype=np.float32).reshape(k, 128).T)

    def cc(v):
        return np.ascontiguousarray(np.asarray(v, dtype=np.float32))

    common = {
        "wmod": cc(inputs["L0_wmod"]),
        "wq": cc(inputs["L0_attn_Wq"]), "wk": cc(inputs["L0_attn_Wk"]),
        "wv": cc(inputs["L0_attn_Wv"]), "wo": cc(inputs["L0_attn_Wo"]),
        "w1": cc(inputs["L0_attn_W1"]), "w2": cc(inputs["L0_attn_W2"]),
        "wg": cc(inputs["L0_ffn_w_gate"]), "wu": cc(inputs["L0_ffn_w_up"]),
        "wd": cc(inputs["L0_ffn_w_down"]),
        "bq_fm": fm(inputs["L0_attn_bq"], 8),
        "bk_row": cc(np.asarray(inputs["L0_attn_bk"]).reshape(1, D)),
        "bv_row": cc(np.asarray(inputs["L0_attn_bv"]).reshape(1, D)),
        "bo_fm": fm(inputs["L0_attn_bo"], 8),
        "b1_fm": fm(inputs["L0_attn_b1"], 32),
        "b2_fm": fm(inputs["L0_attn_b2"], 8),
        "g1_fm": fm(inputs["L0_attn_g1"], 8),
        "be1_fm": fm(inputs["L0_attn_be1"], 8),
        "g2_fm": fm(inputs["L0_attn_g2"], 8),
        "be2_fm": fm(inputs["L0_attn_be2"], 8),
        "g2_row": cc(np.asarray(inputs["L0_attn_g2"]).reshape(1, D)),
        "be2_row": cc(np.asarray(inputs["L0_attn_be2"]).reshape(1, D)),
        "ident": ident,
    }
    in_maps = []
    for core in range(NCORES):
        b = core // 4
        t0 = (core % 4) * T
        m = dict(common)
        m["xin"] = np.ascontiguousarray(x[b, t0:t0 + T])
        m["cfm"] = fm(c[b], 8)
        in_maps.append(m)
    return in_maps


def kernel(**inputs):
    nc = build_program()
    in_maps = make_in_maps(inputs)
    res = run_bass_kernel_spmd(nc, in_maps, core_ids=list(range(NCORES)))
    full = np.empty((2, 16384, D), dtype=np.float32)
    for core in range(NCORES):
        b = core // 4
        t0 = (core % 4) * T
        full[b, t0:t0 + T] = res.results[core]["out"]
    return full


# version 12
# speedup vs baseline: 1.0081x; 1.0081x over previous
import numpy as np
import concourse.bass as bass
import concourse.mybir as mybir
from concourse.bass_utils import run_bass_kernel_spmd

F32 = mybir.dt.float32
BF16 = mybir.dt.bfloat16
OP = mybir.AluOpType
AF = mybir.ActivationFunctionType
AX = mybir.AxisListType
AP = bass.AP

D = 1024
H = 16
FF = 4096
FI = 2816
NCORES = 8
T = 4096
SLAB = 512
NS = T // SLAB
NCT = SLAB // 128
KC = D // 128
EPS_LN = 1e-5
EPS_Z = 1e-6

ENGS = ("sync", "scalar", "vector", "tensor", "gpsimd")

C_BQ, C_BO, C_B2, C_G1, C_BE1, C_G2, C_B1 = 0, 8, 16, 24, 32, 40, 48
M_SA, M_CA, M_GA, M_SB, M_CB, M_GB, M_CA1, M_CB1 = 0, 8, 16, 24, 32, 40, 48, 56
R_BK, R_BV = 0, D
R_GA, R_GB, R_CB1, R_SB = 0, D, 2 * D, 3 * D

W_COLS = 8192
SL = NCT * D


class Buf:
    __slots__ = ("name", "wr", "rds")

    def __init__(self, name):
        self.name = name
        self.wr = None
        self.rds = []


NPOOL = 12


class CB:

    def __init__(self):
        self.nc = bass.Bass("TRN2", target_bir_lowering=False)
        self.q = {e: [] for e in ENGS}
        self.sem = {e: self.nc.alloc_semaphore(f"S{e}") for e in ENGS}
        for e in ("sync", "gpsimd"):
            for j in range(NPOOL):
                self.sem[f"{e}_d{j}"] = self.nc.alloc_semaphore(f"S{e}d{j}")
        self.sem["cc"] = self.nc.alloc_semaphore("Scc")
        self.cnt = {k: 0 for k in self.sem}
        self.waited = {e: {} for e in ENGS}
        self.pool_i = {"sync": 0, "gpsimd": 0}
        self.last_drain = {}

    def _hazard_waits(self, eng, reads, writes, extra=()):
        waits = list(extra)
        for b in reads:
            if b.wr is not None:
                waits.append(b.wr)
        for b in writes:
            if b.wr is not None:
                waits.append(b.wr)
            waits.extend(b.rds)
        wmap = {}
        for (key, val) in waits:
            if key == eng:
                continue
            if wmap.get(key, -1) < val:
                wmap[key] = val
        final = []
        for key, val in wmap.items():
            if self.waited[eng].get(key, -1) < val:
                self.waited[eng][key] = val
                final.append((self.sem[key], val))
        return final

    def _finish(self, eng, tok, reads, writes):
        for b in writes:
            b.wr = tok
            b.rds = []
        for b in reads:
            b.rds.append(tok)
        return tok

    def _emit(self, eng, f, reads, writes, inc, drain, semkey=None):
        semkey = semkey or eng
        need_drain = False
        if drain:
            last = self.last_drain.get(eng, 0)
            dep = 0
            for b in reads:
                if b.wr is not None and b.wr[0] == semkey:
                    dep = max(dep, b.wr[1])
            for b in writes:
                if b.wr is not None and b.wr[0] == semkey:
                    dep = max(dep, b.wr[1])
                for t in b.rds:
                    if t[0] == semkey:
                        dep = max(dep, t[1])
            if dep > last:
                need_drain = True
        final = self._hazard_waits(eng, reads, writes)
        sem = self.sem[semkey]

        def g(e, final=final, f=f, inc=inc, nd=need_drain, sem=sem):
            if nd:
                e.drain()
            for s, v in final:
                e.wait_ge(s, v)
            r = f(e)
            if inc:
                r.then_inc(sem, inc)

        self.q[eng].append(g)
        self.cnt[semkey] += inc
        if need_drain:
            self.last_drain[eng] = self.cnt[semkey] - inc
        return self._finish(eng, (semkey, self.cnt[semkey]), reads, writes)

    def _dma(self, eng, f, reads, writes):
        j = self.pool_i[eng]
        self.pool_i[eng] = (j + 1) % NPOOL
        key = f"{eng}_d{j}"
        guard = [(key, self.cnt[key])] if self.cnt[key] else []
        final = self._hazard_waits(eng, reads, writes, extra=guard)
        sem = self.sem[key]

        def g(e, final=final, f=f, sem=sem):
            for s, v in final:
                e.wait_ge(s, v)
            f(e).then_inc(sem, 16)

        self.q[eng].append(g)
        self.cnt[key] += 16
        return self._finish(eng, (key, self.cnt[key]), reads, writes)

    def v(self, f, rd=(), wr=()):
        return self._emit("vector", f, rd, wr, 1, True)

    def p(self, f, rd=(), wr=()):
        return self._emit("gpsimd", f, rd, wr, 1, True)

    def a(self, f, rd=(), wr=()):
        return self._emit("scalar", f, rd, wr, 1, True)

    def t(self, f, rd=(), wr=()):
        return self._emit("tensor", f, rd, wr, 1, False)

    def dml(self, f, rd=(), wr=()):
        return self._dma("sync", f, rd, wr)

    def dms(self, f, rd=(), wr=()):
        return self._dma("gpsimd", f, rd, wr)


def ap3(handle, offset, dims):
    return AP(handle, offset, [list(d) for d in dims])


def build_program(debug_names=(), skip_collective=False, bc2_stripes=8, stop_after=None):
    cb = CB()
    nc = cb.nc
    dbg = set(debug_names)

    def dram(name, shape, dt=F32):
        kind = "ExternalOutput" if name in dbg else "Internal"
        return nc.dram_tensor(name, list(shape), dt, kind=kind)

    xin = nc.dram_tensor("xin", [T, D], F32, kind="ExternalInput")
    cfm_in = nc.dram_tensor("cfm", [128, 8], F32, kind="ExternalInput")
    wmod_in = nc.dram_tensor("wmod", [D, 6 * D], F32, kind="ExternalInput")
    wq_in = nc.dram_tensor("wq", [D, D], F32, kind="ExternalInput")
    wk_in = nc.dram_tensor("wk", [D, D], F32, kind="ExternalInput")
    wv_in = nc.dram_tensor("wv", [D, D], F32, kind="ExternalInput")
    wo_in = nc.dram_tensor("wo", [D, D], F32, kind="ExternalInput")
    w1_in = nc.dram_tensor("w1", [D, FF], F32, kind="ExternalInput")
    w2_in = nc.dram_tensor("w2", [FF, D], F32, kind="ExternalInput")
    wg_in = nc.dram_tensor("wg", [FI, D], F32, kind="ExternalInput")
    wu_in = nc.dram_tensor("wu", [FI, D], F32, kind="ExternalInput")
    wd_in = nc.dram_tensor("wd", [D, FI], F32, kind="ExternalInput")
    bq_fm = nc.dram_tensor("bq_fm", [128, 8], F32, kind="ExternalInput")
    bk_row = nc.dram_tensor("bk_row", [1, D], F32, kind="ExternalInput")
    bv_row = nc.dram_tensor("bv_row", [1, D], F32, kind="ExternalInput")
    bo_fm = nc.dram_tensor("bo_fm", [128, 8], F32, kind="ExternalInput")
    b1_fm = nc.dram_tensor("b1_fm", [128, 32], F32, kind="ExternalInput")
    b2_fm = nc.dram_tensor("b2_fm", [128, 8], F32, kind="ExternalInput")
    g1_fm = nc.dram_tensor("g1_fm", [128, 8], F32, kind="ExternalInput")
    be1_fm = nc.dram_tensor("be1_fm", [128, 8], F32, kind="ExternalInput")
    g2_fm = nc.dram_tensor("g2_fm", [128, 8], F32, kind="ExternalInput")
    be2_fm = nc.dram_tensor("be2_fm", [128, 8], F32, kind="ExternalInput")
    g2_row = nc.dram_tensor("g2_row", [1, D], F32, kind="ExternalInput")
    be2_row = nc.dram_tensor("be2_row", [1, D], F32, kind="ExternalInput")
    ident_in = nc.dram_tensor("ident", [128, 128], F32, kind="ExternalInput")
    out = nc.dram_tensor("out", [T, D], F32, kind="ExternalOutput")

    h_hi_d = dram("h_hi_d", [NS, 128, SL], BF16)
    h_lo_d = dram("h_lo_d", [NS, 128, SL], BF16)
    q_d = dram("q_d", [NS, 128, SL], F32)
    x1_hi_d = dram("x1_hi_d", [NS, 128, SL], BF16)
    x1_lo_d = dram("x1_lo_d", [NS, 128, SL], BF16)
    xq_hi_d = dram("xq_hi_d", [NS, 128, SL], BF16)
    xq_lo_d = dram("xq_lo_d", [NS, 128, SL], BF16)
    ya_d = dram("ya_d", [NS, 128, SL], F32)
    yb_d = dram("yb_d", [NS, 128, SL], F32)
    x2_d = dram("x2_d", [NS, 128, SL], F32)
    y2a_d = dram("y2a_d", [NS, 128, SL], F32)
    y2b_d = dram("y2b_d", [NS, 128, SL], F32)
    garow_d = dram("garow_d", [4, D], F32)
    w4_d = dram("w4_d", [3, 6, 128, 4096], F32)
    w4b_d = dram("w4b_d", [36, 128, 4096], BF16)
    w12_d = dram("w12_d", [8, 2, 128, W_COLS], BF16)
    cc_in = dram("cc_in", [128, 8 * 65], F32)
    cc_out = dram("cc_out", [128, 8 * 65], F32)

    B = {}

    def sb(name, cols, dt=F32, parts=128):
        h = nc.alloc_sbuf_tensor(name, [parts, cols], dt)
        B[name] = Buf(name)
        return h

    ident = sb("ident_s", 128)
    onesr = sb("onesr", 128, parts=1)
    onescol = sb("onescol", 1)
    cfm = sb("cfm_s", 8)
    silc = sb("silc", 8)
    modv = sb("modv", 64)
    sgn8 = sb("sgn8", 16)
    vecs = sb("vecs", 80)
    be2s = sb("be2s", 8)
    r6c = sb("r6c", 2)
    m1b = sb("m1b", 32)
    r1b = sb("r1b", 32)
    m4b = sb("m4b", 4)
    r4b = sb("r4b", 4)
    minis = sb("minis", 32)
    zb = sb("zb", 64)
    kvexp = sb("kvexp", 8 * 130)
    rep = sb("rep", 4 * D)
    xb = sb("xb", SL)
    tb = sb("tb", SL)
    kb = sb("kb", SL)
    qb = sb("qb", SL)
    a2b = sb("a2b", SL)
    x1b = sb("x1b", SL)
    hhi = sb("hhi", SL, BF16)
    hlo = sb("hlo", SL, BF16)
    wa_hi = sb("wa_hi", W_COLS, BF16)
    wa_lo = sb("wa_lo", W_COLS, BF16)
    wb_hi = sb("wb_hi", W_COLS, BF16)
    wb_lo = sb("wb_lo", W_COLS, BF16)
    ub = sb("ub", SLAB)
    t1s = sb("t1s", SLAB)
    t2s = sb("t2s", SLAB)

    pb = [nc.alloc_psum_tensor(f"pbk{i}", [128, 512], F32) for i in range(8)]
    PB = [Buf(f"pb{i}") for i in range(8)]
    Bcc = Buf("cc")
    Bw4 = Buf("w4")
    WBH = [Buf(f"wbh{i}") for i in range(3)]
    WBL = [Buf(f"wbl{i}") for i in range(3)]
    WAH = [Buf(f"wah{i}") for i in range(2)]
    WAL = [Buf(f"wal{i}") for i in range(2)]
    Bw12 = [Buf(f"w12_{i}") for i in range(8)]
    Bw4b = [Buf(f"w4b_{i}") for i in range(18)]
    Bgarow = Buf("garow")
    SP = {k: Buf("sp_" + k) for k in
          ("h", "q", "x1", "xq", "ya", "yb", "x2", "y2a", "y2b", "out")}

    def vcol(i):
        return vecs[:, i:i + 1]

    cb.dml(lambda e: e.dma_start(out=ident[:, :], in_=ident_in.ap()[:, :]),
           wr=[B["ident_s"]])
    cb.dml(lambda e: e.dma_start(out=cfm[:, :], in_=cfm_in.ap()[:, :]),
           wr=[B["cfm_s"]])
    for i, src in enumerate((bq_fm, bo_fm, b2_fm, g1_fm, be1_fm, g2_fm)):
        cb.dml(lambda e, i=i, src=src: e.dma_start(
            out=vecs[:, i * 8:(i + 1) * 8], in_=src.ap()[:, :]),
            wr=[B["vecs"]])
    cb.dml(lambda e: e.dma_start(out=vecs[:, 48:80], in_=b1_fm.ap()[:, :]),
           wr=[B["vecs"]])
    cb.dml(lambda e: e.dma_start(out=be2s[:, :], in_=be2_fm.ap()[:, :]),
           wr=[B["be2s"]])
    cb.v(lambda e: e.memset(r6c[:, 0:1], 6.0), wr=[B["r6c"]])
    cb.v(lambda e: e.reciprocal(r6c[:, 0:1], r6c[:, 0:1]), rd=[B["r6c"]],
         wr=[B["r6c"]])
    cb.v(lambda e: e.memset(r6c[:, 1:2], EPS_LN), wr=[B["r6c"]])
    cb.v(lambda e: e.memset(onesr[:, :], 1.0), wr=[B["onesr"]])
    cb.v(lambda e: e.memset(onescol[:, :], 1.0), wr=[B["onescol"]])

    cb.a(lambda e: e.activation(silc[:, :], cfm[:, :], AF.Silu),
         rd=[B["cfm_s"]], wr=[B["silc"]])
    for g in range(12):
        def ldw(e, g=g):
            o = ap3(xb, 0, [[SL, 128], [512, 8], [1, 512]])
            i = ap3(wmod_in, g * 512,
                    [[6 * D, 128], [128 * 6 * D, 8], [1, 512]])
            return e.dma_start(out=o, in_=i)
        cb.dml(ldw, wr=[B["xb"]])

        def mm(e, g=g):
            r = None
            for mc in range(4):
                for kc in range(KC):
                    r = e.matmul(
                        pb[0][:, g * 4 + mc:g * 4 + mc + 1],
                        xb[:, kc * 512 + mc * 128:kc * 512 + (mc + 1) * 128],
                        silc[:, kc:kc + 1],
                        start=(kc == 0), stop=(kc == KC - 1))
            return r
        cb.t(mm, rd=[B["xb"], B["silc"]], wr=[PB[0]])
    cb.v(lambda e: e.tensor_copy(modv[:, 0:48], pb[0][:, 0:48]),
         rd=[PB[0]], wr=[B["modv"]])
    cb.v(lambda e: e.tensor_scalar_add(modv[:, M_CA1:M_CA1 + 8],
                                       modv[:, M_CA:M_CA + 8], 1.0),
         rd=[B["modv"]], wr=[B["modv"]])
    cb.v(lambda e: e.tensor_scalar_add(modv[:, M_CB1:M_CB1 + 8],
                                       modv[:, M_CB:M_CB + 8], 1.0),
         rd=[B["modv"]], wr=[B["modv"]])
    for j, col in enumerate((M_GA, M_GB)):
        cb.v(lambda e, j=j, col=col: e.tensor_scalar(
            sgn8[:, j * 8:(j + 1) * 8], modv[:, col:col + 8], 0.0, 2.0,
            OP.is_ge, OP.mult), rd=[B["modv"]], wr=[B["sgn8"]])
        cb.v(lambda e, j=j: e.tensor_scalar_sub(
            sgn8[:, j * 8:(j + 1) * 8], sgn8[:, j * 8:(j + 1) * 8], 1.0),
            rd=[B["sgn8"]], wr=[B["sgn8"]])
        cb.a(lambda e, col=col: e.activation(
            modv[:, col:col + 8], modv[:, col:col + 8], AF.Abs),
            rd=[B["modv"]], wr=[B["modv"]])
        cb.v(lambda e, col=col: e.tensor_scalar_max(
            modv[:, col:col + 8], modv[:, col:col + 8], 0.25),
            rd=[B["modv"]], wr=[B["modv"]])
        cb.v(lambda e, j=j, col=col: e.tensor_mul(
            modv[:, col:col + 8], modv[:, col:col + 8],
            sgn8[:, j * 8:(j + 1) * 8]),
            rd=[B["modv"], B["sgn8"]], wr=[B["modv"]])

    for j, col in enumerate((M_GA, M_GB, M_CB1, M_SB)):
        def st(e, j=j, col=col):
            o = ap3(garow_d, j * D, [[1, 128], [128, 8]])
            return e.dma_start(out=o, in_=modv[:, col:col + 8],
                               allow_slow_non_contiguous=True)
        cb.dms(st, rd=[B["modv"]], wr=[Bgarow])

    def replicate(dst_col, src_dram, src_off):
        def f(e):
            o = ap3(rep, dst_col, [[4 * D, 128], [1, D]])
            i = ap3(src_dram, src_off, [[0, 128], [1, D]])
            return e.dma_start(out=o, in_=i)
        cb.dml(f, rd=[Bgarow], wr=[B["rep"]])

    replicate(R_BK, bk_row, 0)
    replicate(R_BV, bv_row, 0)

    def _bl(b):
        return list(b) if isinstance(b, (list, tuple)) else [b]

    def load_split(w_dram, wcols, row0, col0, nrows, ncols, hi, lo, hib, lob,
                   dcol0=0):
        hib, lob = _bl(hib), _bl(lob)
        nkc = nrows // 128

        def cast(e):
            o = ap3(hi, dcol0, [[W_COLS, 128], [ncols, nkc], [1, ncols]])
            i = ap3(w_dram, row0 * wcols + col0,
                    [[wcols, 128], [128 * wcols, nkc], [1, ncols]])
            return e.dma_start(out=o, in_=i)
        cb.dms(cast, wr=hib)
        hkc = max(1, nkc // 2)
        nhalf = (nkc + hkc - 1) // hkc
        for hh in range(nhalf):
            def full(e, hh=hh):
                o = ap3(xb, 0, [[SL, 128], [ncols, hkc], [1, ncols]])
                i = ap3(w_dram, (row0 + hh * hkc * 128) * wcols + col0,
                        [[wcols, 128], [128 * wcols, hkc], [1, ncols]])
                return e.dma_start(out=o, in_=i)
            cb.dml(full, wr=[B["xb"]])
            cb.v(lambda e, hh=hh: e.tensor_sub(
                ap3(lo, dcol0 + hh * hkc * ncols,
                    [[W_COLS, 128], [1, hkc * ncols]]),
                ap3(xb, 0, [[SL, 128], [1, hkc * ncols]]),
                ap3(hi, dcol0 + hh * hkc * ncols,
                    [[W_COLS, 128], [1, hkc * ncols]])),
                rd=[B["xb"]] + hib, wr=lob)

    load_split(wq_in, D, 0, 0, D, 1024, wa_hi, wa_lo, B["wa_hi"], B["wa_lo"])

    cb.v(lambda e: e.memset(qb[:, 0:520], 0.0), wr=[B["qb"]])
    cb.v(lambda e: e.memset(kvexp[:, :], 0.0), wr=[B["kvexp"]])

    def ln_stats_tm(src, src_b, sqout, sqout_b, mdst, mdst_b, rdst, rdst_b,
                    col0, eps):
        for c in range(NCT):
            cb.v(lambda e, c=c: e.tensor_reduce(
                minis[:, c:c + 1], src[:, c * D:(c + 1) * D], AX.X, OP.add),
                rd=[src_b], wr=[B["minis"]])
            cb.a(lambda e, c=c: e.activation(
                sqout[:, c * D:(c + 1) * D], src[:, c * D:(c + 1) * D],
                AF.Square, accum_out=minis[:, 4 + c:5 + c]),
                rd=[src_b], wr=[sqout_b, B["minis"]])
        cb.v(lambda e: e.tensor_scalar_mul(
            mdst[:, col0:col0 + NCT], minis[:, 0:NCT], 1.0 / D),
            rd=[B["minis"]], wr=[mdst_b])
        cb.v(lambda e: e.tensor_mul(
            minis[:, 8:8 + NCT], mdst[:, col0:col0 + NCT],
            mdst[:, col0:col0 + NCT]), rd=[mdst_b], wr=[B["minis"]])
        cb.v(lambda e: e.scalar_tensor_tensor(
            minis[:, 12:12 + NCT], minis[:, 4:4 + NCT], 1.0 / D,
            minis[:, 8:8 + NCT], OP.mult, OP.subtract),
            rd=[B["minis"]], wr=[B["minis"]])
        cb.a(lambda e: e.activation(
            minis[:, 16:16 + NCT], minis[:, 12:12 + NCT], AF.Sqrt,
            bias=r6c[:, 1:2]),
            rd=[B["minis"], B["r6c"]], wr=[B["minis"]])
        cb.v(lambda e: e.reciprocal(
            rdst[:, col0:col0 + NCT], minis[:, 16:16 + NCT]),
            rd=[B["minis"]], wr=[rdst_b])

    def ln_apply_tm(src, src_b, dst, dst_b, mdst, mdst_b, rdst, rdst_b, col0,
                    rw):
        for c in range(NCT):
            def f(e, c=c):
                r_b = ap3(rdst, col0 + c, [[rw, 128], [0, D]])
                return e.scalar_tensor_tensor(
                    dst[:, c * D:(c + 1) * D], src[:, c * D:(c + 1) * D],
                    mdst[:, col0 + c:col0 + c + 1], r_b, OP.subtract, OP.mult)
            cb.v(f, rd=[src_b, mdst_b, rdst_b], wr=[dst_b])

    def tr_tm_to_fm(src, src_b, dsthi, dsthi_b, dstlo, dstlo_b, banks,
                    affine=None, aff_tmp=None, aff_tmp_b=None, dcol0=0,
                    extra_wr_hi=(), extra_wr_lo=()):
        for f in range(KC):
            bk = banks[f % 2]
            pbk = pb[bk]

            def tr(e, f=f, pbk=pbk):
                r = None
                for c in range(NCT):
                    r = e.transpose(
                        pbk[:, c * 128:(c + 1) * 128],
                        src[:, c * D + f * 128:c * D + (f + 1) * 128],
                        ident[:, :])
                return r
            cb.t(tr, rd=[src_b, B["ident_s"]], wr=[PB[bk]])
            if affine is not None:
                acol, scol = affine

                def ev(e, f=f, pbk=pbk, acol=acol, scol=scol):
                    sa_b = ap3(modv, scol + f, [[64, 128], [0, SLAB]])
                    return e.scalar_tensor_tensor(
                        aff_tmp[:, f * SLAB:(f + 1) * SLAB], pbk[:, 0:SLAB],
                        modv[:, acol + f:acol + f + 1], sa_b, OP.mult, OP.add)
                cb.v(ev, rd=[PB[bk], B["modv"]], wr=[aff_tmp_b])
                cb.a(lambda e, f=f: e.activation(
                    dsthi[:, dcol0 + f * SLAB:dcol0 + (f + 1) * SLAB],
                    aff_tmp[:, f * SLAB:(f + 1) * SLAB], AF.Identity),
                    rd=[aff_tmp_b], wr=[dsthi_b])
                cb.v(lambda e, f=f: e.tensor_sub(
                    dstlo[:, dcol0 + f * SLAB:dcol0 + (f + 1) * SLAB],
                    aff_tmp[:, f * SLAB:(f + 1) * SLAB],
                    dsthi[:, dcol0 + f * SLAB:dcol0 + (f + 1) * SLAB]),
                    rd=[aff_tmp_b, dsthi_b], wr=[dstlo_b])
            else:
                cb.a(lambda e, f=f, pbk=pbk: e.activation(
                    dsthi[:, dcol0 + f * SLAB:dcol0 + (f + 1) * SLAB],
                    pbk[:, 0:SLAB],
                    AF.Identity), rd=[PB[bk]],
                    wr=[dsthi_b] + list(extra_wr_hi))
                cb.v(lambda e, f=f, pbk=pbk: e.tensor_sub(
                    dstlo[:, dcol0 + f * SLAB:dcol0 + (f + 1) * SLAB],
                    pbk[:, 0:SLAB],
                    dsthi[:, dcol0 + f * SLAB:dcol0 + (f + 1) * SLAB]),
                    rd=[PB[bk], dsthi_b], wr=[dstlo_b] + list(extra_wr_lo))

    def mm3_fm(bank, lhs_hi, lhs_lo, lhsb, rhs_hi, rhs_lo, rhsb, fcol, wcol0,
               wcols, nkc=KC, rcol0=0, rstep=SLAB):
        pbk = pb[bank]

        def mm(e):
            r = None
            n = 0
            for (wsrc, xsrc) in ((lhs_hi, rhs_hi), (lhs_hi, rhs_lo),
                                 (lhs_lo, rhs_hi)):
                for kc in range(nkc):
                    r = e.matmul(
                        pbk[:, 0:SLAB],
                        wsrc[:, wcol0 + kc * wcols + fcol * 128:
                             wcol0 + kc * wcols + (fcol + 1) * 128],
                        xsrc[:, rcol0 + kc * rstep:rcol0 + kc * rstep + SLAB],
                        start=(n == 0), stop=(n == 3 * nkc - 1))
                    n += 1
            return r
        rds = []
        for b in (*lhsb, *rhsb):
            rds.extend(_bl(b))
        return cb.t(mm, rd=rds, wr=[PB[bank]])

    def fm_stats(srcbuf, src_b, tmp, tmp_b, eps):
        def f1(e):
            r = None
            for kc in range(KC):
                r = e.matmul(pb[7][0:1, 0:SLAB], onescol[:, :],
                             srcbuf[:, kc * SLAB:(kc + 1) * SLAB],
                             start=(kc == 0), stop=(kc == KC - 1))
            return r
        cb.t(f1, rd=[src_b, B["onescol"]], wr=[PB[7]])
        cb.a(lambda e: e.activation(tmp[:, :], srcbuf[:, :], AF.Square),
             rd=[src_b], wr=[tmp_b])

        def f2(e):
            r = None
            for kc in range(KC):
                r = e.matmul(pb[6][0:1, 0:SLAB], onescol[:, :],
                             tmp[:, kc * SLAB:(kc + 1) * SLAB],
                             start=(kc == 0), stop=(kc == KC - 1))
            return r
        cb.t(f2, rd=[tmp_b, B["onescol"]], wr=[PB[6]])
        cb.v(lambda e: e.tensor_scalar_mul(ub[0:1, 0:SLAB],
                                           pb[7][0:1, 0:SLAB], 1.0 / D),
             rd=[PB[7]], wr=[B["ub"]])
        cb.v(lambda e: e.tensor_mul(t1s[0:1, 0:SLAB], ub[0:1, 0:SLAB],
                                    ub[0:1, 0:SLAB]),
             rd=[B["ub"]], wr=[B["t1s"]])
        cb.v(lambda e: e.scalar_tensor_tensor(
            t2s[0:1, 0:SLAB], pb[6][0:1, 0:SLAB], 1.0 / D, t1s[0:1, 0:SLAB],
            OP.mult, OP.subtract), rd=[PB[6], B["t1s"]], wr=[B["t2s"]])
        cb.a(lambda e: e.activation(t1s[0:1, 0:SLAB], t2s[0:1, 0:SLAB],
                                    AF.Sqrt, bias=r6c[0:1, 1:2]),
             rd=[B["t2s"], B["r6c"]], wr=[B["t1s"]])
        cb.v(lambda e: e.reciprocal(t2s[0:1, 0:SLAB], t1s[0:1, 0:SLAB]),
             rd=[B["t1s"]], wr=[B["t2s"]])
        cb.v(lambda e: e.tensor_mul(t1s[0:1, 0:SLAB], ub[0:1, 0:SLAB],
                                    t2s[0:1, 0:SLAB]),
             rd=[B["ub"], B["t2s"]], wr=[B["t1s"]])

    def fm_apply(srcbuf, src_b, gcol, bvec, bvec_b, bvw, bcol0, dst, dst_b,
                 tmp, tmp_b):
        cb.t(lambda e: e.matmul(pb[7][:, 0:SLAB], onesr[:, :],
                                t2s[0:1, 0:SLAB], start=True, stop=True),
             rd=[B["t2s"], B["onesr"]], wr=[PB[7]])
        cb.t(lambda e: e.matmul(pb[6][:, 0:SLAB], onesr[:, :],
                                t1s[0:1, 0:SLAB], start=True, stop=True),
             rd=[B["t1s"], B["onesr"]], wr=[PB[6]])
        for f in range(KC):
            cb.v(lambda e, f=f: e.tensor_mul(
                tmp[:, f * SLAB:(f + 1) * SLAB],
                srcbuf[:, f * SLAB:(f + 1) * SLAB], pb[7][:, 0:SLAB]),
                rd=[src_b, PB[7]], wr=[tmp_b])
            cb.v(lambda e, f=f: e.tensor_sub(
                tmp[:, f * SLAB:(f + 1) * SLAB],
                tmp[:, f * SLAB:(f + 1) * SLAB], pb[6][:, 0:SLAB]),
                rd=[tmp_b, PB[6]], wr=[tmp_b])

            def f3(e, f=f):
                b_b = ap3(bvec, bcol0 + f, [[bvw, 128], [0, SLAB]])
                return e.scalar_tensor_tensor(
                    dst[:, f * SLAB:(f + 1) * SLAB],
                    tmp[:, f * SLAB:(f + 1) * SLAB],
                    vcol(gcol + f), b_b, OP.mult, OP.add)
            cb.v(f3, rd=[tmp_b, B["vecs"], bvec_b], wr=[dst_b])

    def fm_ln(srcbuf, src_b, gcol, bvec, bvec_b, bvw, bcol0, dst, dst_b, eps,
              tmp, tmp_b):
        fm_stats(srcbuf, src_b, tmp, tmp_b, eps)
        fm_apply(srcbuf, src_b, gcol, bvec, bvec_b, bvw, bcol0, dst, dst_b,
                 tmp, tmp_b)

    def quant_tm(src, src_b, sw, dst, dst_b, dw, col0, F,
                 s1, s1b, s1w, c1, s2, s2b, s2w, c2, s3, s3b, s3w, c3,
                 sc_off=0, pool=False):
        V = cb.p if pool else cb.v
        NB = F // 16
        sA = ap3(src, col0, [[sw, 128], [16, NB], [1, 16]])
        cb.v(lambda e: e.tensor_reduce(ub[:, sc_off:sc_off + NB], sA, AX.X,
                                       OP.max, apply_absolute_value=True),
             rd=[src_b], wr=[B["ub"]])
        cb.v(lambda e: e.tensor_scalar(ub[:, 256 + sc_off:256 + sc_off + NB],
                                       ub[:, sc_off:sc_off + NB],
                                       r6c[:, 0:1], 1e-12, OP.mult, OP.max),
             rd=[B["ub"], B["r6c"]], wr=[B["ub"]])
        cb.v(lambda e: e.reciprocal(ub[:, sc_off:sc_off + NB],
                                    ub[:, 256 + sc_off:256 + sc_off + NB]),
             rd=[B["ub"]], wr=[B["ub"]])
        vf = ap3(s1, c1, [[s1w, 128], [1, F]])
        v3 = ap3(s1, c1, [[s1w, 128], [16, NB], [1, 16]])
        if pool:
            V(lambda e: e.tensor_single_scalar(
                vf, ap3(src, col0, [[sw, 128], [1, F]]), 0.0, OP.abs_max),
              rd=[src_b], wr=[s1b])
        else:
            cb.a(lambda e: e.activation(
                vf, ap3(src, col0, [[sw, 128], [1, F]]), AF.Abs),
                rd=[src_b], wr=[s1b])
        rsb = ap3(ub, sc_off, [[SLAB, 128], [1, NB], [0, 16]])
        V(lambda e: e.tensor_tensor(v3, v3, rsb, OP.mult),
          rd=[s1b, B["ub"]], wr=[s1b])
        tf = ap3(s2, c2, [[s2w, 128], [1, F]])
        kf = ap3(s3, c3, [[s3w, 128], [1, F]])
        V(lambda e: e.tensor_single_scalar(tf, vf, 0.25, OP.is_gt),
          rd=[s1b], wr=[s2b])
        for thr in (0.75, 1.25, 1.75):
            V(lambda e, thr=thr: e.scalar_tensor_tensor(
                tf, vf, thr, tf, OP.is_gt, OP.add), rd=[s1b, s2b], wr=[s2b])
        V(lambda e: e.tensor_single_scalar(kf, vf, 2.5, OP.is_gt),
          rd=[s1b], wr=[s3b])
        for thr in (3.5, 5.0, 5.0):
            V(lambda e, thr=thr: e.scalar_tensor_tensor(
                kf, vf, thr, kf, OP.is_gt, OP.add), rd=[s1b, s3b], wr=[s3b])
        V(lambda e: e.scalar_tensor_tensor(tf, tf, 0.5, kf, OP.mult, OP.add),
          rd=[s2b, s3b], wr=[s2b])
        if pool:
            V(lambda e: e.tensor_single_scalar(
                kf, ap3(src, col0, [[sw, 128], [1, F]]), 0.0, OP.is_lt),
              rd=[src_b], wr=[s3b])
            V(lambda e: e.tensor_single_scalar(
                vf, ap3(src, col0, [[sw, 128], [1, F]]), 0.0, OP.is_gt),
              rd=[src_b], wr=[s1b])
            V(lambda e: e.tensor_tensor(vf, vf, kf, OP.subtract),
              rd=[s1b, s3b], wr=[s1b])
        else:
            cb.a(lambda e: e.activation(
                vf, ap3(src, col0, [[sw, 128], [1, F]]), AF.Sign),
                rd=[src_b], wr=[s1b])
        V(lambda e: e.tensor_tensor(tf, tf, vf, OP.mult),
          rd=[s2b, s1b], wr=[s2b])
        t3 = ap3(s2, c2, [[s2w, 128], [16, NB], [1, 16]])
        d3 = ap3(dst, col0, [[dw, 128], [16, NB], [1, 16]])
        ssb = ap3(ub, 256 + sc_off, [[SLAB, 128], [1, NB], [0, 16]])
        V(lambda e: e.tensor_tensor(d3, t3, ssb, OP.mult),
          rd=[s2b, B["ub"]], wr=[dst_b])

    def prep_fp4_weight(w_dram, wcols, i0, isz, along_rows, mat, stripe,
                        thunks):
        if along_rows:
            nrc, ncols = isz // 128, wcols
            off = i0 * wcols
        else:
            nrc, ncols = D // 128, isz
            off = i0

        def ld(e):
            o = ap3(x1b, 0, [[SL, 128], [ncols, nrc], [1, ncols]])
            i = ap3(w_dram, off, [[wcols, 128], [128 * wcols, nrc],
                                  [1, ncols]])
            return e.dma_start(out=o, in_=i)
        thunks.append(lambda: cb.dml(ld, wr=[B["x1b"]]))
        total = nrc * ncols
        for p0 in range(0, total, 1024):
            pf = min(1024, total - p0)
            thunks.append(lambda p0=p0, pf=pf: quant_tm(
                x1b, B["x1b"], SL, x1b, B["x1b"], SL, p0, pf,
                a2b, B["a2b"], SL, 0, a2b, B["a2b"], SL, 1024,
                a2b, B["a2b"], SL, 2048, sc_off=64))
        nkc = ncols // 128
        for kc in range(nkc):
            for grp in range((nrc + 3) // 4):
                bank = 5 + grp % 2
                pbk = pb[bank]
                cnt = min(4, nrc - grp * 4)

                def tr(e, kc=kc, grp=grp, cnt=cnt, pbk=pbk):
                    r = None
                    for j in range(cnt):
                        rc = grp * 4 + j
                        r = e.transpose(
                            pbk[:, j * 128:(j + 1) * 128],
                            x1b[:, rc * ncols + kc * 128:
                                rc * ncols + (kc + 1) * 128],
                            ident[:, :])
                    return r

                def group(tr=tr, bank=bank, pbk=pbk, cnt=cnt,
                          kc=kc, grp=grp, mat=mat, stripe=stripe, nrc=nrc):
                    cb.t(tr, rd=[B["x1b"], B["ident_s"]], wr=[PB[bank]])
                    cb.v(lambda e, cnt=cnt, pbk=pbk: e.tensor_copy(
                        a2b[:, 3072:3072 + cnt * 128], pbk[:, 0:cnt * 128]),
                        rd=[PB[bank]], wr=[B["a2b"]])
                    dcol = kc * (nrc * 128) + grp * 512

                    def stw(e, mat=mat, stripe=stripe, dcol=dcol, cnt=cnt):
                        o = ap3(w4_d, (mat * 6 + stripe) * 128 * 4096 + dcol,
                                [[4096, 128], [1, cnt * 128]])
                        return e.dma_start(out=o,
                                           in_=a2b[:, 3072:3072 + cnt * 128])
                    cb.dms(stw, rd=[B["a2b"]], wr=[Bw4])
                thunks.append(group)

    STRIPES4 = [(i, min(512, FI - i)) for i in range(0, FI, 512)]
    prep_thunks = []
    for st4, (i0_, isz_) in enumerate(STRIPES4):
        prep_fp4_weight(wg_in, D, i0_, isz_, True, 0, st4, prep_thunks)
        prep_fp4_weight(wu_in, D, i0_, isz_, True, 1, st4, prep_thunks)
        prep_fp4_weight(wd_in, FI, i0_, isz_, False, 2, st4, prep_thunks)

    def drain_prep(n):
        for _ in range(n):
            if prep_thunks:
                prep_thunks.pop(0)()

    for s in range(NS):
        def ldx(e, s=s):
            o = ap3(xb, 0, [[SL, 128], [D, NCT], [1, D]])
            i = ap3(xin, s * SLAB * D, [[D, 128], [128 * D, NCT], [1, D]])
            return e.dma_start(out=o, in_=i)
        cb.dml(ldx, wr=[B["xb"]])
        ln_stats_tm(xb, B["xb"], tb, B["tb"], m1b, B["m1b"], r1b, B["r1b"],
                    s * NCT, EPS_LN)
        ln_apply_tm(xb, B["xb"], tb, B["tb"], m1b, B["m1b"], r1b, B["r1b"],
                    s * NCT, 32)
        tr_tm_to_fm(tb, B["tb"], hhi, B["hhi"], hlo, B["hlo"], (1, 2),
                    affine=(M_CA1, M_SA), aff_tmp=kb, aff_tmp_b=B["kb"])
        cb.dms(lambda e, s=s: e.dma_start(out=h_hi_d.ap()[s], in_=hhi[:, :]),
               rd=[B["hhi"]], wr=[SP["h"]])
        cb.dms(lambda e, s=s: e.dma_start(out=h_lo_d.ap()[s], in_=hlo[:, :]),
               rd=[B["hlo"]], wr=[SP["h"]])
        for f in range(KC):
            bank = 3 + f % 2
            mm3_fm(bank, wa_hi, wa_lo, (B["wa_hi"], B["wa_lo"]),
                   hhi, hlo, (B["hhi"], B["hlo"]), f, 0, 1024)
            cb.a(lambda e, f=f, bank=bank: e.activation(
                ub[:, :], pb[bank][:, 0:SLAB], AF.Identity,
                bias=vcol(C_BQ + f)), rd=[PB[bank], B["vecs"]], wr=[B["ub"]])
            cb.v(lambda e: e.tensor_scalar_min(t1s[:, :], ub[:, :], 0.0),
                 rd=[B["ub"]], wr=[B["t1s"]])
            cb.a(lambda e: e.activation(t2s[:, :], t1s[:, :], AF.Exp),
                 rd=[B["t1s"]], wr=[B["t2s"]])
            cb.v(lambda e, f=f: e.scalar_tensor_tensor(
                qb[:, f * SLAB:(f + 1) * SLAB], ub[:, :], 0.0, t2s[:, :],
                OP.max, OP.add), rd=[B["ub"], B["t2s"]], wr=[B["qb"]])
        cb.dms(lambda e, s=s: e.dma_start(out=q_d.ap()[s], in_=qb[:, :]),
               rd=[B["qb"]], wr=[SP["q"]])

    if stop_after == "a1":
        return _finish_queues(cb, nc)
    load_split(wk_in, D, 0, 0, D, 1024, wa_hi, wa_lo, B["wa_hi"], B["wa_lo"])
    load_split(wv_in, D, 0, 0, D, 1024, wb_hi, wb_lo, WBH[0], WBL[0])
    for s in range(NS):
        cb.dml(lambda e, s=s: e.dma_start(out=hhi[:, :], in_=h_hi_d.ap()[s]),
               rd=[SP["h"]], wr=[B["hhi"]])
        cb.dml(lambda e, s=s: e.dma_start(out=hlo[:, :], in_=h_lo_d.ap()[s]),
               rd=[SP["h"]], wr=[B["hlo"]])
        for proj in ("k", "v"):
            whi_, wlo_ = (wa_hi, wa_lo) if proj == "k" else (wb_hi, wb_lo)
            whib = B["wa_hi"] if proj == "k" else WBH[0]
            wlob = B["wa_lo"] if proj == "k" else WBL[0]
            for c in range(NCT):
                for nh in range(2):
                    bank = 3 + (c * 2 + nh) % 2
                    pbk = pb[bank]

                    def mm(e, c=c, nh=nh, pbk=pbk, whi_=whi_, wlo_=wlo_):
                        r = None
                        n = 0
                        for (wsrc, xsrc) in ((whi_, hhi), (wlo_, hhi),
                                             (whi_, hlo)):
                            for kc in range(KC):
                                r = e.matmul(
                                    pbk[:, 0:512],
                                    xsrc[:, kc * SLAB + c * 128:
                                         kc * SLAB + (c + 1) * 128],
                                    wsrc[:, kc * 1024 + nh * 512:
                                         kc * 1024 + (nh + 1) * 512],
                                    start=(n == 0), stop=(n == 3 * KC - 1))
                                n += 1
                        return r
                    cb.t(mm, rd=[whib, wlob, B["hhi"], B["hlo"]],
                         wr=[PB[bank]])
                    if proj == "k":
                        cb.v(lambda e, nh=nh, pbk=pbk: e.tensor_add(
                            ub[:, :], pbk[:, 0:512],
                            rep[:, R_BK + nh * 512:R_BK + (nh + 1) * 512]),
                            rd=[PB[bank], B["rep"]], wr=[B["ub"]])
                        cb.v(lambda e: e.tensor_scalar_min(
                            t1s[:, :], ub[:, :], 0.0),
                            rd=[B["ub"]], wr=[B["t1s"]])
                        cb.a(lambda e: e.activation(t2s[:, :], t1s[:, :],
                                                    AF.Exp),
                             rd=[B["t1s"]], wr=[B["t2s"]])
                        cb.v(lambda e, c=c, nh=nh: e.scalar_tensor_tensor(
                            kb[:, c * D + nh * 512:c * D + (nh + 1) * 512],
                            ub[:, :], 0.0, t2s[:, :], OP.max, OP.add),
                            rd=[B["ub"], B["t2s"]], wr=[B["kb"]])
                    else:
                        cb.v(lambda e, c=c, nh=nh, pbk=pbk: e.tensor_add(
                            tb[:, c * D + nh * 512:c * D + (nh + 1) * 512],
                            pbk[:, 0:512],
                            rep[:, R_BV + nh * 512:R_BV + (nh + 1) * 512]),
                            rd=[PB[bank], B["rep"]], wr=[B["tb"]])
        for c in range(NCT):
            bank = 5 + c % 2
            pbk = pb[bank]
            p7off = (c % 2) * 65

            def mmkv(e, c=c, pbk=pbk, p7off=p7off):
                r = None
                for hh in range(H):
                    hp, half = hh // 2, hh % 2
                    dst = (pbk[half * 64:(half + 1) * 64,
                               hp * 65:hp * 65 + 64] if hp < 7 else
                           pb[7][half * 64:(half + 1) * 64,
                                 p7off:p7off + 64])
                    r = e.matmul(
                        dst,
                        kb[:, c * D + hh * 64:c * D + (hh + 1) * 64],
                        tb[:, c * D + hh * 64:c * D + (hh + 1) * 64],
                        start=True, stop=True)
                for hh in range(H):
                    hp, half = hh // 2, hh % 2
                    dst = (pbk[half * 64:(half + 1) * 64,
                               hp * 65 + 64:hp * 65 + 65] if hp < 7 else
                           pb[7][half * 64:(half + 1) * 64,
                                 p7off + 64:p7off + 65])
                    r = e.matmul(
                        dst,
                        kb[:, c * D + hh * 64:c * D + (hh + 1) * 64],
                        onescol[:, :],
                        start=True, stop=True)
                return r
            cb.t(mmkv, rd=[B["kb"], B["tb"], B["onescol"]],
                 wr=[PB[bank], PB[7]])
            cb.v(lambda e, pbk=pbk: e.tensor_add(
                qb[:, 0:455], qb[:, 0:455], pbk[:, 0:455]),
                rd=[PB[bank], B["qb"]], wr=[B["qb"]])
            cb.v(lambda e, p7off=p7off: e.tensor_add(
                qb[:, 455:520], qb[:, 455:520],
                pb[7][:, p7off:p7off + 65]),
                rd=[PB[7], B["qb"]], wr=[B["qb"]])
        drain_prep(4)

    if stop_after == "a2":
        return _finish_queues(cb, nc)
    drain_prep(len(prep_thunks))

    cb.dms(lambda e: e.dma_start(out=cc_in.ap()[:, :], in_=qb[:, 0:520]),
           rd=[B["qb"]], wr=[Bcc])

    if skip_collective:
        cb.dms(lambda e: e.dma_start(out=cc_out.ap()[:, :],
                                     in_=cc_in.ap()[:, :]),
               rd=[Bcc], wr=[Bcc])
    else:
        def ccf(e):
            return e.collective_compute(
                "AllReduce", OP.add,
                replica_groups=[[0, 1, 2, 3], [4, 5, 6, 7]],
                ins=[cc_in.ap().opt()], outs=[cc_out.ap().opt()])
        cb._emit("gpsimd", ccf, [Bcc], [Bcc], 1, False, semkey="cc")
    cb.dml(lambda e: e.dma_start(out=qb[:, 0:520], in_=cc_out.ap()[:, :]),
           rd=[Bcc], wr=[B["qb"]])

    def expand_kv(half):
        def f(e):
            o = ap3(kvexp, half * (64 * 8 * 130 + 65),
                    [[8 * 130, 64], [130, 8], [1, 65]])
            i = ap3(qb, half * 64 * SL, [[SL, 64], [65, 8], [1, 65]])
            return e.dma_start(out=o, in_=i)
        cb.dms(f, rd=[B["qb"]], wr=[B["kvexp"]])
    expand_kv(0)
    expand_kv(1)

    for j, rc in enumerate((R_GA, R_GB, R_CB1, R_SB)):
        replicate(rc, garow_d, j * D)

    if stop_after == "cc":
        return _finish_queues(cb, nc)
    load_split(wo_in, D, 0, 0, D, 1024, wa_hi, wa_lo, B["wa_hi"], B["wa_lo"])
    x1hi = wb_hi
    x1lo = wb_lo

    def bc1_A(s, dst, dst_b):
        cb.dml(lambda e, s=s: e.dma_start(out=qb[:, :], in_=q_d.ap()[s]),
               rd=[SP["q"]], wr=[B["qb"]])
        cb.dml(lambda e, s=s: e.dma_start(out=hhi[:, :], in_=h_hi_d.ap()[s]),
               rd=[SP["h"]], wr=[B["hhi"]])
        cb.dml(lambda e, s=s: e.dma_start(out=hlo[:, :], in_=h_lo_d.ap()[s]),
               rd=[SP["h"]], wr=[B["hlo"]])
        for c in range(NCT):
            def mmo(e, c=c):
                r = None
                for hp in range(8):
                    b_i = hp // 3
                    off = (hp % 3) * 130
                    r = e.matmul(
                        pb[b_i][:, off:off + 130],
                        qb[:, hp * SLAB + c * 128:hp * SLAB + (c + 1) * 128],
                        kvexp[:, hp * 130:(hp + 1) * 130],
                        start=True, stop=True)
                return r
            cb.t(mmo, rd=[B["qb"], B["kvexp"]], wr=[PB[0], PB[1], PB[2]])
            for b_i, nh in ((0, 6), (1, 6), (2, 4)):
                cb.v(lambda e, b_i=b_i, nh=nh: e.tensor_scalar_add(
                    zb[:, b_i * 6:b_i * 6 + nh],
                    ap3(pb[b_i], 64, [[512, 128], [65, nh]]), EPS_Z),
                    rd=[PB[b_i]], wr=[B["zb"]])
            cb.v(lambda e: e.reciprocal(zb[:, 32:48], zb[:, 0:16]),
                 rd=[B["zb"]], wr=[B["zb"]])
            for b_i, nh in ((0, 6), (1, 6), (2, 4)):
                def f(e, c=c, b_i=b_i, nh=nh):
                    o = ap3(xb, c * D + b_i * 6 * 64,
                            [[SL, 128], [64, nh], [1, 64]])
                    i = ap3(pb[b_i], 0, [[512, 128], [65, nh], [1, 64]])
                    i2 = ap3(zb, 32 + b_i * 6, [[64, 128], [1, nh], [0, 64]])
                    return e.tensor_tensor(o, i, i2, OP.mult)
                cb.v(f, rd=[PB[b_i], B["zb"]], wr=[B["xb"]])
        tr_tm_to_fm(xb, B["xb"], wb_hi, WBH[2], wb_lo, WBL[2],
                    (3, 4), dcol0=4096)
        for f in range(KC):
            bank = 5 + f % 2
            mm3_fm(bank, wa_hi, wa_lo, (B["wa_hi"], B["wa_lo"]),
                   wb_hi, wb_lo, (WBH[2], WBL[2]), f, 0, 1024,
                   rcol0=4096)
            cb.a(lambda e, f=f, bank=bank: e.activation(
                xb[:, 0:SLAB], pb[bank][:, 0:SLAB],
                AF.Identity, bias=vcol(C_BO + f)),
                rd=[PB[bank], B["vecs"]], wr=[B["xb"]])
            cb.v(lambda e, f=f: e.tensor_add(
                xb[:, SLAB:2 * SLAB], xb[:, 0:SLAB],
                hhi[:, f * SLAB:(f + 1) * SLAB]),
                rd=[B["xb"], B["hhi"]], wr=[B["xb"]])
            cb.v(lambda e, f=f: e.tensor_add(
                dst[:, f * SLAB:(f + 1) * SLAB], xb[:, SLAB:2 * SLAB],
                hlo[:, f * SLAB:(f + 1) * SLAB]),
                rd=[B["xb"], B["hlo"]], wr=[dst_b])

    def bc1_P(s, srcbuf, src_b):
        fm_apply(srcbuf, src_b, C_G1, vecs, B["vecs"], 80, C_BE1,
                 x1b, B["x1b"], tb, B["tb"])
        cb.a(lambda e: e.activation(x1hi[:, 0:SL], x1b[:, :], AF.Identity),
             rd=[B["x1b"]], wr=[WBH[0], WBH[1]])
        cb.v(lambda e: e.tensor_sub(x1lo[:, 0:SL], x1b[:, :],
                                    x1hi[:, 0:SL]),
             rd=[B["x1b"], WBH[0], WBH[1]], wr=[WBL[0], WBL[1]])
        cb.dms(lambda e, s=s: e.dma_start(out=x1_hi_d.ap()[s],
                                          in_=x1hi[:, 0:SL]),
               rd=[WBH[0], WBH[1]], wr=[SP["x1"]])
        cb.dms(lambda e, s=s: e.dma_start(out=x1_lo_d.ap()[s],
                                          in_=x1lo[:, 0:SL]),
               rd=[WBL[0], WBL[1]], wr=[SP["x1"]])

    BC1_PIPE = True
    if BC1_PIPE:
        prev = None
        for s in range(NS):
            dst, dst_b = (a2b, B["a2b"]) if s % 2 == 0 else (kb, B["kb"])
            bc1_A(s, dst, dst_b)
            if prev is not None:
                bc1_P(prev[0], prev[1], prev[2])
            fm_stats(dst, dst_b, tb, B["tb"], EPS_LN)
            prev = (s, dst, dst_b)
        bc1_P(prev[0], prev[1], prev[2])
    else:
        for s in range(NS):
            dst, dst_b = (a2b, B["a2b"]) if s % 2 == 0 else (kb, B["kb"])
            bc1_A(s, dst, dst_b)
            fm_stats(dst, dst_b, tb, B["tb"], EPS_LN)
            bc1_P(s, dst, dst_b)

    if stop_after == "bc1":
        return _finish_queues(cb, nc)

    cb.dml(lambda e: e.dma_start(out=tb[0:1, 0:D], in_=g2_row.ap()[:, :]),
           wr=[B["tb"]])
    cb.dml(lambda e: e.dma_start(out=tb[0:1, D:2 * D],
                                 in_=be2_row.ap()[:, :]), wr=[B["tb"]])
    for j in range(2):
        cb.t(lambda e, j=j: e.matmul(
            pb[4][:, 0:SLAB], onesr[:, :],
            tb[0:1, D + j * SLAB:D + (j + 1) * SLAB],
            start=True, stop=True), rd=[B["tb"], B["onesr"]], wr=[PB[4]])
        cb.v(lambda e, j=j: e.tensor_mul(
            kvexp[:, j * SLAB:(j + 1) * SLAB], pb[4][:, 0:SLAB],
            rep[:, R_GA + j * SLAB:R_GA + (j + 1) * SLAB]),
            rd=[PB[4], B["rep"]], wr=[B["kvexp"]])
    for j in range(2):
        cb.t(lambda e, j=j: e.matmul(
            pb[5][:, 0:SLAB], onesr[:, :], tb[0:1, j * SLAB:(j + 1) * SLAB],
            start=True, stop=True), rd=[B["tb"], B["onesr"]], wr=[PB[5]])
        cb.v(lambda e, j=j: e.tensor_mul(
            rep[:, R_GA + j * SLAB:R_GA + (j + 1) * SLAB],
            rep[:, R_GA + j * SLAB:R_GA + (j + 1) * SLAB],
            pb[5][:, 0:SLAB]), rd=[PB[5], B["rep"]], wr=[B["rep"]])

    def w12_off(st, part, col):
        return (st * 2 + part) * 128 * W_COLS + col

    def emit_wload(s, st):
        if s == 0:
            first = st == 0
            load_split(w1_in, FF, 0, st * 512, D, 512, wa_hi, wa_lo,
                       [WAH[0]] + ([B["wa_hi"]] if first else []),
                       [WAL[0]] + ([B["wa_lo"]] if first else []), dcol0=0)
            load_split(w2_in, D, st * 512, 0, 512, 1024, wa_hi, wa_lo,
                       [WAH[1]] + ([B["wa_hi"]] if first else []),
                       [WAL[1]] + ([B["wa_lo"]] if first else []),
                       dcol0=4096)
            for part, src, srcb in ((0, wa_hi, [WAH[0]]),
                                    (1, wa_lo, [WAL[0]])):
                def sp1(e, st=st, part=part, src=src):
                    o = ap3(w12_d, w12_off(st, part, 0),
                            [[W_COLS, 128], [1, 4096]])
                    return e.dma_start(out=o, in_=src[:, 0:4096])
                cb.dms(sp1, rd=srcb, wr=[Bw12[st]])
            for part, src, srcb in ((0, wa_hi, [WAH[1]]),
                                    (1, wa_lo, [WAL[1]])):
                def sp2(e, st=st, part=part, src=src):
                    o = ap3(w12_d, w12_off(st, part, 4096),
                            [[W_COLS, 128], [1, 4096]])
                    return e.dma_start(out=o, in_=src[:, 4096:8192])
                cb.dms(sp2, rd=srcb, wr=[Bw12[st]])
        else:
            for part, dst, dstb in ((0, wa_hi, [WAH[0]]),
                                    (1, wa_lo, [WAL[0]])):
                def ld1(e, st=st, part=part, dst=dst):
                    i = ap3(w12_d, w12_off(st, part, 0),
                            [[W_COLS, 128], [1, 4096]])
                    return e.dma_start(out=dst[:, 0:4096], in_=i)
                cb.dml(ld1, rd=[Bw12[st]], wr=dstb)
            for part, dst, dstb in ((0, wa_hi, [WAH[1]]),
                                    (1, wa_lo, [WAL[1]])):
                def ld2(e, st=st, part=part, dst=dst):
                    i = ap3(w12_d, w12_off(st, part, 4096),
                            [[W_COLS, 128], [1, 4096]])
                    return e.dma_start(out=dst[:, 4096:8192], in_=i)
                cb.dml(ld2, rd=[Bw12[st]], wr=dstb)

    def slab_bufs(s):
        p = s % 2
        Y, BY = (qb, B["qb"]) if p == 0 else (a2b, B["a2b"])
        if p == 0:
            return Y, BY, hhi, hlo, [B["hhi"]], [B["hlo"]]
        return Y, BY, wb_hi, wb_lo, [WBH[0], WBH[1]], [WBL[0], WBL[1]]

    def emit_stripe(s, st):
        Y, BY, xh, xl, Bxh, Bxl = slab_bufs(s)
        for gf in range(4):
            bank = gf % 2
            mm3_fm(bank, wa_hi, wa_lo, ([WAH[0]], [WAL[0]]), xh, xl,
                   (Bxh, Bxl), gf, 0, 512, nkc=8)
            cb.a(lambda e, gf=gf, bank=bank, st=st: e.activation(
                tb[:, gf * SLAB:(gf + 1) * SLAB], pb[bank][:, 0:SLAB],
                AF.Gelu, bias=vcol(C_B1 + st * 4 + gf)),
                rd=[PB[bank], B["vecs"]], wr=[B["tb"]])
            cb.a(lambda e, gf=gf: e.activation(
                wb_hi[:, 4096 + gf * SLAB:4096 + (gf + 1) * SLAB],
                tb[:, gf * SLAB:(gf + 1) * SLAB], AF.Identity),
                rd=[B["tb"]], wr=[WBH[2]])
            cb.v(lambda e, gf=gf: e.tensor_sub(
                wb_lo[:, 4096 + gf * SLAB:4096 + (gf + 1) * SLAB],
                tb[:, gf * SLAB:(gf + 1) * SLAB],
                wb_hi[:, 4096 + gf * SLAB:4096 + (gf + 1) * SLAB]),
                rd=[B["tb"], WBH[2]], wr=[WBL[2]])
        for yo in range(KC):
            bank = (2, 3, 6, 7)[yo % 4]
            mm3_fm(bank, wa_hi, wa_lo, ([WAH[1]], [WAL[1]]),
                   wb_hi, wb_lo, (WBH[2], WBL[2]), yo, 4096, 1024,
                   nkc=4, rcol0=4096, rstep=SLAB)
            if st == 0:
                cb.a(lambda e, yo=yo, bank=bank, Y=Y, BY=BY: e.activation(
                    Y[:, yo * SLAB:(yo + 1) * SLAB], pb[bank][:, 0:SLAB],
                    AF.Identity, bias=vcol(C_B2 + yo)),
                    rd=[PB[bank], B["vecs"]], wr=[BY])
            else:
                cb.v(lambda e, yo=yo, bank=bank, Y=Y, BY=BY: e.tensor_add(
                    Y[:, yo * SLAB:(yo + 1) * SLAB], pb[bank][:, 0:SLAB],
                    Y[:, yo * SLAB:(yo + 1) * SLAB]),
                    rd=[PB[bank], BY], wr=[BY])

    def bc3_piece(s, pid):
        Y, BY, xh, xl, Bxh, Bxl = slab_bufs(s)
        if pid == 0:
            cb.v(lambda e, Y=Y, xh=xh: e.tensor_add(x1b[:, :], Y[:, :],
                                                    xh[:, 0:SL]),
                 rd=[BY] + Bxh, wr=[B["x1b"]])
            cb.v(lambda e, xl=xl: e.tensor_add(x1b[:, :], x1b[:, :],
                                               xl[:, 0:SL]),
                 rd=[B["x1b"]] + Bxl, wr=[B["x1b"]])
        elif pid == 1:
            for f in range(KC):
                bk = 4 + f % 2
                pbk = pb[bk]

                def tr(e, f=f, pbk=pbk):
                    r = None
                    for c in range(NCT):
                        r = e.transpose(
                            pbk[:, c * 128:(c + 1) * 128],
                            x1b[:, f * SLAB + c * 128:
                                f * SLAB + (c + 1) * 128],
                            ident[:, :])
                    return r
                cb.t(tr, rd=[B["x1b"], B["ident_s"]], wr=[PB[bk]])

                def ev(e, f=f, pbk=pbk):
                    o = ap3(kb, f * 128, [[SL, 128], [D, NCT], [1, 128]])
                    i = ap3(pbk, 0, [[512, 128], [128, NCT], [1, 128]])
                    return e.activation(o, i, AF.Identity)
                cb.a(ev, rd=[PB[bk]], wr=[B["kb"]])
        elif pid == 2:
            ln_stats_tm(kb, B["kb"], x1b, B["x1b"], m4b, B["m4b"],
                        r4b, B["r4b"], 0, EPS_LN)
        elif pid == 3:
            ln_apply_tm(kb, B["kb"], kb, B["kb"], m4b, B["m4b"],
                        r4b, B["r4b"], 0, 4)

            def gm(e):
                k3 = ap3(kb, 0, [[SL, 128], [D, NCT], [1, D]])
                r3 = ap3(rep, R_GA, [[4 * D, 128], [0, NCT], [1, D]])
                return e.tensor_tensor(k3, k3, r3, OP.mult)
            cb.v(gm, rd=[B["kb"], B["rep"]], wr=[B["kb"]])
        elif pid == 4:
            def gab(e):
                k3 = ap3(kb, 0, [[SL, 128], [D, NCT], [1, D]])
                g3 = ap3(kvexp, 0, [[8 * 130, 128], [0, NCT], [1, D]])
                return e.tensor_tensor(k3, k3, g3, OP.add)
            cb.v(gab, rd=[B["kb"], B["kvexp"]], wr=[B["kb"]])
            cb.v(lambda e: e.tensor_add(kb[:, :], kb[:, :], xb[:, :]),
                 rd=[B["kb"], B["xb"]], wr=[B["kb"]])
            cb.dms(lambda e, s=s: e.dma_start(out=x2_d.ap()[s],
                                              in_=kb[:, :]),
                   rd=[B["kb"]], wr=[SP["x2"]])
        elif pid == 5:
            ln_stats_tm(kb, B["kb"], x1b, B["x1b"], m1b, B["m1b"],
                        r1b, B["r1b"], 0, EPS_LN)
        elif pid == 6:
            ln_apply_tm(kb, B["kb"], kb, B["kb"], m1b, B["m1b"],
                        r1b, B["r1b"], 0, 32)

            def cm(e):
                k3 = ap3(kb, 0, [[SL, 128], [D, NCT], [1, D]])
                r3 = ap3(rep, R_CB1, [[4 * D, 128], [0, NCT], [1, D]])
                return e.tensor_tensor(k3, k3, r3, OP.mult)
            cb.v(cm, rd=[B["kb"], B["rep"]], wr=[B["kb"]])

            def sm(e):
                k3 = ap3(kb, 0, [[SL, 128], [D, NCT], [1, D]])
                r3 = ap3(rep, R_SB, [[4 * D, 128], [0, NCT], [1, D]])
                return e.tensor_tensor(k3, k3, r3, OP.add)
            cb.v(sm, rd=[B["kb"], B["rep"]], wr=[B["kb"]])
        elif pid == 7:
            sA = ap3(kb, 0, [[SL, 128], [16, 256], [1, 16]])
            cb.v(lambda e, sA=sA: e.tensor_reduce(
                ub[:, 0:256], sA, AX.X, OP.max, apply_absolute_value=True),
                rd=[B["kb"]], wr=[B["ub"]])
            cb.v(lambda e: e.tensor_scalar(ub[:, 256:512], ub[:, 0:256],
                                           r6c[:, 0:1], 1e-12, OP.mult,
                                           OP.max),
                 rd=[B["ub"], B["r6c"]], wr=[B["ub"]])
            cb.v(lambda e: e.reciprocal(ub[:, 0:256], ub[:, 256:512]),
                 rd=[B["ub"]], wr=[B["ub"]])
            cb.a(lambda e, Y=Y: e.activation(Y[:, :], kb[:, :], AF.Abs),
                 rd=[B["kb"]], wr=[BY])
            v3 = ap3(Y, 0, [[SL, 128], [16, 256], [1, 16]])
            rsb = ap3(ub, 0, [[SLAB, 128], [1, 256], [0, 16]])
            cb.v(lambda e, v3=v3, rsb=rsb: e.tensor_tensor(v3, v3, rsb,
                                                           OP.mult),
                 rd=[BY, B["ub"]], wr=[BY])
        elif pid == 8:
            vf = ap3(Y, 0, [[SL, 128], [1, SL]])
            tf = ap3(x1b, 0, [[SL, 128], [1, SL]])
            cb.v(lambda e, vf=vf, tf=tf: e.tensor_single_scalar(
                tf, vf, 0.25, OP.is_gt), rd=[BY], wr=[B["x1b"]])
            for thr in (0.75, 1.25, 1.75):
                cb.v(lambda e, vf=vf, tf=tf, thr=thr: e.scalar_tensor_tensor(
                    tf, vf, thr, tf, OP.is_gt, OP.add),
                    rd=[BY, B["x1b"]], wr=[B["x1b"]])
        elif pid == 9:
            vf = ap3(Y, 0, [[SL, 128], [1, SL]])
            kf = ap3(xb, 0, [[SL, 128], [1, SL]])
            cb.v(lambda e, vf=vf, kf=kf: e.tensor_single_scalar(
                kf, vf, 2.5, OP.is_gt), rd=[BY], wr=[B["xb"]])
            for thr in (3.5, 5.0, 5.0):
                cb.v(lambda e, vf=vf, kf=kf, thr=thr: e.scalar_tensor_tensor(
                    kf, vf, thr, kf, OP.is_gt, OP.add),
                    rd=[BY, B["xb"]], wr=[B["xb"]])
        elif pid == 10:
            vf = ap3(Y, 0, [[SL, 128], [1, SL]])
            tf = ap3(x1b, 0, [[SL, 128], [1, SL]])
            kf = ap3(xb, 0, [[SL, 128], [1, SL]])
            cb.v(lambda e, tf=tf, kf=kf: e.scalar_tensor_tensor(
                tf, tf, 0.5, kf, OP.mult, OP.add),
                rd=[B["x1b"], B["xb"]], wr=[B["x1b"]])
            cb.a(lambda e, Y=Y: e.activation(Y[:, :], kb[:, :], AF.Sign),
                 rd=[B["kb"]], wr=[BY])
            cb.v(lambda e, tf=tf, vf=vf: e.tensor_tensor(tf, tf, vf,
                                                         OP.mult),
                 rd=[B["x1b"], BY], wr=[B["x1b"]])
            t3 = ap3(x1b, 0, [[SL, 128], [16, 256], [1, 16]])
            d3 = ap3(kb, 0, [[SL, 128], [16, 256], [1, 16]])
            ssb = ap3(ub, 256, [[SLAB, 128], [1, 256], [0, 16]])
            cb.v(lambda e, t3=t3, d3=d3, ssb=ssb: e.tensor_tensor(
                d3, t3, ssb, OP.mult), rd=[B["x1b"], B["ub"]], wr=[B["kb"]])
        elif pid == 11:
            tr_tm_to_fm(kb, B["kb"], wa_hi, WAH[1], wa_lo, WAL[1], (4, 5),
                        dcol0=4096)
            cb.dms(lambda e, s=s: e.dma_start(out=xq_hi_d.ap()[s],
                                              in_=wa_hi[:, 4096:4096 + SL]),
                   rd=[WAH[1]], wr=[SP["xq"]])
            cb.dms(lambda e, s=s: e.dma_start(out=xq_lo_d.ap()[s],
                                              in_=wa_lo[:, 4096:4096 + SL]),
                   rd=[WAL[1]], wr=[SP["xq"]])

    PIECE_MAP = {0: (0, 1, 2), 1: (3,), 2: (4,), 3: (5, 6), 4: (7,),
                 5: (8,), 6: (9, 10), 7: (11,)}

    cb.dml(lambda e: e.dma_start(out=hhi[:, :], in_=x1_hi_d.ap()[0]),
           rd=[SP["x1"]], wr=[B["hhi"]])
    cb.dml(lambda e: e.dma_start(out=hlo[:, :], in_=x1_lo_d.ap()[0]),
           rd=[SP["x1"]], wr=[B["hlo"]])
    emit_wload(0, 0)
    for s in range(NS):
        for st in range(8):
            emit_stripe(s, st)
            if s >= 1:
                for pid in PIECE_MAP[st]:
                    bc3_piece(s - 1, pid)
            if st < 7:
                emit_wload(s, st + 1)
            elif s + 1 < NS:
                emit_wload(s + 1, 0)
            if st == 1 and s + 1 < NS:
                _, _, nh, nl, nhb, nlb = slab_bufs(s + 1)
                cb.dml(lambda e, s=s, nh=nh: e.dma_start(
                    out=nh[:, 0:SL], in_=x1_hi_d.ap()[s + 1]),
                    rd=[SP["x1"]], wr=nhb)
                cb.dml(lambda e, s=s, nl=nl: e.dma_start(
                    out=nl[:, 0:SL], in_=x1_lo_d.ap()[s + 1]),
                    rd=[SP["x1"]], wr=nlb)
            if st == 6:
                def ldx2(e, s=s):
                    o = ap3(xb, 0, [[SL, 128], [D, NCT], [1, D]])
                    i = ap3(xin, s * SLAB * D,
                            [[D, 128], [128 * D, NCT], [1, D]])
                    return e.dma_start(out=o, in_=i)
                cb.dml(ldx2, wr=[B["xb"]])
    for pid in range(12):
        bc3_piece(NS - 1, pid)

    if stop_after in ("bc2", "bc3"):
        return _finish_queues(cb, nc)

    STRIPES = STRIPES4
    NST4 = len(STRIPES)

    def w4_bufs(mat):
        if mat == 0:
            return wa_hi, wa_lo, 0, [WAH[0]], [WAL[0]]
        if mat == 1:
            return wa_hi, wa_lo, 4096, [WAH[1]], [WAL[1]]
        return wb_hi, wb_lo, 0, [WBH[0], WBH[1]], [WBL[0], WBL[1]]

    def emit_w4load_one(s, sti, mat):
        i0, isz = STRIPES[sti]
        nic = isz // 128
        ncols = nic * 1024 if mat == 2 else KC * isz
        hi_t, lo_t, dc0, hib, lob = w4_bufs(mat)
        first = s == 0 and sti == 0 and mat == 0
        wh = hib + ([B["wa_hi"]] if first else [])
        wl = lob + ([B["wa_lo"]] if first else [])
        if s == 0:
            def ldc(e, mat=mat, dc0=dc0, ncols=ncols, sti=sti, hi_t=hi_t):
                o = ap3(hi_t, dc0, [[W_COLS, 128], [1, ncols]])
                i = ap3(w4_d, (mat * 6 + sti) * 128 * 4096,
                        [[4096, 128], [1, ncols]])
                return e.dma_start(out=o, in_=i)
            cb.dms(ldc, rd=[Bw4], wr=wh)

            def ldf(e, mat=mat, ncols=ncols, sti=sti):
                o = ap3(xb, 0, [[SL, 128], [1, ncols]])
                i = ap3(w4_d, (mat * 6 + sti) * 128 * 4096,
                        [[4096, 128], [1, ncols]])
                return e.dma_start(out=o, in_=i)
            cb.dml(ldf, rd=[Bw4], wr=[B["xb"]])
            cb.v(lambda e, dc0=dc0, ncols=ncols, hi_t=hi_t, lo_t=lo_t:
                 e.tensor_sub(
                     ap3(lo_t, dc0, [[W_COLS, 128], [1, ncols]]),
                     ap3(xb, 0, [[SL, 128], [1, ncols]]),
                     ap3(hi_t, dc0, [[W_COLS, 128], [1, ncols]])),
                 rd=[B["xb"]] + hib, wr=wl)
            for part, src, srcb in ((0, hi_t, hib), (1, lo_t, lob)):
                def sp(e, mat=mat, sti=sti, part=part, src=src,
                       dc0=dc0, ncols=ncols):
                    o = ap3(w4b_d,
                            ((mat * 6 + sti) * 2 + part) * 128 * 4096,
                            [[4096, 128], [1, ncols]])
                    return e.dma_start(
                        out=o, in_=ap3(src, dc0,
                                       [[W_COLS, 128], [1, ncols]]))
                cb.dms(sp, rd=srcb, wr=[Bw4b[mat * 6 + sti]])
        else:
            for part, dst, dstb in ((0, hi_t, wh), (1, lo_t, wl)):
                def ldb(e, mat=mat, sti=sti, part=part, dst=dst,
                        dc0=dc0, ncols=ncols):
                    o = ap3(dst, dc0, [[W_COLS, 128], [1, ncols]])
                    i = ap3(w4b_d,
                            ((mat * 6 + sti) * 2 + part) * 128 * 4096,
                            [[4096, 128], [1, ncols]])
                    return e.dma_start(out=o, in_=i)
                cb.dml(ldb, rd=[Bw4b[mat * 6 + sti]], wr=dstb)

    def h_slot(sti):
        return ((a2b, 0, B["a2b"]), (a2b, 2048, B["a2b"]),
                (qb, 0, B["qb"]))[sti % 3]

    def bc4_p1(s, sti):
        i0, isz = STRIPES[sti]
        hb, ho, hbb = h_slot(sti)
        for c in range(NCT):
            def mmg(e, c=c, isz=isz):
                r = None
                n = 0
                for (wsrc, xsrc) in ((wa_hi, hhi), (wa_lo, hhi),
                                     (wa_hi, hlo)):
                    for kc in range(KC):
                        r = e.matmul(
                            pb[c][:, 0:isz],
                            xsrc[:, kc * SLAB + c * 128:
                                 kc * SLAB + (c + 1) * 128],
                            wsrc[:, kc * isz:(kc + 1) * isz],
                            start=(n == 0), stop=(n == 3 * KC - 1))
                        n += 1
                return r
            cb.t(mmg, rd=[WAH[0], WAL[0], B["hhi"], B["hlo"]], wr=[PB[c]])
        for c in range(NCT):
            def mmu(e, c=c, isz=isz):
                r = None
                n = 0
                for (wsrc, xsrc) in ((wa_hi, hhi), (wa_lo, hhi),
                                     (wa_hi, hlo)):
                    for kc in range(KC):
                        r = e.matmul(
                            pb[4 + c][:, 0:isz],
                            xsrc[:, kc * SLAB + c * 128:
                                 kc * SLAB + (c + 1) * 128],
                            wsrc[:, 4096 + kc * isz:4096 + (kc + 1) * isz],
                            start=(n == 0), stop=(n == 3 * KC - 1))
                        n += 1
                return r
            cb.t(mmu, rd=[WAH[1], WAL[1], B["hhi"], B["hlo"]],
                 wr=[PB[4 + c]])
        for c in range(NCT):
            p = c % 2
            cb.a(lambda e, p=p, c=c, isz=isz: e.activation(
                t2s[:, 0:isz] if p else t1s[:, 0:isz],
                pb[c][:, 0:isz], AF.Silu),
                rd=[PB[c]], wr=[B["t2s"] if p else B["t1s"]])
            cb.v(lambda e, p=p, c=c, isz=isz, hb=hb, ho=ho: e.tensor_mul(
                hb[:, ho + c * isz:ho + (c + 1) * isz],
                (t2s if p else t1s)[:, 0:isz], pb[4 + c][:, 0:isz]),
                rd=[B["t2s"] if p else B["t1s"], PB[4 + c]],
                wr=[hbb])
        quant_tm(hb, hbb, SL, hb, hbb, SL, ho, NCT * isz,
                 tb, B["tb"], SL, 0, tb, B["tb"], SL, 2048,
                 kb, B["kb"], SL, 0, sc_off=0)

    def bc4_p2(s, sti):
        i0, isz = STRIPES[sti]
        nic = isz // 128
        last = sti == NST4 - 1
        hb, ho, hbb = h_slot(sti)
        for c in range(NCT):
            pbk = pb[c % 2]

            def trh(e, c=c, nic=nic, pbk=pbk, hb=hb, ho=ho, isz=isz):
                r = None
                for ic in range(nic):
                    r = e.transpose(
                        pbk[:, ic * 128:(ic + 1) * 128],
                        hb[:, ho + c * isz + ic * 128:
                           ho + c * isz + (ic + 1) * 128],
                        ident[:, :])
                return r
            cb.t(trh, rd=[hbb, B["ident_s"]], wr=[PB[c % 2]])
            hqc = 4096 + c * 512
            cb.a(lambda e, hqc=hqc, nic=nic, pbk=pbk: e.activation(
                wb_hi[:, hqc:hqc + nic * 128], pbk[:, 0:nic * 128],
                AF.Identity), rd=[PB[c % 2]], wr=[WBH[2]])
            cb.v(lambda e, hqc=hqc, nic=nic, pbk=pbk: e.tensor_sub(
                wb_lo[:, hqc:hqc + nic * 128], pbk[:, 0:nic * 128],
                wb_hi[:, hqc:hqc + nic * 128]),
                rd=[PB[c % 2], WBH[2]], wr=[WBL[2]])
            for no in range(2):
                bank = 2 + no

                def mmd(e, c=c, no=no, bank=bank, nic=nic, hqc=hqc):
                    r = None
                    n = 0
                    for (wsrc, xsrc) in ((wb_hi, wb_hi), (wb_lo, wb_hi),
                                         (wb_hi, wb_lo)):
                        for ic in range(nic):
                            r = e.matmul(
                                pb[bank][:, 0:512],
                                xsrc[:, hqc + ic * 128:
                                     hqc + (ic + 1) * 128],
                                wsrc[:, ic * 1024 + no * 512:
                                     ic * 1024 + (no + 1) * 512],
                                start=(n == 0), stop=(n == 3 * nic - 1))
                            n += 1
                    return r
                cb.t(mmd, rd=[WBH[0], WBH[1], WBL[0], WBL[1],
                              WBH[2], WBL[2]], wr=[PB[bank]])
                if sti == 0:
                    cb.v(lambda e, c=c, no=no, bank=bank: e.tensor_copy(
                        x1b[:, c * D + no * 512:c * D + (no + 1) * 512],
                        pb[bank][:, 0:512]),
                        rd=[PB[bank]], wr=[B["x1b"]])
                elif not last:
                    cb.v(lambda e, c=c, no=no, bank=bank: e.tensor_add(
                        x1b[:, c * D + no * 512:c * D + (no + 1) * 512],
                        pb[bank][:, 0:512],
                        x1b[:, c * D + no * 512:c * D + (no + 1) * 512]),
                        rd=[PB[bank], B["x1b"]], wr=[B["x1b"]])
                else:
                    cb.v(lambda e, c=c, no=no, bank=bank: e.tensor_add(
                        t1s[:, :], pb[bank][:, 0:512],
                        x1b[:, c * D + no * 512:c * D + (no + 1) * 512]),
                        rd=[PB[bank], B["x1b"]], wr=[B["t1s"]])
                    cb.v(lambda e, no=no: e.tensor_mul(
                        t2s[:, :], t1s[:, :],
                        rep[:, R_GB + no * 512:R_GB + (no + 1) * 512]),
                        rd=[B["t1s"], B["rep"]], wr=[B["t2s"]])
                    cb.v(lambda e, c=c, no=no: e.tensor_add(
                        x1b[:, c * D + no * 512:c * D + (no + 1) * 512],
                        t2s[:, :],
                        xb[:, c * D + no * 512:c * D + (no + 1) * 512]),
                        rd=[B["t2s"], B["xb"]], wr=[B["x1b"]])
        if last:
            def stout(e, s=s):
                oo = ap3(out, s * SLAB * D,
                         [[D, 128], [128 * D, NCT], [1, D]])
                i = ap3(x1b, 0, [[SL, 128], [D, NCT], [1, D]])
                return e.dma_start(out=oo, in_=i)
            cb.dms(stout, rd=[B["x1b"]], wr=[SP["out"]])

    def xq_load(s):
        cb.dml(lambda e, s=s: e.dma_start(out=hhi[:, :],
                                          in_=xq_hi_d.ap()[s]),
               rd=[SP["xq"]], wr=[B["hhi"]])
        cb.dml(lambda e, s=s: e.dma_start(out=hlo[:, :],
                                          in_=xq_lo_d.ap()[s]),
               rd=[SP["xq"]], wr=[B["hlo"]])

    jobs = [(s, sti) for s in range(NS) for sti in range(NST4)]
    for mat in range(3):
        emit_w4load_one(0, 0, mat)
    xq_load(0)
    bc4_p1(*jobs[0])
    for mat in range(2):
        emit_w4load_one(*jobs[1], mat)
    bc4_p1(*jobs[1])
    for idx, cur in enumerate(jobs):
        n1 = jobs[idx + 1] if idx + 1 < len(jobs) else None
        n2 = jobs[idx + 2] if idx + 2 < len(jobs) else None
        if n2 is not None:
            if n2[1] == 0:
                xq_load(n2[0])
            for mat in range(2):
                emit_w4load_one(*n2, mat)
            bc4_p1(*n2)
        if cur[1] == NST4 - 1:
            cb.dml(lambda e, s=cur[0]: e.dma_start(out=xb[:, :],
                                                   in_=x2_d.ap()[s]),
                   rd=[SP["x2"]], wr=[B["xb"]])
        bc4_p2(*cur)
        if n1 is not None:
            emit_w4load_one(*n1, 2)

    return _finish_queues(cb, nc)

def _finish_queues(cb, nc):
    fin = [(cb.sem[f"gpsimd_d{j}"], cb.cnt[f"gpsimd_d{j}"])
           for j in range(NPOOL) if cb.cnt[f"gpsimd_d{j}"]]

    def _final(e, fin=fin):
        for s, v in fin:
            e.wait_ge(s, v)
    cb.q["gpsimd"].append(_final)

    from contextlib import ExitStack
    with ExitStack() as ctx:
        block = ctx.enter_context(nc.Block())

        @block.sync
        def _(sync):
            for g in cb.q["sync"]:
                g(sync)

        @block.scalar
        def _(scalar):
            for g in cb.q["scalar"]:
                g(scalar)

        @block.vector
        def _(vector):
            for g in cb.q["vector"]:
                g(vector)

        @block.tensor
        def _(tensor):
            for g in cb.q["tensor"]:
                g(tensor)

        @block.gpsimd
        def _(gpsimd):
            for g in cb.q["gpsimd"]:
                g(gpsimd)

    return nc


def make_in_maps(inputs):
    x = np.ascontiguousarray(np.asarray(inputs["L0_attn_src"],
                                        dtype=np.float32))
    c = np.asarray(inputs["L0_c"], dtype=np.float32)
    ident = np.eye(128, dtype=np.float32)

    def fm(v, k):
        return np.ascontiguousarray(
            np.asarray(v, dtype=np.float32).reshape(k, 128).T)

    def cc(v):
        return np.ascontiguousarray(np.asarray(v, dtype=np.float32))

    common = {
        "wmod": cc(inputs["L0_wmod"]),
        "wq": cc(inputs["L0_attn_Wq"]), "wk": cc(inputs["L0_attn_Wk"]),
        "wv": cc(inputs["L0_attn_Wv"]), "wo": cc(inputs["L0_attn_Wo"]),
        "w1": cc(inputs["L0_attn_W1"]), "w2": cc(inputs["L0_attn_W2"]),
        "wg": cc(inputs["L0_ffn_w_gate"]), "wu": cc(inputs["L0_ffn_w_up"]),
        "wd": cc(inputs["L0_ffn_w_down"]),
        "bq_fm": fm(inputs["L0_attn_bq"], 8),
        "bk_row": cc(np.asarray(inputs["L0_attn_bk"]).reshape(1, D)),
        "bv_row": cc(np.asarray(inputs["L0_attn_bv"]).reshape(1, D)),
        "bo_fm": fm(inputs["L0_attn_bo"], 8),
        "b1_fm": fm(inputs["L0_attn_b1"], 32),
        "b2_fm": fm(inputs["L0_attn_b2"], 8),
        "g1_fm": fm(inputs["L0_attn_g1"], 8),
        "be1_fm": fm(inputs["L0_attn_be1"], 8),
        "g2_fm": fm(inputs["L0_attn_g2"], 8),
        "be2_fm": fm(inputs["L0_attn_be2"], 8),
        "g2_row": cc(np.asarray(inputs["L0_attn_g2"]).reshape(1, D)),
        "be2_row": cc(np.asarray(inputs["L0_attn_be2"]).reshape(1, D)),
        "ident": ident,
    }
    in_maps = []
    for core in range(NCORES):
        b = core // 4
        t0 = (core % 4) * T
        m = dict(common)
        m["xin"] = np.ascontiguousarray(x[b, t0:t0 + T])
        m["cfm"] = fm(c[b], 8)
        in_maps.append(m)
    return in_maps


def kernel(**inputs):
    nc = build_program()
    in_maps = make_in_maps(inputs)
    res = run_bass_kernel_spmd(nc, in_maps, core_ids=list(range(NCORES)))
    full = np.empty((2, 16384, D), dtype=np.float32)
    for core in range(NCORES):
        b = core // 4
        t0 = (core % 4) * T
        full[b, t0:t0 + T] = res.results[core]["out"]
    return full


# version 13
# speedup vs baseline: 1.0747x; 1.0661x over previous
import numpy as np
import concourse.bass as bass
import concourse.mybir as mybir
from concourse.bass_utils import run_bass_kernel_spmd

F32 = mybir.dt.float32
BF16 = mybir.dt.bfloat16
OP = mybir.AluOpType
AF = mybir.ActivationFunctionType
AX = mybir.AxisListType
AP = bass.AP

D = 1024
H = 16
FF = 4096
FI = 2816
NCORES = 8
T = 4096
SLAB = 512
NS = T // SLAB
NCT = SLAB // 128
KC = D // 128
EPS_LN = 1e-5
EPS_Z = 1e-6

ENGS = ("sync", "scalar", "vector", "tensor", "gpsimd")

C_BQ, C_BO, C_B2, C_G1, C_BE1, C_G2, C_B1 = 0, 8, 16, 24, 32, 40, 48
M_SA, M_CA, M_GA, M_SB, M_CB, M_GB, M_CA1, M_CB1 = 0, 8, 16, 24, 32, 40, 48, 56
R_BK, R_BV = 0, D
R_GA, R_GB, R_CB1, R_SB = 0, D, 2 * D, 3 * D

W_COLS = 8192
SL = NCT * D


class Buf:
    __slots__ = ("name", "wr", "rds")

    def __init__(self, name):
        self.name = name
        self.wr = None
        self.rds = []


NPOOL = 12


class CB:

    def __init__(self):
        self.nc = bass.Bass("TRN2", target_bir_lowering=False)
        self.q = {e: [] for e in ENGS}
        self.sem = {e: self.nc.alloc_semaphore(f"S{e}") for e in ENGS}
        for e in ("sync", "gpsimd"):
            for j in range(NPOOL):
                self.sem[f"{e}_d{j}"] = self.nc.alloc_semaphore(f"S{e}d{j}")
        self.sem["cc"] = self.nc.alloc_semaphore("Scc")
        self.cnt = {k: 0 for k in self.sem}
        self.waited = {e: {} for e in ENGS}
        self.pool_i = {"sync": 0, "gpsimd": 0}
        self.last_drain = {}

    def _hazard_waits(self, eng, reads, writes, extra=()):
        waits = list(extra)
        for b in reads:
            if b.wr is not None:
                waits.append(b.wr)
        for b in writes:
            if b.wr is not None:
                waits.append(b.wr)
            waits.extend(b.rds)
        wmap = {}
        for (key, val) in waits:
            if key == eng:
                continue
            if wmap.get(key, -1) < val:
                wmap[key] = val
        final = []
        for key, val in wmap.items():
            if self.waited[eng].get(key, -1) < val:
                self.waited[eng][key] = val
                final.append((self.sem[key], val))
        return final

    def _finish(self, eng, tok, reads, writes):
        for b in writes:
            b.wr = tok
            b.rds = []
        for b in reads:
            b.rds.append(tok)
        return tok

    def _emit(self, eng, f, reads, writes, inc, drain, semkey=None):
        semkey = semkey or eng
        need_drain = False
        if drain:
            last = self.last_drain.get(eng, 0)
            dep = 0
            for b in reads:
                if b.wr is not None and b.wr[0] == semkey:
                    dep = max(dep, b.wr[1])
            for b in writes:
                if b.wr is not None and b.wr[0] == semkey:
                    dep = max(dep, b.wr[1])
                for t in b.rds:
                    if t[0] == semkey:
                        dep = max(dep, t[1])
            if dep > last:
                need_drain = True
        final = self._hazard_waits(eng, reads, writes)
        sem = self.sem[semkey]

        def g(e, final=final, f=f, inc=inc, nd=need_drain, sem=sem):
            if nd:
                e.drain()
            for s, v in final:
                e.wait_ge(s, v)
            r = f(e)
            if inc:
                r.then_inc(sem, inc)

        self.q[eng].append(g)
        self.cnt[semkey] += inc
        if need_drain:
            self.last_drain[eng] = self.cnt[semkey] - inc
        return self._finish(eng, (semkey, self.cnt[semkey]), reads, writes)

    def _dma(self, eng, f, reads, writes):
        j = self.pool_i[eng]
        self.pool_i[eng] = (j + 1) % NPOOL
        key = f"{eng}_d{j}"
        guard = [(key, self.cnt[key])] if self.cnt[key] else []
        final = self._hazard_waits(eng, reads, writes, extra=guard)
        sem = self.sem[key]

        def g(e, final=final, f=f, sem=sem):
            for s, v in final:
                e.wait_ge(s, v)
            f(e).then_inc(sem, 16)

        self.q[eng].append(g)
        self.cnt[key] += 16
        return self._finish(eng, (key, self.cnt[key]), reads, writes)

    def v(self, f, rd=(), wr=()):
        return self._emit("vector", f, rd, wr, 1, True)

    def p(self, f, rd=(), wr=()):
        return self._emit("gpsimd", f, rd, wr, 1, True)

    def a(self, f, rd=(), wr=()):
        return self._emit("scalar", f, rd, wr, 1, True)

    def t(self, f, rd=(), wr=()):
        return self._emit("tensor", f, rd, wr, 1, False)

    def dml(self, f, rd=(), wr=()):
        return self._dma("sync", f, rd, wr)

    def dms(self, f, rd=(), wr=()):
        return self._dma("gpsimd", f, rd, wr)


def ap3(handle, offset, dims):
    return AP(handle, offset, [list(d) for d in dims])


def build_program(debug_names=(), skip_collective=False, bc2_stripes=8, stop_after=None):
    cb = CB()
    nc = cb.nc
    dbg = set(debug_names)

    def dram(name, shape, dt=F32):
        kind = "ExternalOutput" if name in dbg else "Internal"
        return nc.dram_tensor(name, list(shape), dt, kind=kind)

    xin = nc.dram_tensor("xin", [T, D], F32, kind="ExternalInput")
    cfm_in = nc.dram_tensor("cfm", [128, 8], F32, kind="ExternalInput")
    wmod_in = nc.dram_tensor("wmod", [D, 6 * D], F32, kind="ExternalInput")
    wq_in = nc.dram_tensor("wq", [D, D], F32, kind="ExternalInput")
    wk_in = nc.dram_tensor("wk", [D, D], F32, kind="ExternalInput")
    wv_in = nc.dram_tensor("wv", [D, D], F32, kind="ExternalInput")
    wo_in = nc.dram_tensor("wo", [D, D], F32, kind="ExternalInput")
    w1_in = nc.dram_tensor("w1", [D, FF], F32, kind="ExternalInput")
    w2_in = nc.dram_tensor("w2", [FF, D], F32, kind="ExternalInput")
    wg_in = nc.dram_tensor("wg", [FI, D], F32, kind="ExternalInput")
    wu_in = nc.dram_tensor("wu", [FI, D], F32, kind="ExternalInput")
    wd_in = nc.dram_tensor("wd", [D, FI], F32, kind="ExternalInput")
    bq_fm = nc.dram_tensor("bq_fm", [128, 8], F32, kind="ExternalInput")
    bk_row = nc.dram_tensor("bk_row", [1, D], F32, kind="ExternalInput")
    bv_row = nc.dram_tensor("bv_row", [1, D], F32, kind="ExternalInput")
    bo_fm = nc.dram_tensor("bo_fm", [128, 8], F32, kind="ExternalInput")
    b1_fm = nc.dram_tensor("b1_fm", [128, 32], F32, kind="ExternalInput")
    b2_fm = nc.dram_tensor("b2_fm", [128, 8], F32, kind="ExternalInput")
    g1_fm = nc.dram_tensor("g1_fm", [128, 8], F32, kind="ExternalInput")
    be1_fm = nc.dram_tensor("be1_fm", [128, 8], F32, kind="ExternalInput")
    g2_fm = nc.dram_tensor("g2_fm", [128, 8], F32, kind="ExternalInput")
    be2_fm = nc.dram_tensor("be2_fm", [128, 8], F32, kind="ExternalInput")
    g2_row = nc.dram_tensor("g2_row", [1, D], F32, kind="ExternalInput")
    be2_row = nc.dram_tensor("be2_row", [1, D], F32, kind="ExternalInput")
    ident_in = nc.dram_tensor("ident", [128, 128], F32, kind="ExternalInput")
    out = nc.dram_tensor("out", [T, D], F32, kind="ExternalOutput")

    h_hi_d = dram("h_hi_d", [NS, 128, SL], BF16)
    h_lo_d = dram("h_lo_d", [NS, 128, SL], BF16)
    q_d = dram("q_d", [NS, 128, SL], F32)
    x1_hi_d = dram("x1_hi_d", [NS, 128, SL], BF16)
    x1_lo_d = dram("x1_lo_d", [NS, 128, SL], BF16)
    xq_hi_d = dram("xq_hi_d", [NS, 128, SL], BF16)
    xq_lo_d = dram("xq_lo_d", [NS, 128, SL], BF16)
    ya_d = dram("ya_d", [NS, 128, SL], F32)
    yb_d = dram("yb_d", [NS, 128, SL], F32)
    x2_d = dram("x2_d", [NS, 128, SL], F32)
    y2a_d = dram("y2a_d", [NS, 128, SL], F32)
    y2b_d = dram("y2b_d", [NS, 128, SL], F32)
    garow_d = dram("garow_d", [4, D], F32)
    w4_d = dram("w4_d", [3, 6, 128, 4096], F32)
    w4b_d = dram("w4b_d", [36, 128, 4096], BF16)
    w12_d = dram("w12_d", [8, 2, 128, W_COLS], BF16)
    cc_in = dram("cc_in", [128, 8 * 65], F32)
    cc_out = dram("cc_out", [128, 8 * 65], F32)

    B = {}

    def sb(name, cols, dt=F32, parts=128):
        h = nc.alloc_sbuf_tensor(name, [parts, cols], dt)
        B[name] = Buf(name)
        return h

    ident = sb("ident_s", 128)
    onesr = sb("onesr", 128, parts=1)
    onescol = sb("onescol", 1)
    cfm = sb("cfm_s", 8)
    silc = sb("silc", 8)
    modv = sb("modv", 64)
    sgn8 = sb("sgn8", 16)
    vecs = sb("vecs", 80)
    be2s = sb("be2s", 8)
    r6c = sb("r6c", 2)
    m1b = sb("m1b", 32)
    r1b = sb("r1b", 32)
    m4b = sb("m4b", 4)
    r4b = sb("r4b", 4)
    minis = sb("minis", 32)
    zb = sb("zb", 64)
    kvexp = sb("kvexp", 8 * 130)
    rep = sb("rep", 4 * D)
    xb = sb("xb", SL)
    tb = sb("tb", SL)
    kb = sb("kb", SL)
    qb = sb("qb", SL)
    a2b = sb("a2b", SL)
    x1b = sb("x1b", SL)
    hhi = sb("hhi", SL, BF16)
    hlo = sb("hlo", SL, BF16)
    wa_hi = sb("wa_hi", W_COLS, BF16)
    wa_lo = sb("wa_lo", W_COLS, BF16)
    wb_hi = sb("wb_hi", W_COLS, BF16)
    wb_lo = sb("wb_lo", W_COLS, BF16)
    ub = sb("ub", SLAB)
    t1s = sb("t1s", SLAB)
    t2s = sb("t2s", SLAB)

    pb = [nc.alloc_psum_tensor(f"pbk{i}", [128, 512], F32) for i in range(8)]
    PB = [Buf(f"pb{i}") for i in range(8)]
    Bcc = Buf("cc")
    Bw4 = Buf("w4")
    WBH = [Buf(f"wbh{i}") for i in range(3)]
    WBL = [Buf(f"wbl{i}") for i in range(3)]
    WAH = [Buf(f"wah{i}") for i in range(2)]
    WAL = [Buf(f"wal{i}") for i in range(2)]
    Bw12 = [Buf(f"w12_{i}") for i in range(8)]
    Bw4b = [Buf(f"w4b_{i}") for i in range(18)]
    KB0 = Buf("kb0")
    KB1 = Buf("kb1")
    Bgarow = Buf("garow")
    SP = {k: Buf("sp_" + k) for k in
          ("h", "q", "x1", "xq", "ya", "yb", "x2", "y2a", "y2b", "out")}

    def vcol(i):
        return vecs[:, i:i + 1]

    cb.dml(lambda e: e.dma_start(out=ident[:, :], in_=ident_in.ap()[:, :]),
           wr=[B["ident_s"]])
    cb.dml(lambda e: e.dma_start(out=cfm[:, :], in_=cfm_in.ap()[:, :]),
           wr=[B["cfm_s"]])
    for i, src in enumerate((bq_fm, bo_fm, b2_fm, g1_fm, be1_fm, g2_fm)):
        cb.dml(lambda e, i=i, src=src: e.dma_start(
            out=vecs[:, i * 8:(i + 1) * 8], in_=src.ap()[:, :]),
            wr=[B["vecs"]])
    cb.dml(lambda e: e.dma_start(out=vecs[:, 48:80], in_=b1_fm.ap()[:, :]),
           wr=[B["vecs"]])
    cb.dml(lambda e: e.dma_start(out=be2s[:, :], in_=be2_fm.ap()[:, :]),
           wr=[B["be2s"]])
    cb.v(lambda e: e.memset(r6c[:, 0:1], 6.0), wr=[B["r6c"]])
    cb.v(lambda e: e.reciprocal(r6c[:, 0:1], r6c[:, 0:1]), rd=[B["r6c"]],
         wr=[B["r6c"]])
    cb.v(lambda e: e.memset(r6c[:, 1:2], EPS_LN), wr=[B["r6c"]])
    cb.v(lambda e: e.memset(onesr[:, :], 1.0), wr=[B["onesr"]])
    cb.v(lambda e: e.memset(onescol[:, :], 1.0), wr=[B["onescol"]])

    cb.a(lambda e: e.activation(silc[:, :], cfm[:, :], AF.Silu),
         rd=[B["cfm_s"]], wr=[B["silc"]])
    for g in range(12):
        def ldw(e, g=g):
            o = ap3(xb, 0, [[SL, 128], [512, 8], [1, 512]])
            i = ap3(wmod_in, g * 512,
                    [[6 * D, 128], [128 * 6 * D, 8], [1, 512]])
            return e.dma_start(out=o, in_=i)
        cb.dml(ldw, wr=[B["xb"]])

        def mm(e, g=g):
            r = None
            for mc in range(4):
                for kc in range(KC):
                    r = e.matmul(
                        pb[0][:, g * 4 + mc:g * 4 + mc + 1],
                        xb[:, kc * 512 + mc * 128:kc * 512 + (mc + 1) * 128],
                        silc[:, kc:kc + 1],
                        start=(kc == 0), stop=(kc == KC - 1))
            return r
        cb.t(mm, rd=[B["xb"], B["silc"]], wr=[PB[0]])
    cb.v(lambda e: e.tensor_copy(modv[:, 0:48], pb[0][:, 0:48]),
         rd=[PB[0]], wr=[B["modv"]])
    cb.v(lambda e: e.tensor_scalar_add(modv[:, M_CA1:M_CA1 + 8],
                                       modv[:, M_CA:M_CA + 8], 1.0),
         rd=[B["modv"]], wr=[B["modv"]])
    cb.v(lambda e: e.tensor_scalar_add(modv[:, M_CB1:M_CB1 + 8],
                                       modv[:, M_CB:M_CB + 8], 1.0),
         rd=[B["modv"]], wr=[B["modv"]])
    for j, col in enumerate((M_GA, M_GB)):
        cb.v(lambda e, j=j, col=col: e.tensor_scalar(
            sgn8[:, j * 8:(j + 1) * 8], modv[:, col:col + 8], 0.0, 2.0,
            OP.is_ge, OP.mult), rd=[B["modv"]], wr=[B["sgn8"]])
        cb.v(lambda e, j=j: e.tensor_scalar_sub(
            sgn8[:, j * 8:(j + 1) * 8], sgn8[:, j * 8:(j + 1) * 8], 1.0),
            rd=[B["sgn8"]], wr=[B["sgn8"]])
        cb.a(lambda e, col=col: e.activation(
            modv[:, col:col + 8], modv[:, col:col + 8], AF.Abs),
            rd=[B["modv"]], wr=[B["modv"]])
        cb.v(lambda e, col=col: e.tensor_scalar_max(
            modv[:, col:col + 8], modv[:, col:col + 8], 0.25),
            rd=[B["modv"]], wr=[B["modv"]])
        cb.v(lambda e, j=j, col=col: e.tensor_mul(
            modv[:, col:col + 8], modv[:, col:col + 8],
            sgn8[:, j * 8:(j + 1) * 8]),
            rd=[B["modv"], B["sgn8"]], wr=[B["modv"]])

    for j, col in enumerate((M_GA, M_GB, M_CB1, M_SB)):
        def st(e, j=j, col=col):
            o = ap3(garow_d, j * D, [[1, 128], [128, 8]])
            return e.dma_start(out=o, in_=modv[:, col:col + 8],
                               allow_slow_non_contiguous=True)
        cb.dms(st, rd=[B["modv"]], wr=[Bgarow])

    def replicate(dst_col, src_dram, src_off):
        def f(e):
            o = ap3(rep, dst_col, [[4 * D, 128], [1, D]])
            i = ap3(src_dram, src_off, [[0, 128], [1, D]])
            return e.dma_start(out=o, in_=i)
        cb.dml(f, rd=[Bgarow], wr=[B["rep"]])

    replicate(R_BK, bk_row, 0)
    replicate(R_BV, bv_row, 0)

    def _bl(b):
        return list(b) if isinstance(b, (list, tuple)) else [b]

    def load_split(w_dram, wcols, row0, col0, nrows, ncols, hi, lo, hib, lob,
                   dcol0=0):
        hib, lob = _bl(hib), _bl(lob)
        nkc = nrows // 128

        def cast(e):
            o = ap3(hi, dcol0, [[W_COLS, 128], [ncols, nkc], [1, ncols]])
            i = ap3(w_dram, row0 * wcols + col0,
                    [[wcols, 128], [128 * wcols, nkc], [1, ncols]])
            return e.dma_start(out=o, in_=i)
        cb.dms(cast, wr=hib)
        hkc = max(1, nkc // 2)
        nhalf = (nkc + hkc - 1) // hkc
        for hh in range(nhalf):
            def full(e, hh=hh):
                o = ap3(xb, 0, [[SL, 128], [ncols, hkc], [1, ncols]])
                i = ap3(w_dram, (row0 + hh * hkc * 128) * wcols + col0,
                        [[wcols, 128], [128 * wcols, hkc], [1, ncols]])
                return e.dma_start(out=o, in_=i)
            cb.dml(full, wr=[B["xb"]])
            cb.v(lambda e, hh=hh: e.tensor_sub(
                ap3(lo, dcol0 + hh * hkc * ncols,
                    [[W_COLS, 128], [1, hkc * ncols]]),
                ap3(xb, 0, [[SL, 128], [1, hkc * ncols]]),
                ap3(hi, dcol0 + hh * hkc * ncols,
                    [[W_COLS, 128], [1, hkc * ncols]])),
                rd=[B["xb"]] + hib, wr=lob)

    load_split(wq_in, D, 0, 0, D, 1024, wa_hi, wa_lo, B["wa_hi"], B["wa_lo"])

    cb.v(lambda e: e.memset(qb[:, 0:520], 0.0), wr=[B["qb"]])
    cb.v(lambda e: e.memset(kvexp[:, :], 0.0), wr=[B["kvexp"]])

    def ln_stats_tm(src, src_b, sqout, sqout_b, mdst, mdst_b, rdst, rdst_b,
                    col0, eps):
        for c in range(NCT):
            cb.v(lambda e, c=c: e.tensor_reduce(
                minis[:, c:c + 1], src[:, c * D:(c + 1) * D], AX.X, OP.add),
                rd=[src_b], wr=[B["minis"]])
            cb.a(lambda e, c=c: e.activation(
                sqout[:, c * D:(c + 1) * D], src[:, c * D:(c + 1) * D],
                AF.Square, accum_out=minis[:, 4 + c:5 + c]),
                rd=[src_b], wr=[sqout_b, B["minis"]])
        cb.v(lambda e: e.tensor_scalar_mul(
            mdst[:, col0:col0 + NCT], minis[:, 0:NCT], 1.0 / D),
            rd=[B["minis"]], wr=[mdst_b])
        cb.v(lambda e: e.tensor_mul(
            minis[:, 8:8 + NCT], mdst[:, col0:col0 + NCT],
            mdst[:, col0:col0 + NCT]), rd=[mdst_b], wr=[B["minis"]])
        cb.v(lambda e: e.scalar_tensor_tensor(
            minis[:, 12:12 + NCT], minis[:, 4:4 + NCT], 1.0 / D,
            minis[:, 8:8 + NCT], OP.mult, OP.subtract),
            rd=[B["minis"]], wr=[B["minis"]])
        cb.a(lambda e: e.activation(
            minis[:, 16:16 + NCT], minis[:, 12:12 + NCT], AF.Sqrt,
            bias=r6c[:, 1:2]),
            rd=[B["minis"], B["r6c"]], wr=[B["minis"]])
        cb.v(lambda e: e.reciprocal(
            rdst[:, col0:col0 + NCT], minis[:, 16:16 + NCT]),
            rd=[B["minis"]], wr=[rdst_b])

    def ln_apply_tm(src, src_b, dst, dst_b, mdst, mdst_b, rdst, rdst_b, col0,
                    rw):
        for c in range(NCT):
            def f(e, c=c):
                r_b = ap3(rdst, col0 + c, [[rw, 128], [0, D]])
                return e.scalar_tensor_tensor(
                    dst[:, c * D:(c + 1) * D], src[:, c * D:(c + 1) * D],
                    mdst[:, col0 + c:col0 + c + 1], r_b, OP.subtract, OP.mult)
            cb.v(f, rd=[src_b, mdst_b, rdst_b], wr=[dst_b])

    def tr_tm_to_fm(src, src_b, dsthi, dsthi_b, dstlo, dstlo_b, banks,
                    affine=None, aff_tmp=None, aff_tmp_b=None, dcol0=0,
                    extra_wr_hi=(), extra_wr_lo=()):
        for f in range(KC):
            bk = banks[f % 2]
            pbk = pb[bk]

            def tr(e, f=f, pbk=pbk):
                r = None
                for c in range(NCT):
                    r = e.transpose(
                        pbk[:, c * 128:(c + 1) * 128],
                        src[:, c * D + f * 128:c * D + (f + 1) * 128],
                        ident[:, :])
                return r
            cb.t(tr, rd=[src_b, B["ident_s"]], wr=[PB[bk]])
            if affine is not None:
                acol, scol = affine

                def ev(e, f=f, pbk=pbk, acol=acol, scol=scol):
                    sa_b = ap3(modv, scol + f, [[64, 128], [0, SLAB]])
                    return e.scalar_tensor_tensor(
                        aff_tmp[:, f * SLAB:(f + 1) * SLAB], pbk[:, 0:SLAB],
                        modv[:, acol + f:acol + f + 1], sa_b, OP.mult, OP.add)
                cb.v(ev, rd=[PB[bk], B["modv"]], wr=[aff_tmp_b])
                cb.a(lambda e, f=f: e.activation(
                    dsthi[:, dcol0 + f * SLAB:dcol0 + (f + 1) * SLAB],
                    aff_tmp[:, f * SLAB:(f + 1) * SLAB], AF.Identity),
                    rd=[aff_tmp_b], wr=[dsthi_b])
                cb.v(lambda e, f=f: e.tensor_sub(
                    dstlo[:, dcol0 + f * SLAB:dcol0 + (f + 1) * SLAB],
                    aff_tmp[:, f * SLAB:(f + 1) * SLAB],
                    dsthi[:, dcol0 + f * SLAB:dcol0 + (f + 1) * SLAB]),
                    rd=[aff_tmp_b, dsthi_b], wr=[dstlo_b])
            else:
                cb.a(lambda e, f=f, pbk=pbk: e.activation(
                    dsthi[:, dcol0 + f * SLAB:dcol0 + (f + 1) * SLAB],
                    pbk[:, 0:SLAB],
                    AF.Identity), rd=[PB[bk]],
                    wr=[dsthi_b] + list(extra_wr_hi))
                cb.v(lambda e, f=f, pbk=pbk: e.tensor_sub(
                    dstlo[:, dcol0 + f * SLAB:dcol0 + (f + 1) * SLAB],
                    pbk[:, 0:SLAB],
                    dsthi[:, dcol0 + f * SLAB:dcol0 + (f + 1) * SLAB]),
                    rd=[PB[bk], dsthi_b], wr=[dstlo_b] + list(extra_wr_lo))

    def mm3_fm(bank, lhs_hi, lhs_lo, lhsb, rhs_hi, rhs_lo, rhsb, fcol, wcol0,
               wcols, nkc=KC, rcol0=0, rstep=SLAB):
        pbk = pb[bank]

        def mm(e):
            r = None
            n = 0
            for (wsrc, xsrc) in ((lhs_hi, rhs_hi), (lhs_hi, rhs_lo),
                                 (lhs_lo, rhs_hi)):
                for kc in range(nkc):
                    r = e.matmul(
                        pbk[:, 0:SLAB],
                        wsrc[:, wcol0 + kc * wcols + fcol * 128:
                             wcol0 + kc * wcols + (fcol + 1) * 128],
                        xsrc[:, rcol0 + kc * rstep:rcol0 + kc * rstep + SLAB],
                        start=(n == 0), stop=(n == 3 * nkc - 1))
                    n += 1
            return r
        rds = []
        for b in (*lhsb, *rhsb):
            rds.extend(_bl(b))
        return cb.t(mm, rd=rds, wr=[PB[bank]])

    def fm_stats(srcbuf, src_b, tmp, tmp_b, eps):
        def f1(e):
            r = None
            for kc in range(KC):
                r = e.matmul(pb[7][0:1, 0:SLAB], onescol[:, :],
                             srcbuf[:, kc * SLAB:(kc + 1) * SLAB],
                             start=(kc == 0), stop=(kc == KC - 1))
            return r
        cb.t(f1, rd=[src_b, B["onescol"]], wr=[PB[7]])
        cb.a(lambda e: e.activation(tmp[:, :], srcbuf[:, :], AF.Square),
             rd=[src_b], wr=[tmp_b])

        def f2(e):
            r = None
            for kc in range(KC):
                r = e.matmul(pb[6][0:1, 0:SLAB], onescol[:, :],
                             tmp[:, kc * SLAB:(kc + 1) * SLAB],
                             start=(kc == 0), stop=(kc == KC - 1))
            return r
        cb.t(f2, rd=[tmp_b, B["onescol"]], wr=[PB[6]])
        cb.v(lambda e: e.tensor_scalar_mul(ub[0:1, 0:SLAB],
                                           pb[7][0:1, 0:SLAB], 1.0 / D),
             rd=[PB[7]], wr=[B["ub"]])
        cb.v(lambda e: e.tensor_mul(t1s[0:1, 0:SLAB], ub[0:1, 0:SLAB],
                                    ub[0:1, 0:SLAB]),
             rd=[B["ub"]], wr=[B["t1s"]])
        cb.v(lambda e: e.scalar_tensor_tensor(
            t2s[0:1, 0:SLAB], pb[6][0:1, 0:SLAB], 1.0 / D, t1s[0:1, 0:SLAB],
            OP.mult, OP.subtract), rd=[PB[6], B["t1s"]], wr=[B["t2s"]])
        cb.a(lambda e: e.activation(t1s[0:1, 0:SLAB], t2s[0:1, 0:SLAB],
                                    AF.Sqrt, bias=r6c[0:1, 1:2]),
             rd=[B["t2s"], B["r6c"]], wr=[B["t1s"]])
        cb.v(lambda e: e.reciprocal(t2s[0:1, 0:SLAB], t1s[0:1, 0:SLAB]),
             rd=[B["t1s"]], wr=[B["t2s"]])
        cb.v(lambda e: e.tensor_mul(t1s[0:1, 0:SLAB], ub[0:1, 0:SLAB],
                                    t2s[0:1, 0:SLAB]),
             rd=[B["ub"], B["t2s"]], wr=[B["t1s"]])

    def fm_apply(srcbuf, src_b, gcol, bvec, bvec_b, bvw, bcol0, dst, dst_b,
                 tmp, tmp_b):
        cb.t(lambda e: e.matmul(pb[7][:, 0:SLAB], onesr[:, :],
                                t2s[0:1, 0:SLAB], start=True, stop=True),
             rd=[B["t2s"], B["onesr"]], wr=[PB[7]])
        cb.t(lambda e: e.matmul(pb[6][:, 0:SLAB], onesr[:, :],
                                t1s[0:1, 0:SLAB], start=True, stop=True),
             rd=[B["t1s"], B["onesr"]], wr=[PB[6]])
        for f in range(KC):
            cb.v(lambda e, f=f: e.tensor_mul(
                tmp[:, f * SLAB:(f + 1) * SLAB],
                srcbuf[:, f * SLAB:(f + 1) * SLAB], pb[7][:, 0:SLAB]),
                rd=[src_b, PB[7]], wr=[tmp_b])
            cb.v(lambda e, f=f: e.tensor_sub(
                tmp[:, f * SLAB:(f + 1) * SLAB],
                tmp[:, f * SLAB:(f + 1) * SLAB], pb[6][:, 0:SLAB]),
                rd=[tmp_b, PB[6]], wr=[tmp_b])

            def f3(e, f=f):
                b_b = ap3(bvec, bcol0 + f, [[bvw, 128], [0, SLAB]])
                return e.scalar_tensor_tensor(
                    dst[:, f * SLAB:(f + 1) * SLAB],
                    tmp[:, f * SLAB:(f + 1) * SLAB],
                    vcol(gcol + f), b_b, OP.mult, OP.add)
            cb.v(f3, rd=[tmp_b, B["vecs"], bvec_b], wr=[dst_b])

    def fm_ln(srcbuf, src_b, gcol, bvec, bvec_b, bvw, bcol0, dst, dst_b, eps,
              tmp, tmp_b):
        fm_stats(srcbuf, src_b, tmp, tmp_b, eps)
        fm_apply(srcbuf, src_b, gcol, bvec, bvec_b, bvw, bcol0, dst, dst_b,
                 tmp, tmp_b)

    def quant_tm(src, src_b, sw, dst, dst_b, dw, col0, F,
                 s1, s1b, s1w, c1, s2, s2b, s2w, c2, s3, s3b, s3w, c3,
                 sc_off=0, pool=False):
        V = cb.p if pool else cb.v
        NB = F // 16
        sA = ap3(src, col0, [[sw, 128], [16, NB], [1, 16]])
        cb.v(lambda e: e.tensor_reduce(ub[:, sc_off:sc_off + NB], sA, AX.X,
                                       OP.max, apply_absolute_value=True),
             rd=[src_b], wr=[B["ub"]])
        cb.v(lambda e: e.tensor_scalar(ub[:, 256 + sc_off:256 + sc_off + NB],
                                       ub[:, sc_off:sc_off + NB],
                                       r6c[:, 0:1], 1e-12, OP.mult, OP.max),
             rd=[B["ub"], B["r6c"]], wr=[B["ub"]])
        cb.v(lambda e: e.reciprocal(ub[:, sc_off:sc_off + NB],
                                    ub[:, 256 + sc_off:256 + sc_off + NB]),
             rd=[B["ub"]], wr=[B["ub"]])
        vf = ap3(s1, c1, [[s1w, 128], [1, F]])
        v3 = ap3(s1, c1, [[s1w, 128], [16, NB], [1, 16]])
        if pool:
            V(lambda e: e.tensor_single_scalar(
                vf, ap3(src, col0, [[sw, 128], [1, F]]), 0.0, OP.abs_max),
              rd=[src_b], wr=[s1b])
        else:
            cb.a(lambda e: e.activation(
                vf, ap3(src, col0, [[sw, 128], [1, F]]), AF.Abs),
                rd=[src_b], wr=[s1b])
        rsb = ap3(ub, sc_off, [[SLAB, 128], [1, NB], [0, 16]])
        V(lambda e: e.tensor_tensor(v3, v3, rsb, OP.mult),
          rd=[s1b, B["ub"]], wr=[s1b])
        tf = ap3(s2, c2, [[s2w, 128], [1, F]])
        kf = ap3(s3, c3, [[s3w, 128], [1, F]])
        V(lambda e: e.tensor_single_scalar(tf, vf, 0.25, OP.is_gt),
          rd=[s1b], wr=[s2b])
        for thr in (0.75, 1.25, 1.75):
            V(lambda e, thr=thr: e.scalar_tensor_tensor(
                tf, vf, thr, tf, OP.is_gt, OP.add), rd=[s1b, s2b], wr=[s2b])
        V(lambda e: e.tensor_single_scalar(kf, vf, 2.5, OP.is_gt),
          rd=[s1b], wr=[s3b])
        for thr in (3.5, 5.0, 5.0):
            V(lambda e, thr=thr: e.scalar_tensor_tensor(
                kf, vf, thr, kf, OP.is_gt, OP.add), rd=[s1b, s3b], wr=[s3b])
        V(lambda e: e.scalar_tensor_tensor(tf, tf, 0.5, kf, OP.mult, OP.add),
          rd=[s2b, s3b], wr=[s2b])
        if pool:
            V(lambda e: e.tensor_single_scalar(
                kf, ap3(src, col0, [[sw, 128], [1, F]]), 0.0, OP.is_lt),
              rd=[src_b], wr=[s3b])
            V(lambda e: e.tensor_single_scalar(
                vf, ap3(src, col0, [[sw, 128], [1, F]]), 0.0, OP.is_gt),
              rd=[src_b], wr=[s1b])
            V(lambda e: e.tensor_tensor(vf, vf, kf, OP.subtract),
              rd=[s1b, s3b], wr=[s1b])
        else:
            cb.a(lambda e: e.activation(
                vf, ap3(src, col0, [[sw, 128], [1, F]]), AF.Sign),
                rd=[src_b], wr=[s1b])
        V(lambda e: e.tensor_tensor(tf, tf, vf, OP.mult),
          rd=[s2b, s1b], wr=[s2b])
        t3 = ap3(s2, c2, [[s2w, 128], [16, NB], [1, 16]])
        d3 = ap3(dst, col0, [[dw, 128], [16, NB], [1, 16]])
        ssb = ap3(ub, 256 + sc_off, [[SLAB, 128], [1, NB], [0, 16]])
        V(lambda e: e.tensor_tensor(d3, t3, ssb, OP.mult),
          rd=[s2b, B["ub"]], wr=[dst_b])

    def prep_fp4_weight(w_dram, wcols, i0, isz, along_rows, mat, stripe,
                        thunks):
        if along_rows:
            nrc, ncols = isz // 128, wcols
            off = i0 * wcols
        else:
            nrc, ncols = D // 128, isz
            off = i0

        def ld(e):
            o = ap3(x1b, 0, [[SL, 128], [ncols, nrc], [1, ncols]])
            i = ap3(w_dram, off, [[wcols, 128], [128 * wcols, nrc],
                                  [1, ncols]])
            return e.dma_start(out=o, in_=i)
        thunks.append(lambda: cb.dml(ld, wr=[B["x1b"]]))
        total = nrc * ncols
        for p0 in range(0, total, 1024):
            pf = min(1024, total - p0)
            thunks.append(lambda p0=p0, pf=pf: quant_tm(
                x1b, B["x1b"], SL, x1b, B["x1b"], SL, p0, pf,
                a2b, B["a2b"], SL, 0, a2b, B["a2b"], SL, 1024,
                a2b, B["a2b"], SL, 2048, sc_off=64))
        nkc = ncols // 128
        for kc in range(nkc):
            for grp in range((nrc + 3) // 4):
                bank = 5 + grp % 2
                pbk = pb[bank]
                cnt = min(4, nrc - grp * 4)

                def tr(e, kc=kc, grp=grp, cnt=cnt, pbk=pbk):
                    r = None
                    for j in range(cnt):
                        rc = grp * 4 + j
                        r = e.transpose(
                            pbk[:, j * 128:(j + 1) * 128],
                            x1b[:, rc * ncols + kc * 128:
                                rc * ncols + (kc + 1) * 128],
                            ident[:, :])
                    return r

                def group(tr=tr, bank=bank, pbk=pbk, cnt=cnt,
                          kc=kc, grp=grp, mat=mat, stripe=stripe, nrc=nrc):
                    cb.t(tr, rd=[B["x1b"], B["ident_s"]], wr=[PB[bank]])
                    cb.v(lambda e, cnt=cnt, pbk=pbk: e.tensor_copy(
                        a2b[:, 3072:3072 + cnt * 128], pbk[:, 0:cnt * 128]),
                        rd=[PB[bank]], wr=[B["a2b"]])
                    dcol = kc * (nrc * 128) + grp * 512

                    def stw(e, mat=mat, stripe=stripe, dcol=dcol, cnt=cnt):
                        o = ap3(w4_d, (mat * 6 + stripe) * 128 * 4096 + dcol,
                                [[4096, 128], [1, cnt * 128]])
                        return e.dma_start(out=o,
                                           in_=a2b[:, 3072:3072 + cnt * 128])
                    cb.dms(stw, rd=[B["a2b"]], wr=[Bw4])
                thunks.append(group)

    STRIPES4 = [(i, min(512, FI - i)) for i in range(0, FI, 512)]
    prep_thunks = []
    for st4, (i0_, isz_) in enumerate(STRIPES4):
        prep_fp4_weight(wg_in, D, i0_, isz_, True, 0, st4, prep_thunks)
        prep_fp4_weight(wu_in, D, i0_, isz_, True, 1, st4, prep_thunks)
        prep_fp4_weight(wd_in, FI, i0_, isz_, False, 2, st4, prep_thunks)

    def drain_prep(n):
        for _ in range(n):
            if prep_thunks:
                prep_thunks.pop(0)()

    for s in range(NS):
        def ldx(e, s=s):
            o = ap3(xb, 0, [[SL, 128], [D, NCT], [1, D]])
            i = ap3(xin, s * SLAB * D, [[D, 128], [128 * D, NCT], [1, D]])
            return e.dma_start(out=o, in_=i)
        cb.dml(ldx, wr=[B["xb"]])
        ln_stats_tm(xb, B["xb"], tb, B["tb"], m1b, B["m1b"], r1b, B["r1b"],
                    s * NCT, EPS_LN)
        ln_apply_tm(xb, B["xb"], tb, B["tb"], m1b, B["m1b"], r1b, B["r1b"],
                    s * NCT, 32)
        tr_tm_to_fm(tb, B["tb"], hhi, B["hhi"], hlo, B["hlo"], (1, 2),
                    affine=(M_CA1, M_SA), aff_tmp=kb, aff_tmp_b=B["kb"])
        cb.dms(lambda e, s=s: e.dma_start(out=h_hi_d.ap()[s], in_=hhi[:, :]),
               rd=[B["hhi"]], wr=[SP["h"]])
        cb.dms(lambda e, s=s: e.dma_start(out=h_lo_d.ap()[s], in_=hlo[:, :]),
               rd=[B["hlo"]], wr=[SP["h"]])
        for f in range(KC):
            bank = 3 + f % 2
            mm3_fm(bank, wa_hi, wa_lo, (B["wa_hi"], B["wa_lo"]),
                   hhi, hlo, (B["hhi"], B["hlo"]), f, 0, 1024)
            cb.a(lambda e, f=f, bank=bank: e.activation(
                ub[:, :], pb[bank][:, 0:SLAB], AF.Identity,
                bias=vcol(C_BQ + f)), rd=[PB[bank], B["vecs"]], wr=[B["ub"]])
            cb.v(lambda e: e.tensor_scalar_min(t1s[:, :], ub[:, :], 0.0),
                 rd=[B["ub"]], wr=[B["t1s"]])
            cb.a(lambda e: e.activation(t2s[:, :], t1s[:, :], AF.Exp),
                 rd=[B["t1s"]], wr=[B["t2s"]])
            cb.v(lambda e, f=f: e.scalar_tensor_tensor(
                qb[:, f * SLAB:(f + 1) * SLAB], ub[:, :], 0.0, t2s[:, :],
                OP.max, OP.add), rd=[B["ub"], B["t2s"]], wr=[B["qb"]])
        cb.dms(lambda e, s=s: e.dma_start(out=q_d.ap()[s], in_=qb[:, :]),
               rd=[B["qb"]], wr=[SP["q"]])

    if stop_after == "a1":
        return _finish_queues(cb, nc)
    load_split(wk_in, D, 0, 0, D, 1024, wa_hi, wa_lo, B["wa_hi"], B["wa_lo"])
    load_split(wv_in, D, 0, 0, D, 1024, wb_hi, wb_lo, WBH[0], WBL[0])
    for s in range(NS):
        cb.dml(lambda e, s=s: e.dma_start(out=hhi[:, :], in_=h_hi_d.ap()[s]),
               rd=[SP["h"]], wr=[B["hhi"]])
        cb.dml(lambda e, s=s: e.dma_start(out=hlo[:, :], in_=h_lo_d.ap()[s]),
               rd=[SP["h"]], wr=[B["hlo"]])
        for proj in ("k", "v"):
            whi_, wlo_ = (wa_hi, wa_lo) if proj == "k" else (wb_hi, wb_lo)
            whib = B["wa_hi"] if proj == "k" else WBH[0]
            wlob = B["wa_lo"] if proj == "k" else WBL[0]
            for c in range(NCT):
                for nh in range(2):
                    bank = 3 + (c * 2 + nh) % 2
                    pbk = pb[bank]

                    def mm(e, c=c, nh=nh, pbk=pbk, whi_=whi_, wlo_=wlo_):
                        r = None
                        n = 0
                        for (wsrc, xsrc) in ((whi_, hhi), (wlo_, hhi),
                                             (whi_, hlo)):
                            for kc in range(KC):
                                r = e.matmul(
                                    pbk[:, 0:512],
                                    xsrc[:, kc * SLAB + c * 128:
                                         kc * SLAB + (c + 1) * 128],
                                    wsrc[:, kc * 1024 + nh * 512:
                                         kc * 1024 + (nh + 1) * 512],
                                    start=(n == 0), stop=(n == 3 * KC - 1))
                                n += 1
                        return r
                    cb.t(mm, rd=[whib, wlob, B["hhi"], B["hlo"]],
                         wr=[PB[bank]])
                    if proj == "k":
                        cb.v(lambda e, nh=nh, pbk=pbk: e.tensor_add(
                            ub[:, :], pbk[:, 0:512],
                            rep[:, R_BK + nh * 512:R_BK + (nh + 1) * 512]),
                            rd=[PB[bank], B["rep"]], wr=[B["ub"]])
                        cb.v(lambda e: e.tensor_scalar_min(
                            t1s[:, :], ub[:, :], 0.0),
                            rd=[B["ub"]], wr=[B["t1s"]])
                        cb.a(lambda e: e.activation(t2s[:, :], t1s[:, :],
                                                    AF.Exp),
                             rd=[B["t1s"]], wr=[B["t2s"]])
                        cb.v(lambda e, c=c, nh=nh: e.scalar_tensor_tensor(
                            kb[:, c * D + nh * 512:c * D + (nh + 1) * 512],
                            ub[:, :], 0.0, t2s[:, :], OP.max, OP.add),
                            rd=[B["ub"], B["t2s"]], wr=[B["kb"]])
                    else:
                        cb.v(lambda e, c=c, nh=nh, pbk=pbk: e.tensor_add(
                            tb[:, c * D + nh * 512:c * D + (nh + 1) * 512],
                            pbk[:, 0:512],
                            rep[:, R_BV + nh * 512:R_BV + (nh + 1) * 512]),
                            rd=[PB[bank], B["rep"]], wr=[B["tb"]])
        for c in range(NCT):
            bank = 5 + c % 2
            pbk = pb[bank]
            p7off = (c % 2) * 65

            def mmkv(e, c=c, pbk=pbk, p7off=p7off):
                r = None
                for hh in range(H):
                    hp, half = hh // 2, hh % 2
                    dst = (pbk[half * 64:(half + 1) * 64,
                               hp * 65:hp * 65 + 64] if hp < 7 else
                           pb[7][half * 64:(half + 1) * 64,
                                 p7off:p7off + 64])
                    r = e.matmul(
                        dst,
                        kb[:, c * D + hh * 64:c * D + (hh + 1) * 64],
                        tb[:, c * D + hh * 64:c * D + (hh + 1) * 64],
                        start=True, stop=True)
                for hh in range(H):
                    hp, half = hh // 2, hh % 2
                    dst = (pbk[half * 64:(half + 1) * 64,
                               hp * 65 + 64:hp * 65 + 65] if hp < 7 else
                           pb[7][half * 64:(half + 1) * 64,
                                 p7off + 64:p7off + 65])
                    r = e.matmul(
                        dst,
                        kb[:, c * D + hh * 64:c * D + (hh + 1) * 64],
                        onescol[:, :],
                        start=True, stop=True)
                return r
            cb.t(mmkv, rd=[B["kb"], B["tb"], B["onescol"]],
                 wr=[PB[bank], PB[7]])
            cb.v(lambda e, pbk=pbk: e.tensor_add(
                qb[:, 0:455], qb[:, 0:455], pbk[:, 0:455]),
                rd=[PB[bank], B["qb"]], wr=[B["qb"]])
            cb.v(lambda e, p7off=p7off: e.tensor_add(
                qb[:, 455:520], qb[:, 455:520],
                pb[7][:, p7off:p7off + 65]),
                rd=[PB[7], B["qb"]], wr=[B["qb"]])
        drain_prep(4)

    if stop_after == "a2":
        return _finish_queues(cb, nc)
    drain_prep(len(prep_thunks))

    cb.dms(lambda e: e.dma_start(out=cc_in.ap()[:, :], in_=qb[:, 0:520]),
           rd=[B["qb"]], wr=[Bcc])

    if skip_collective:
        cb.dms(lambda e: e.dma_start(out=cc_out.ap()[:, :],
                                     in_=cc_in.ap()[:, :]),
               rd=[Bcc], wr=[Bcc])
    else:
        def ccf(e):
            return e.collective_compute(
                "AllReduce", OP.add,
                replica_groups=[[0, 1, 2, 3], [4, 5, 6, 7]],
                ins=[cc_in.ap().opt()], outs=[cc_out.ap().opt()])
        cb._emit("gpsimd", ccf, [Bcc], [Bcc], 1, False, semkey="cc")
    cb.dml(lambda e: e.dma_start(out=qb[:, 0:520], in_=cc_out.ap()[:, :]),
           rd=[Bcc], wr=[B["qb"]])

    def expand_kv(half):
        def f(e):
            o = ap3(kvexp, half * (64 * 8 * 130 + 65),
                    [[8 * 130, 64], [130, 8], [1, 65]])
            i = ap3(qb, half * 64 * SL, [[SL, 64], [65, 8], [1, 65]])
            return e.dma_start(out=o, in_=i)
        cb.dms(f, rd=[B["qb"]], wr=[B["kvexp"]])
    expand_kv(0)
    expand_kv(1)

    for j, rc in enumerate((R_GA, R_GB, R_CB1, R_SB)):
        replicate(rc, garow_d, j * D)

    if stop_after == "cc":
        return _finish_queues(cb, nc)
    load_split(wo_in, D, 0, 0, D, 1024, wa_hi, wa_lo, B["wa_hi"], B["wa_lo"])
    x1hi = wb_hi
    x1lo = wb_lo

    def bc1_A(s, dst, dst_b):
        cb.dml(lambda e, s=s: e.dma_start(out=qb[:, :], in_=q_d.ap()[s]),
               rd=[SP["q"]], wr=[B["qb"]])
        cb.dml(lambda e, s=s: e.dma_start(out=hhi[:, :], in_=h_hi_d.ap()[s]),
               rd=[SP["h"]], wr=[B["hhi"]])
        cb.dml(lambda e, s=s: e.dma_start(out=hlo[:, :], in_=h_lo_d.ap()[s]),
               rd=[SP["h"]], wr=[B["hlo"]])
        for c in range(NCT):
            def mmo(e, c=c):
                r = None
                for hp in range(8):
                    b_i = hp // 3
                    off = (hp % 3) * 130
                    r = e.matmul(
                        pb[b_i][:, off:off + 130],
                        qb[:, hp * SLAB + c * 128:hp * SLAB + (c + 1) * 128],
                        kvexp[:, hp * 130:(hp + 1) * 130],
                        start=True, stop=True)
                return r
            cb.t(mmo, rd=[B["qb"], B["kvexp"]], wr=[PB[0], PB[1], PB[2]])
            for b_i, nh in ((0, 6), (1, 6), (2, 4)):
                cb.v(lambda e, b_i=b_i, nh=nh: e.tensor_scalar_add(
                    zb[:, b_i * 6:b_i * 6 + nh],
                    ap3(pb[b_i], 64, [[512, 128], [65, nh]]), EPS_Z),
                    rd=[PB[b_i]], wr=[B["zb"]])
            cb.v(lambda e: e.reciprocal(zb[:, 32:48], zb[:, 0:16]),
                 rd=[B["zb"]], wr=[B["zb"]])
            for b_i, nh in ((0, 6), (1, 6), (2, 4)):
                def f(e, c=c, b_i=b_i, nh=nh):
                    o = ap3(xb, c * D + b_i * 6 * 64,
                            [[SL, 128], [64, nh], [1, 64]])
                    i = ap3(pb[b_i], 0, [[512, 128], [65, nh], [1, 64]])
                    i2 = ap3(zb, 32 + b_i * 6, [[64, 128], [1, nh], [0, 64]])
                    return e.tensor_tensor(o, i, i2, OP.mult)
                cb.v(f, rd=[PB[b_i], B["zb"]], wr=[B["xb"]])
        tr_tm_to_fm(xb, B["xb"], wb_hi, WBH[2], wb_lo, WBL[2],
                    (3, 4), dcol0=4096)
        for f in range(KC):
            bank = 5 + f % 2
            mm3_fm(bank, wa_hi, wa_lo, (B["wa_hi"], B["wa_lo"]),
                   wb_hi, wb_lo, (WBH[2], WBL[2]), f, 0, 1024,
                   rcol0=4096)
            cb.a(lambda e, f=f, bank=bank: e.activation(
                xb[:, 0:SLAB], pb[bank][:, 0:SLAB],
                AF.Identity, bias=vcol(C_BO + f)),
                rd=[PB[bank], B["vecs"]], wr=[B["xb"]])
            cb.v(lambda e, f=f: e.tensor_add(
                xb[:, SLAB:2 * SLAB], xb[:, 0:SLAB],
                hhi[:, f * SLAB:(f + 1) * SLAB]),
                rd=[B["xb"], B["hhi"]], wr=[B["xb"]])
            cb.v(lambda e, f=f: e.tensor_add(
                dst[:, f * SLAB:(f + 1) * SLAB], xb[:, SLAB:2 * SLAB],
                hlo[:, f * SLAB:(f + 1) * SLAB]),
                rd=[B["xb"], B["hlo"]], wr=[dst_b])

    def bc1_P(s, srcbuf, src_b):
        fm_apply(srcbuf, src_b, C_G1, vecs, B["vecs"], 80, C_BE1,
                 x1b, B["x1b"], tb, B["tb"])
        cb.a(lambda e: e.activation(x1hi[:, 0:SL], x1b[:, :], AF.Identity),
             rd=[B["x1b"]], wr=[WBH[0], WBH[1]])
        cb.v(lambda e: e.tensor_sub(x1lo[:, 0:SL], x1b[:, :],
                                    x1hi[:, 0:SL]),
             rd=[B["x1b"], WBH[0], WBH[1]], wr=[WBL[0], WBL[1]])
        cb.dms(lambda e, s=s: e.dma_start(out=x1_hi_d.ap()[s],
                                          in_=x1hi[:, 0:SL]),
               rd=[WBH[0], WBH[1]], wr=[SP["x1"]])
        cb.dms(lambda e, s=s: e.dma_start(out=x1_lo_d.ap()[s],
                                          in_=x1lo[:, 0:SL]),
               rd=[WBL[0], WBL[1]], wr=[SP["x1"]])

    BC1_PIPE = True
    if BC1_PIPE:
        prev = None
        for s in range(NS):
            dst, dst_b = (a2b, B["a2b"]) if s % 2 == 0 else (kb, B["kb"])
            bc1_A(s, dst, dst_b)
            if prev is not None:
                bc1_P(prev[0], prev[1], prev[2])
            fm_stats(dst, dst_b, tb, B["tb"], EPS_LN)
            prev = (s, dst, dst_b)
        bc1_P(prev[0], prev[1], prev[2])
    else:
        for s in range(NS):
            dst, dst_b = (a2b, B["a2b"]) if s % 2 == 0 else (kb, B["kb"])
            bc1_A(s, dst, dst_b)
            fm_stats(dst, dst_b, tb, B["tb"], EPS_LN)
            bc1_P(s, dst, dst_b)

    if stop_after == "bc1":
        return _finish_queues(cb, nc)

    cb.dml(lambda e: e.dma_start(out=tb[0:1, 0:D], in_=g2_row.ap()[:, :]),
           wr=[B["tb"]])
    cb.dml(lambda e: e.dma_start(out=tb[0:1, D:2 * D],
                                 in_=be2_row.ap()[:, :]), wr=[B["tb"]])
    for j in range(2):
        cb.t(lambda e, j=j: e.matmul(
            pb[4][:, 0:SLAB], onesr[:, :],
            tb[0:1, D + j * SLAB:D + (j + 1) * SLAB],
            start=True, stop=True), rd=[B["tb"], B["onesr"]], wr=[PB[4]])
        cb.v(lambda e, j=j: e.tensor_mul(
            kvexp[:, j * SLAB:(j + 1) * SLAB], pb[4][:, 0:SLAB],
            rep[:, R_GA + j * SLAB:R_GA + (j + 1) * SLAB]),
            rd=[PB[4], B["rep"]], wr=[B["kvexp"]])
    for j in range(2):
        cb.t(lambda e, j=j: e.matmul(
            pb[5][:, 0:SLAB], onesr[:, :], tb[0:1, j * SLAB:(j + 1) * SLAB],
            start=True, stop=True), rd=[B["tb"], B["onesr"]], wr=[PB[5]])
        cb.v(lambda e, j=j: e.tensor_mul(
            rep[:, R_GA + j * SLAB:R_GA + (j + 1) * SLAB],
            rep[:, R_GA + j * SLAB:R_GA + (j + 1) * SLAB],
            pb[5][:, 0:SLAB]), rd=[PB[5], B["rep"]], wr=[B["rep"]])

    def w12_off(st, part, col):
        return (st * 2 + part) * 128 * W_COLS + col

    def emit_wload(s, st):
        if s == 0:
            first = st == 0
            load_split(w1_in, FF, 0, st * 512, D, 512, wa_hi, wa_lo,
                       [WAH[0]] + ([B["wa_hi"]] if first else []),
                       [WAL[0]] + ([B["wa_lo"]] if first else []), dcol0=0)
            load_split(w2_in, D, st * 512, 0, 512, 1024, wa_hi, wa_lo,
                       [WAH[1]] + ([B["wa_hi"]] if first else []),
                       [WAL[1]] + ([B["wa_lo"]] if first else []),
                       dcol0=4096)
            for part, src, srcb in ((0, wa_hi, [WAH[0]]),
                                    (1, wa_lo, [WAL[0]])):
                def sp1(e, st=st, part=part, src=src):
                    o = ap3(w12_d, w12_off(st, part, 0),
                            [[W_COLS, 128], [1, 4096]])
                    return e.dma_start(out=o, in_=src[:, 0:4096])
                cb.dms(sp1, rd=srcb, wr=[Bw12[st]])
            for part, src, srcb in ((0, wa_hi, [WAH[1]]),
                                    (1, wa_lo, [WAL[1]])):
                def sp2(e, st=st, part=part, src=src):
                    o = ap3(w12_d, w12_off(st, part, 4096),
                            [[W_COLS, 128], [1, 4096]])
                    return e.dma_start(out=o, in_=src[:, 4096:8192])
                cb.dms(sp2, rd=srcb, wr=[Bw12[st]])
        else:
            for part, dst, dstb in ((0, wa_hi, [WAH[0]]),
                                    (1, wa_lo, [WAL[0]])):
                def ld1(e, st=st, part=part, dst=dst):
                    i = ap3(w12_d, w12_off(st, part, 0),
                            [[W_COLS, 128], [1, 4096]])
                    return e.dma_start(out=dst[:, 0:4096], in_=i)
                cb.dml(ld1, rd=[Bw12[st]], wr=dstb)
            for part, dst, dstb in ((0, wa_hi, [WAH[1]]),
                                    (1, wa_lo, [WAL[1]])):
                def ld2(e, st=st, part=part, dst=dst):
                    i = ap3(w12_d, w12_off(st, part, 4096),
                            [[W_COLS, 128], [1, 4096]])
                    return e.dma_start(out=dst[:, 4096:8192], in_=i)
                cb.dml(ld2, rd=[Bw12[st]], wr=dstb)

    def slab_bufs(s):
        p = s % 2
        Y, BY = (qb, B["qb"]) if p == 0 else (a2b, B["a2b"])
        if p == 0:
            return Y, BY, hhi, hlo, [B["hhi"]], [B["hlo"]]
        return Y, BY, wb_hi, wb_lo, [WBH[0], WBH[1]], [WBL[0], WBL[1]]

    def emit_stripe(s, st):
        Y, BY, xh, xl, Bxh, Bxl = slab_bufs(s)
        for gf in range(4):
            bank = gf % 2
            mm3_fm(bank, wa_hi, wa_lo, ([WAH[0]], [WAL[0]]), xh, xl,
                   (Bxh, Bxl), gf, 0, 512, nkc=8)
            cb.a(lambda e, gf=gf, bank=bank, st=st: e.activation(
                tb[:, gf * SLAB:(gf + 1) * SLAB], pb[bank][:, 0:SLAB],
                AF.Gelu, bias=vcol(C_B1 + st * 4 + gf)),
                rd=[PB[bank], B["vecs"]], wr=[B["tb"]])
            cb.a(lambda e, gf=gf: e.activation(
                wb_hi[:, 4096 + gf * SLAB:4096 + (gf + 1) * SLAB],
                tb[:, gf * SLAB:(gf + 1) * SLAB], AF.Identity),
                rd=[B["tb"]], wr=[WBH[2]])
            cb.v(lambda e, gf=gf: e.tensor_sub(
                wb_lo[:, 4096 + gf * SLAB:4096 + (gf + 1) * SLAB],
                tb[:, gf * SLAB:(gf + 1) * SLAB],
                wb_hi[:, 4096 + gf * SLAB:4096 + (gf + 1) * SLAB]),
                rd=[B["tb"], WBH[2]], wr=[WBL[2]])
        for yo in range(KC):
            bank = (2, 3, 6, 7)[yo % 4]
            mm3_fm(bank, wa_hi, wa_lo, ([WAH[1]], [WAL[1]]),
                   wb_hi, wb_lo, (WBH[2], WBL[2]), yo, 4096, 1024,
                   nkc=4, rcol0=4096, rstep=SLAB)
            if st == 0:
                cb.a(lambda e, yo=yo, bank=bank, Y=Y, BY=BY: e.activation(
                    Y[:, yo * SLAB:(yo + 1) * SLAB], pb[bank][:, 0:SLAB],
                    AF.Identity, bias=vcol(C_B2 + yo)),
                    rd=[PB[bank], B["vecs"]], wr=[BY])
            else:
                cb.v(lambda e, yo=yo, bank=bank, Y=Y, BY=BY: e.tensor_add(
                    Y[:, yo * SLAB:(yo + 1) * SLAB], pb[bank][:, 0:SLAB],
                    Y[:, yo * SLAB:(yo + 1) * SLAB]),
                    rd=[PB[bank], BY], wr=[BY])

    def bc3_piece(s, pid):
        Y, BY, xh, xl, Bxh, Bxl = slab_bufs(s)
        if pid == 0:
            cb.v(lambda e, Y=Y, xh=xh: e.tensor_add(x1b[:, :], Y[:, :],
                                                    xh[:, 0:SL]),
                 rd=[BY] + Bxh, wr=[B["x1b"]])
            cb.v(lambda e, xl=xl: e.tensor_add(x1b[:, :], x1b[:, :],
                                               xl[:, 0:SL]),
                 rd=[B["x1b"]] + Bxl, wr=[B["x1b"]])
        elif pid == 1:
            for f in range(KC):
                bk = 4 + f % 2
                pbk = pb[bk]

                def tr(e, f=f, pbk=pbk):
                    r = None
                    for c in range(NCT):
                        r = e.transpose(
                            pbk[:, c * 128:(c + 1) * 128],
                            x1b[:, f * SLAB + c * 128:
                                f * SLAB + (c + 1) * 128],
                            ident[:, :])
                    return r
                cb.t(tr, rd=[B["x1b"], B["ident_s"]], wr=[PB[bk]])

                def ev(e, f=f, pbk=pbk):
                    o = ap3(kb, f * 128, [[SL, 128], [D, NCT], [1, 128]])
                    i = ap3(pbk, 0, [[512, 128], [128, NCT], [1, 128]])
                    return e.activation(o, i, AF.Identity)
                cb.a(ev, rd=[PB[bk]], wr=[B["kb"]])
        elif pid == 2:
            ln_stats_tm(kb, B["kb"], x1b, B["x1b"], m4b, B["m4b"],
                        r4b, B["r4b"], 0, EPS_LN)
        elif pid == 3:
            ln_apply_tm(kb, B["kb"], kb, B["kb"], m4b, B["m4b"],
                        r4b, B["r4b"], 0, 4)

            def gm(e):
                k3 = ap3(kb, 0, [[SL, 128], [D, NCT], [1, D]])
                r3 = ap3(rep, R_GA, [[4 * D, 128], [0, NCT], [1, D]])
                return e.tensor_tensor(k3, k3, r3, OP.mult)
            cb.v(gm, rd=[B["kb"], B["rep"]], wr=[B["kb"]])
        elif pid == 4:
            def gab(e):
                k3 = ap3(kb, 0, [[SL, 128], [D, NCT], [1, D]])
                g3 = ap3(kvexp, 0, [[8 * 130, 128], [0, NCT], [1, D]])
                return e.tensor_tensor(k3, k3, g3, OP.add)
            cb.v(gab, rd=[B["kb"], B["kvexp"]], wr=[B["kb"]])
            cb.v(lambda e: e.tensor_add(kb[:, :], kb[:, :], xb[:, :]),
                 rd=[B["kb"], B["xb"]], wr=[B["kb"]])
            cb.dms(lambda e, s=s: e.dma_start(out=x2_d.ap()[s],
                                              in_=kb[:, :]),
                   rd=[B["kb"]], wr=[SP["x2"]])
        elif pid == 5:
            ln_stats_tm(kb, B["kb"], x1b, B["x1b"], m1b, B["m1b"],
                        r1b, B["r1b"], 0, EPS_LN)
        elif pid == 6:
            ln_apply_tm(kb, B["kb"], kb, B["kb"], m1b, B["m1b"],
                        r1b, B["r1b"], 0, 32)

            def cm(e):
                k3 = ap3(kb, 0, [[SL, 128], [D, NCT], [1, D]])
                r3 = ap3(rep, R_CB1, [[4 * D, 128], [0, NCT], [1, D]])
                return e.tensor_tensor(k3, k3, r3, OP.mult)
            cb.v(cm, rd=[B["kb"], B["rep"]], wr=[B["kb"]])

            def sm(e):
                k3 = ap3(kb, 0, [[SL, 128], [D, NCT], [1, D]])
                r3 = ap3(rep, R_SB, [[4 * D, 128], [0, NCT], [1, D]])
                return e.tensor_tensor(k3, k3, r3, OP.add)
            cb.v(sm, rd=[B["kb"], B["rep"]], wr=[B["kb"]])
        elif pid == 7:
            sA = ap3(kb, 0, [[SL, 128], [16, 256], [1, 16]])
            cb.v(lambda e, sA=sA: e.tensor_reduce(
                ub[:, 0:256], sA, AX.X, OP.max, apply_absolute_value=True),
                rd=[B["kb"]], wr=[B["ub"]])
            cb.v(lambda e: e.tensor_scalar(ub[:, 256:512], ub[:, 0:256],
                                           r6c[:, 0:1], 1e-12, OP.mult,
                                           OP.max),
                 rd=[B["ub"], B["r6c"]], wr=[B["ub"]])
            cb.v(lambda e: e.reciprocal(ub[:, 0:256], ub[:, 256:512]),
                 rd=[B["ub"]], wr=[B["ub"]])
            cb.a(lambda e, Y=Y: e.activation(Y[:, :], kb[:, :], AF.Abs),
                 rd=[B["kb"]], wr=[BY])
            v3 = ap3(Y, 0, [[SL, 128], [16, 256], [1, 16]])
            rsb = ap3(ub, 0, [[SLAB, 128], [1, 256], [0, 16]])
            cb.v(lambda e, v3=v3, rsb=rsb: e.tensor_tensor(v3, v3, rsb,
                                                           OP.mult),
                 rd=[BY, B["ub"]], wr=[BY])
        elif pid == 8:
            vf = ap3(Y, 0, [[SL, 128], [1, SL]])
            tf = ap3(x1b, 0, [[SL, 128], [1, SL]])
            cb.v(lambda e, vf=vf, tf=tf: e.tensor_single_scalar(
                tf, vf, 0.25, OP.is_gt), rd=[BY], wr=[B["x1b"]])
            for thr in (0.75, 1.25, 1.75):
                cb.v(lambda e, vf=vf, tf=tf, thr=thr: e.scalar_tensor_tensor(
                    tf, vf, thr, tf, OP.is_gt, OP.add),
                    rd=[BY, B["x1b"]], wr=[B["x1b"]])
        elif pid == 9:
            vf = ap3(Y, 0, [[SL, 128], [1, SL]])
            kf = ap3(xb, 0, [[SL, 128], [1, SL]])
            cb.v(lambda e, vf=vf, kf=kf: e.tensor_single_scalar(
                kf, vf, 2.5, OP.is_gt), rd=[BY], wr=[B["xb"]])
            for thr in (3.5, 5.0, 5.0):
                cb.v(lambda e, vf=vf, kf=kf, thr=thr: e.scalar_tensor_tensor(
                    kf, vf, thr, kf, OP.is_gt, OP.add),
                    rd=[BY, B["xb"]], wr=[B["xb"]])
        elif pid == 10:
            vf = ap3(Y, 0, [[SL, 128], [1, SL]])
            tf = ap3(x1b, 0, [[SL, 128], [1, SL]])
            kf = ap3(xb, 0, [[SL, 128], [1, SL]])
            cb.v(lambda e, tf=tf, kf=kf: e.scalar_tensor_tensor(
                tf, tf, 0.5, kf, OP.mult, OP.add),
                rd=[B["x1b"], B["xb"]], wr=[B["x1b"]])
            cb.a(lambda e, Y=Y: e.activation(Y[:, :], kb[:, :], AF.Sign),
                 rd=[B["kb"]], wr=[BY])
            cb.v(lambda e, tf=tf, vf=vf: e.tensor_tensor(tf, tf, vf,
                                                         OP.mult),
                 rd=[B["x1b"], BY], wr=[B["x1b"]])
            t3 = ap3(x1b, 0, [[SL, 128], [16, 256], [1, 16]])
            d3 = ap3(kb, 0, [[SL, 128], [16, 256], [1, 16]])
            ssb = ap3(ub, 256, [[SLAB, 128], [1, 256], [0, 16]])
            cb.v(lambda e, t3=t3, d3=d3, ssb=ssb: e.tensor_tensor(
                d3, t3, ssb, OP.mult), rd=[B["x1b"], B["ub"]], wr=[B["kb"]])
        elif pid == 11:
            tr_tm_to_fm(kb, B["kb"], wa_hi, WAH[1], wa_lo, WAL[1], (4, 5),
                        dcol0=4096)
            cb.dms(lambda e, s=s: e.dma_start(out=xq_hi_d.ap()[s],
                                              in_=wa_hi[:, 4096:4096 + SL]),
                   rd=[WAH[1]], wr=[SP["xq"]])
            cb.dms(lambda e, s=s: e.dma_start(out=xq_lo_d.ap()[s],
                                              in_=wa_lo[:, 4096:4096 + SL]),
                   rd=[WAL[1]], wr=[SP["xq"]])

    PIECE_MAP = {0: (0, 1, 2), 1: (3,), 2: (4,), 3: (5, 6), 4: (7,),
                 5: (8,), 6: (9, 10), 7: (11,)}

    cb.dml(lambda e: e.dma_start(out=hhi[:, :], in_=x1_hi_d.ap()[0]),
           rd=[SP["x1"]], wr=[B["hhi"]])
    cb.dml(lambda e: e.dma_start(out=hlo[:, :], in_=x1_lo_d.ap()[0]),
           rd=[SP["x1"]], wr=[B["hlo"]])
    emit_wload(0, 0)
    for s in range(NS):
        for st in range(8):
            emit_stripe(s, st)
            if s >= 1:
                for pid in PIECE_MAP[st]:
                    bc3_piece(s - 1, pid)
            if st < 7:
                emit_wload(s, st + 1)
            elif s + 1 < NS:
                emit_wload(s + 1, 0)
            if st == 1 and s + 1 < NS:
                _, _, nh, nl, nhb, nlb = slab_bufs(s + 1)
                cb.dml(lambda e, s=s, nh=nh: e.dma_start(
                    out=nh[:, 0:SL], in_=x1_hi_d.ap()[s + 1]),
                    rd=[SP["x1"]], wr=nhb)
                cb.dml(lambda e, s=s, nl=nl: e.dma_start(
                    out=nl[:, 0:SL], in_=x1_lo_d.ap()[s + 1]),
                    rd=[SP["x1"]], wr=nlb)
            if st == 6:
                def ldx2(e, s=s):
                    o = ap3(xb, 0, [[SL, 128], [D, NCT], [1, D]])
                    i = ap3(xin, s * SLAB * D,
                            [[D, 128], [128 * D, NCT], [1, D]])
                    return e.dma_start(out=o, in_=i)
                cb.dml(ldx2, wr=[B["xb"]])
    for pid in range(12):
        bc3_piece(NS - 1, pid)

    if stop_after in ("bc2", "bc3"):
        return _finish_queues(cb, nc)

    STRIPES = STRIPES4
    NST4 = len(STRIPES)

    def w4_bufs(mat):
        if mat == 0:
            return wa_hi, wa_lo, 0, [WAH[0]], [WAL[0]]
        if mat == 1:
            return wa_hi, wa_lo, 4096, [WAH[1]], [WAL[1]]
        return wb_hi, wb_lo, 0, [WBH[0], WBH[1]], [WBL[0], WBL[1]]

    def emit_w4load_one(s, sti, mat):
        i0, isz = STRIPES[sti]
        nic = isz // 128
        ncols = nic * 1024 if mat == 2 else KC * isz
        hi_t, lo_t, dc0, hib, lob = w4_bufs(mat)
        first = s == 0 and sti == 0 and mat == 0
        wh = hib + ([B["wa_hi"]] if first else [])
        wl = lob + ([B["wa_lo"]] if first else [])
        if s == 0:
            def ldc(e, mat=mat, dc0=dc0, ncols=ncols, sti=sti, hi_t=hi_t):
                o = ap3(hi_t, dc0, [[W_COLS, 128], [1, ncols]])
                i = ap3(w4_d, (mat * 6 + sti) * 128 * 4096,
                        [[4096, 128], [1, ncols]])
                return e.dma_start(out=o, in_=i)
            cb.dms(ldc, rd=[Bw4], wr=wh)

            def ldf(e, mat=mat, ncols=ncols, sti=sti):
                o = ap3(xb, 0, [[SL, 128], [1, ncols]])
                i = ap3(w4_d, (mat * 6 + sti) * 128 * 4096,
                        [[4096, 128], [1, ncols]])
                return e.dma_start(out=o, in_=i)
            cb.dml(ldf, rd=[Bw4], wr=[B["xb"]])
            cb.v(lambda e, dc0=dc0, ncols=ncols, hi_t=hi_t, lo_t=lo_t:
                 e.tensor_sub(
                     ap3(lo_t, dc0, [[W_COLS, 128], [1, ncols]]),
                     ap3(xb, 0, [[SL, 128], [1, ncols]]),
                     ap3(hi_t, dc0, [[W_COLS, 128], [1, ncols]])),
                 rd=[B["xb"]] + hib, wr=wl)
            for part, src, srcb in ((0, hi_t, hib), (1, lo_t, lob)):
                def sp(e, mat=mat, sti=sti, part=part, src=src,
                       dc0=dc0, ncols=ncols):
                    o = ap3(w4b_d,
                            ((mat * 6 + sti) * 2 + part) * 128 * 4096,
                            [[4096, 128], [1, ncols]])
                    return e.dma_start(
                        out=o, in_=ap3(src, dc0,
                                       [[W_COLS, 128], [1, ncols]]))
                cb.dms(sp, rd=srcb, wr=[Bw4b[mat * 6 + sti]])
        else:
            for part, dst, dstb in ((0, hi_t, wh), (1, lo_t, wl)):
                def ldb(e, mat=mat, sti=sti, part=part, dst=dst,
                        dc0=dc0, ncols=ncols):
                    o = ap3(dst, dc0, [[W_COLS, 128], [1, ncols]])
                    i = ap3(w4b_d,
                            ((mat * 6 + sti) * 2 + part) * 128 * 4096,
                            [[4096, 128], [1, ncols]])
                    return e.dma_start(out=o, in_=i)
                cb.dms(ldb, rd=[Bw4b[mat * 6 + sti]], wr=dstb)

    def h_slot(sti):
        return ((a2b, 0, B["a2b"]), (a2b, 2048, B["a2b"]),
                (qb, 0, B["qb"]))[sti % 3]

    def bc4_p1(s, sti):
        i0, isz = STRIPES[sti]
        hb, ho, hbb = h_slot(sti)
        for c in range(NCT):
            def mmg(e, c=c, isz=isz):
                r = None
                n = 0
                for (wsrc, xsrc) in ((wa_hi, hhi), (wa_lo, hhi),
                                     (wa_hi, hlo)):
                    for kc in range(KC):
                        r = e.matmul(
                            pb[c][:, 0:isz],
                            xsrc[:, kc * SLAB + c * 128:
                                 kc * SLAB + (c + 1) * 128],
                            wsrc[:, kc * isz:(kc + 1) * isz],
                            start=(n == 0), stop=(n == 3 * KC - 1))
                        n += 1
                return r
            cb.t(mmg, rd=[WAH[0], WAL[0], B["hhi"], B["hlo"]], wr=[PB[c]])
        for c in range(NCT):
            def mmu(e, c=c, isz=isz):
                r = None
                n = 0
                for (wsrc, xsrc) in ((wa_hi, hhi), (wa_lo, hhi),
                                     (wa_hi, hlo)):
                    for kc in range(KC):
                        r = e.matmul(
                            pb[4 + c][:, 0:isz],
                            xsrc[:, kc * SLAB + c * 128:
                                 kc * SLAB + (c + 1) * 128],
                            wsrc[:, 4096 + kc * isz:4096 + (kc + 1) * isz],
                            start=(n == 0), stop=(n == 3 * KC - 1))
                        n += 1
                return r
            cb.t(mmu, rd=[WAH[1], WAL[1], B["hhi"], B["hlo"]],
                 wr=[PB[4 + c]])
        for c in range(NCT):
            cb.a(lambda e, c=c, isz=isz: e.activation(
                kb[:, 2048 + c * 512:2048 + c * 512 + isz],
                pb[c][:, 0:isz], AF.Silu),
                rd=[PB[c]], wr=[KB1])
        for c in range(NCT):
            cb.v(lambda e, c=c, isz=isz, hb=hb, ho=ho: e.tensor_mul(
                hb[:, ho + c * isz:ho + (c + 1) * isz],
                kb[:, 2048 + c * 512:2048 + c * 512 + isz],
                pb[4 + c][:, 0:isz]),
                rd=[KB1, PB[4 + c]], wr=[hbb])

    def bc4_p1q(s, sti):
        i0, isz = STRIPES[sti]
        hb, ho, hbb = h_slot(sti)
        quant_tm(hb, hbb, SL, hb, hbb, SL, ho, NCT * isz,
                 tb, B["tb"], SL, 0, tb, B["tb"], SL, 2048,
                 kb, KB0, SL, 0, sc_off=0)

    def bc4_p2(s, sti):
        i0, isz = STRIPES[sti]
        nic = isz // 128
        last = sti == NST4 - 1
        hb, ho, hbb = h_slot(sti)
        for c in range(NCT):
            pbk = pb[c % 2]

            def trh(e, c=c, nic=nic, pbk=pbk, hb=hb, ho=ho, isz=isz):
                r = None
                for ic in range(nic):
                    r = e.transpose(
                        pbk[:, ic * 128:(ic + 1) * 128],
                        hb[:, ho + c * isz + ic * 128:
                           ho + c * isz + (ic + 1) * 128],
                        ident[:, :])
                return r
            cb.t(trh, rd=[hbb, B["ident_s"]], wr=[PB[c % 2]])
            hqc = 4096 + c * 512
            cb.a(lambda e, hqc=hqc, nic=nic, pbk=pbk: e.activation(
                wb_hi[:, hqc:hqc + nic * 128], pbk[:, 0:nic * 128],
                AF.Identity), rd=[PB[c % 2]], wr=[WBH[2]])
            cb.v(lambda e, hqc=hqc, nic=nic, pbk=pbk: e.tensor_sub(
                wb_lo[:, hqc:hqc + nic * 128], pbk[:, 0:nic * 128],
                wb_hi[:, hqc:hqc + nic * 128]),
                rd=[PB[c % 2], WBH[2]], wr=[WBL[2]])
            for no in range(2):
                bank = 2 + no

                def mmd(e, c=c, no=no, bank=bank, nic=nic, hqc=hqc):
                    r = None
                    n = 0
                    for (wsrc, xsrc) in ((wb_hi, wb_hi), (wb_lo, wb_hi),
                                         (wb_hi, wb_lo)):
                        for ic in range(nic):
                            r = e.matmul(
                                pb[bank][:, 0:512],
                                xsrc[:, hqc + ic * 128:
                                     hqc + (ic + 1) * 128],
                                wsrc[:, ic * 1024 + no * 512:
                                     ic * 1024 + (no + 1) * 512],
                                start=(n == 0), stop=(n == 3 * nic - 1))
                            n += 1
                    return r
                cb.t(mmd, rd=[WBH[0], WBH[1], WBL[0], WBL[1],
                              WBH[2], WBL[2]], wr=[PB[bank]])
                if sti == 0:
                    cb.v(lambda e, c=c, no=no, bank=bank: e.tensor_copy(
                        x1b[:, c * D + no * 512:c * D + (no + 1) * 512],
                        pb[bank][:, 0:512]),
                        rd=[PB[bank]], wr=[B["x1b"]])
                elif not last:
                    cb.v(lambda e, c=c, no=no, bank=bank: e.tensor_add(
                        x1b[:, c * D + no * 512:c * D + (no + 1) * 512],
                        pb[bank][:, 0:512],
                        x1b[:, c * D + no * 512:c * D + (no + 1) * 512]),
                        rd=[PB[bank], B["x1b"]], wr=[B["x1b"]])
                else:
                    cb.v(lambda e, c=c, no=no, bank=bank: e.tensor_add(
                        t1s[:, :], pb[bank][:, 0:512],
                        x1b[:, c * D + no * 512:c * D + (no + 1) * 512]),
                        rd=[PB[bank], B["x1b"]], wr=[B["t1s"]])
                    cb.v(lambda e, no=no: e.tensor_mul(
                        t2s[:, :], t1s[:, :],
                        rep[:, R_GB + no * 512:R_GB + (no + 1) * 512]),
                        rd=[B["t1s"], B["rep"]], wr=[B["t2s"]])
                    cb.v(lambda e, c=c, no=no: e.tensor_add(
                        x1b[:, c * D + no * 512:c * D + (no + 1) * 512],
                        t2s[:, :],
                        xb[:, c * D + no * 512:c * D + (no + 1) * 512]),
                        rd=[B["t2s"], B["xb"]], wr=[B["x1b"]])
        if last:
            def stout(e, s=s):
                oo = ap3(out, s * SLAB * D,
                         [[D, 128], [128 * D, NCT], [1, D]])
                i = ap3(x1b, 0, [[SL, 128], [D, NCT], [1, D]])
                return e.dma_start(out=oo, in_=i)
            cb.dms(stout, rd=[B["x1b"]], wr=[SP["out"]])

    def xq_load(s):
        cb.dml(lambda e, s=s: e.dma_start(out=hhi[:, :],
                                          in_=xq_hi_d.ap()[s]),
               rd=[SP["xq"]], wr=[B["hhi"]])
        cb.dml(lambda e, s=s: e.dma_start(out=hlo[:, :],
                                          in_=xq_lo_d.ap()[s]),
               rd=[SP["xq"]], wr=[B["hlo"]])

    jobs = [(s, sti) for s in range(NS) for sti in range(NST4)]
    cb.v(lambda e: e.memset(kb[:, 0:1], 0.0), wr=[KB0, KB1, B["kb"]])
    for mat in range(3):
        emit_w4load_one(0, 0, mat)
    xq_load(0)
    bc4_p1(*jobs[0])
    bc4_p1q(*jobs[0])
    for mat in range(2):
        emit_w4load_one(*jobs[1], mat)
    bc4_p1(*jobs[1])
    bc4_p1q(*jobs[1])
    for idx, cur in enumerate(jobs):
        n1 = jobs[idx + 1] if idx + 1 < len(jobs) else None
        n2 = jobs[idx + 2] if idx + 2 < len(jobs) else None
        if n2 is not None:
            if n2[1] == 0:
                xq_load(n2[0])
            for mat in range(2):
                emit_w4load_one(*n2, mat)
            bc4_p1(*n2)
        if cur[1] == NST4 - 1:
            cb.dml(lambda e, s=cur[0]: e.dma_start(out=xb[:, :],
                                                   in_=x2_d.ap()[s]),
                   rd=[SP["x2"]], wr=[B["xb"]])
        bc4_p2(*cur)
        if n2 is not None:
            bc4_p1q(*n2)
        if n1 is not None:
            emit_w4load_one(*n1, 2)

    return _finish_queues(cb, nc)

def _finish_queues(cb, nc):
    fin = [(cb.sem[f"gpsimd_d{j}"], cb.cnt[f"gpsimd_d{j}"])
           for j in range(NPOOL) if cb.cnt[f"gpsimd_d{j}"]]

    def _final(e, fin=fin):
        for s, v in fin:
            e.wait_ge(s, v)
    cb.q["gpsimd"].append(_final)

    from contextlib import ExitStack
    with ExitStack() as ctx:
        block = ctx.enter_context(nc.Block())

        @block.sync
        def _(sync):
            for g in cb.q["sync"]:
                g(sync)

        @block.scalar
        def _(scalar):
            for g in cb.q["scalar"]:
                g(scalar)

        @block.vector
        def _(vector):
            for g in cb.q["vector"]:
                g(vector)

        @block.tensor
        def _(tensor):
            for g in cb.q["tensor"]:
                g(tensor)

        @block.gpsimd
        def _(gpsimd):
            for g in cb.q["gpsimd"]:
                g(gpsimd)

    return nc


def make_in_maps(inputs):
    x = np.ascontiguousarray(np.asarray(inputs["L0_attn_src"],
                                        dtype=np.float32))
    c = np.asarray(inputs["L0_c"], dtype=np.float32)
    ident = np.eye(128, dtype=np.float32)

    def fm(v, k):
        return np.ascontiguousarray(
            np.asarray(v, dtype=np.float32).reshape(k, 128).T)

    def cc(v):
        return np.ascontiguousarray(np.asarray(v, dtype=np.float32))

    common = {
        "wmod": cc(inputs["L0_wmod"]),
        "wq": cc(inputs["L0_attn_Wq"]), "wk": cc(inputs["L0_attn_Wk"]),
        "wv": cc(inputs["L0_attn_Wv"]), "wo": cc(inputs["L0_attn_Wo"]),
        "w1": cc(inputs["L0_attn_W1"]), "w2": cc(inputs["L0_attn_W2"]),
        "wg": cc(inputs["L0_ffn_w_gate"]), "wu": cc(inputs["L0_ffn_w_up"]),
        "wd": cc(inputs["L0_ffn_w_down"]),
        "bq_fm": fm(inputs["L0_attn_bq"], 8),
        "bk_row": cc(np.asarray(inputs["L0_attn_bk"]).reshape(1, D)),
        "bv_row": cc(np.asarray(inputs["L0_attn_bv"]).reshape(1, D)),
        "bo_fm": fm(inputs["L0_attn_bo"], 8),
        "b1_fm": fm(inputs["L0_attn_b1"], 32),
        "b2_fm": fm(inputs["L0_attn_b2"], 8),
        "g1_fm": fm(inputs["L0_attn_g1"], 8),
        "be1_fm": fm(inputs["L0_attn_be1"], 8),
        "g2_fm": fm(inputs["L0_attn_g2"], 8),
        "be2_fm": fm(inputs["L0_attn_be2"], 8),
        "g2_row": cc(np.asarray(inputs["L0_attn_g2"]).reshape(1, D)),
        "be2_row": cc(np.asarray(inputs["L0_attn_be2"]).reshape(1, D)),
        "ident": ident,
    }
    in_maps = []
    for core in range(NCORES):
        b = core // 4
        t0 = (core % 4) * T
        m = dict(common)
        m["xin"] = np.ascontiguousarray(x[b, t0:t0 + T])
        m["cfm"] = fm(c[b], 8)
        in_maps.append(m)
    return in_maps


def kernel(**inputs):
    nc = build_program()
    in_maps = make_in_maps(inputs)
    res = run_bass_kernel_spmd(nc, in_maps, core_ids=list(range(NCORES)))
    full = np.empty((2, 16384, D), dtype=np.float32)
    for core in range(NCORES):
        b = core // 4
        t0 = (core % 4) * T
        full[b, t0:t0 + T] = res.results[core]["out"]
    return full


# version 14
# speedup vs baseline: 1.0820x; 1.0068x over previous
import numpy as np
import concourse.bass as bass
import concourse.mybir as mybir
from concourse.bass_utils import run_bass_kernel_spmd

F32 = mybir.dt.float32
BF16 = mybir.dt.bfloat16
OP = mybir.AluOpType
AF = mybir.ActivationFunctionType
AX = mybir.AxisListType
AP = bass.AP

D = 1024
H = 16
FF = 4096
FI = 2816
NCORES = 8
T = 4096
SLAB = 512
NS = T // SLAB
NCT = SLAB // 128
KC = D // 128
EPS_LN = 1e-5
EPS_Z = 1e-6

ENGS = ("sync", "scalar", "vector", "tensor", "gpsimd")

C_BQ, C_BO, C_B2, C_G1, C_BE1, C_G2, C_B1 = 0, 8, 16, 24, 32, 40, 48
M_SA, M_CA, M_GA, M_SB, M_CB, M_GB, M_CA1, M_CB1 = 0, 8, 16, 24, 32, 40, 48, 56
R_BK, R_BV = 0, D
R_GA, R_GB, R_CB1, R_SB = 0, D, 2 * D, 3 * D

W_COLS = 8192
SL = NCT * D


class Buf:
    __slots__ = ("name", "wr", "rds")

    def __init__(self, name):
        self.name = name
        self.wr = None
        self.rds = []


NPOOL = 12


class CB:

    def __init__(self):
        self.nc = bass.Bass("TRN2", target_bir_lowering=False)
        self.q = {e: [] for e in ENGS}
        self.sem = {e: self.nc.alloc_semaphore(f"S{e}") for e in ENGS}
        for e in ("sync", "gpsimd"):
            for j in range(NPOOL):
                self.sem[f"{e}_d{j}"] = self.nc.alloc_semaphore(f"S{e}d{j}")
        self.sem["cc"] = self.nc.alloc_semaphore("Scc")
        self.cnt = {k: 0 for k in self.sem}
        self.waited = {e: {} for e in ENGS}
        self.pool_i = {"sync": 0, "gpsimd": 0}
        self.last_drain = {}

    def _hazard_waits(self, eng, reads, writes, extra=()):
        waits = list(extra)
        for b in reads:
            if b.wr is not None:
                waits.append(b.wr)
        for b in writes:
            if b.wr is not None:
                waits.append(b.wr)
            waits.extend(b.rds)
        wmap = {}
        for (key, val) in waits:
            if key == eng:
                continue
            if wmap.get(key, -1) < val:
                wmap[key] = val
        final = []
        for key, val in wmap.items():
            if self.waited[eng].get(key, -1) < val:
                self.waited[eng][key] = val
                final.append((self.sem[key], val))
        return final

    def _finish(self, eng, tok, reads, writes):
        for b in writes:
            b.wr = tok
            b.rds = []
        for b in reads:
            b.rds.append(tok)
        return tok

    def _emit(self, eng, f, reads, writes, inc, drain, semkey=None):
        semkey = semkey or eng
        need_drain = False
        if drain:
            last = self.last_drain.get(eng, 0)
            dep = 0
            for b in reads:
                if b.wr is not None and b.wr[0] == semkey:
                    dep = max(dep, b.wr[1])
            for b in writes:
                if b.wr is not None and b.wr[0] == semkey:
                    dep = max(dep, b.wr[1])
                for t in b.rds:
                    if t[0] == semkey:
                        dep = max(dep, t[1])
            if dep > last:
                need_drain = True
        final = self._hazard_waits(eng, reads, writes)
        sem = self.sem[semkey]

        def g(e, final=final, f=f, inc=inc, nd=need_drain, sem=sem):
            if nd:
                e.drain()
            for s, v in final:
                e.wait_ge(s, v)
            r = f(e)
            if inc:
                r.then_inc(sem, inc)

        self.q[eng].append(g)
        self.cnt[semkey] += inc
        if need_drain:
            self.last_drain[eng] = self.cnt[semkey] - inc
        return self._finish(eng, (semkey, self.cnt[semkey]), reads, writes)

    def _dma(self, eng, f, reads, writes):
        j = self.pool_i[eng]
        self.pool_i[eng] = (j + 1) % NPOOL
        key = f"{eng}_d{j}"
        guard = [(key, self.cnt[key])] if self.cnt[key] else []
        final = self._hazard_waits(eng, reads, writes, extra=guard)
        sem = self.sem[key]

        def g(e, final=final, f=f, sem=sem):
            for s, v in final:
                e.wait_ge(s, v)
            f(e).then_inc(sem, 16)

        self.q[eng].append(g)
        self.cnt[key] += 16
        return self._finish(eng, (key, self.cnt[key]), reads, writes)

    def v(self, f, rd=(), wr=()):
        return self._emit("vector", f, rd, wr, 1, True)

    def p(self, f, rd=(), wr=()):
        return self._emit("gpsimd", f, rd, wr, 1, True)

    def a(self, f, rd=(), wr=()):
        return self._emit("scalar", f, rd, wr, 1, True)

    def t(self, f, rd=(), wr=()):
        return self._emit("tensor", f, rd, wr, 1, False)

    def dml(self, f, rd=(), wr=()):
        return self._dma("sync", f, rd, wr)

    def dms(self, f, rd=(), wr=()):
        return self._dma("gpsimd", f, rd, wr)


def ap3(handle, offset, dims):
    return AP(handle, offset, [list(d) for d in dims])


def build_program(debug_names=(), skip_collective=False, bc2_stripes=8, stop_after=None):
    cb = CB()
    nc = cb.nc
    dbg = set(debug_names)

    def dram(name, shape, dt=F32):
        kind = "ExternalOutput" if name in dbg else "Internal"
        return nc.dram_tensor(name, list(shape), dt, kind=kind)

    xin = nc.dram_tensor("xin", [T, D], F32, kind="ExternalInput")
    cfm_in = nc.dram_tensor("cfm", [128, 8], F32, kind="ExternalInput")
    wmod_in = nc.dram_tensor("wmod", [D, 6 * D], F32, kind="ExternalInput")
    wq_in = nc.dram_tensor("wq", [D, D], F32, kind="ExternalInput")
    wk_in = nc.dram_tensor("wk", [D, D], F32, kind="ExternalInput")
    wv_in = nc.dram_tensor("wv", [D, D], F32, kind="ExternalInput")
    wo_in = nc.dram_tensor("wo", [D, D], F32, kind="ExternalInput")
    w1_in = nc.dram_tensor("w1", [D, FF], F32, kind="ExternalInput")
    w2_in = nc.dram_tensor("w2", [FF, D], F32, kind="ExternalInput")
    wg_in = nc.dram_tensor("wg", [FI, D], F32, kind="ExternalInput")
    wu_in = nc.dram_tensor("wu", [FI, D], F32, kind="ExternalInput")
    wd_in = nc.dram_tensor("wd", [D, FI], F32, kind="ExternalInput")
    bq_fm = nc.dram_tensor("bq_fm", [128, 8], F32, kind="ExternalInput")
    bk_row = nc.dram_tensor("bk_row", [1, D], F32, kind="ExternalInput")
    bv_row = nc.dram_tensor("bv_row", [1, D], F32, kind="ExternalInput")
    bo_fm = nc.dram_tensor("bo_fm", [128, 8], F32, kind="ExternalInput")
    b1_fm = nc.dram_tensor("b1_fm", [128, 32], F32, kind="ExternalInput")
    b2_fm = nc.dram_tensor("b2_fm", [128, 8], F32, kind="ExternalInput")
    g1_fm = nc.dram_tensor("g1_fm", [128, 8], F32, kind="ExternalInput")
    be1_fm = nc.dram_tensor("be1_fm", [128, 8], F32, kind="ExternalInput")
    g2_fm = nc.dram_tensor("g2_fm", [128, 8], F32, kind="ExternalInput")
    be2_fm = nc.dram_tensor("be2_fm", [128, 8], F32, kind="ExternalInput")
    g2_row = nc.dram_tensor("g2_row", [1, D], F32, kind="ExternalInput")
    be2_row = nc.dram_tensor("be2_row", [1, D], F32, kind="ExternalInput")
    ident_in = nc.dram_tensor("ident", [128, 128], F32, kind="ExternalInput")
    out = nc.dram_tensor("out", [T, D], F32, kind="ExternalOutput")

    h_hi_d = dram("h_hi_d", [NS, 128, SL], BF16)
    h_lo_d = dram("h_lo_d", [NS, 128, SL], BF16)
    q_d = dram("q_d", [NS, 128, SL], F32)
    x1_hi_d = dram("x1_hi_d", [NS, 128, SL], BF16)
    x1_lo_d = dram("x1_lo_d", [NS, 128, SL], BF16)
    xq_hi_d = dram("xq_hi_d", [NS, 128, SL], BF16)
    xq_lo_d = dram("xq_lo_d", [NS, 128, SL], BF16)
    ya_d = dram("ya_d", [NS, 128, SL], F32)
    yb_d = dram("yb_d", [NS, 128, SL], F32)
    x2_d = dram("x2_d", [NS, 128, SL], F32)
    y2a_d = dram("y2a_d", [NS, 128, SL], F32)
    y2b_d = dram("y2b_d", [NS, 128, SL], F32)
    garow_d = dram("garow_d", [4, D], F32)
    w4_d = dram("w4_d", [3, 6, 128, 4096], F32)
    w4b_d = dram("w4b_d", [36, 128, 4096], BF16)
    w12_d = dram("w12_d", [8, 2, 128, W_COLS], BF16)
    cc_in = dram("cc_in", [128, 8 * 65], F32)
    cc_out = dram("cc_out", [128, 8 * 65], F32)

    B = {}

    def sb(name, cols, dt=F32, parts=128):
        h = nc.alloc_sbuf_tensor(name, [parts, cols], dt)
        B[name] = Buf(name)
        return h

    ident = sb("ident_s", 128)
    onesr = sb("onesr", 128, parts=1)
    onescol = sb("onescol", 1)
    cfm = sb("cfm_s", 8)
    silc = sb("silc", 8)
    modv = sb("modv", 64)
    sgn8 = sb("sgn8", 16)
    vecs = sb("vecs", 80)
    be2s = sb("be2s", 8)
    r6c = sb("r6c", 2)
    m1b = sb("m1b", 32)
    r1b = sb("r1b", 32)
    m4b = sb("m4b", 4)
    r4b = sb("r4b", 4)
    minis = sb("minis", 32)
    zb = sb("zb", 64)
    kvexp = sb("kvexp", 8 * 130)
    rep = sb("rep", 4 * D)
    xb = sb("xb", SL)
    tb = sb("tb", SL)
    kb = sb("kb", SL)
    qb = sb("qb", SL)
    a2b = sb("a2b", SL)
    x1b = sb("x1b", SL)
    hhi = sb("hhi", SL, BF16)
    hlo = sb("hlo", SL, BF16)
    wa_hi = sb("wa_hi", W_COLS, BF16)
    wa_lo = sb("wa_lo", W_COLS, BF16)
    wb_hi = sb("wb_hi", W_COLS, BF16)
    wb_lo = sb("wb_lo", W_COLS, BF16)
    ub = sb("ub", SLAB)
    t1s = sb("t1s", SLAB)
    t2s = sb("t2s", SLAB)

    pb = [nc.alloc_psum_tensor(f"pbk{i}", [128, 512], F32) for i in range(8)]
    PB = [Buf(f"pb{i}") for i in range(8)]
    Bcc = Buf("cc")
    Bw4 = Buf("w4")
    WBH = [Buf(f"wbh{i}") for i in range(3)]
    WBL = [Buf(f"wbl{i}") for i in range(3)]
    WAH = [Buf(f"wah{i}") for i in range(2)]
    WAL = [Buf(f"wal{i}") for i in range(2)]
    Bw12 = [Buf(f"w12_{i}") for i in range(8)]
    Bw4b = [Buf(f"w4b_{i}") for i in range(18)]
    KB0 = Buf("kb0")
    KB1 = Buf("kb1")
    Bgarow = Buf("garow")
    SP = {k: Buf("sp_" + k) for k in
          ("h", "q", "x1", "xq", "ya", "yb", "x2", "y2a", "y2b", "out")}

    def vcol(i):
        return vecs[:, i:i + 1]

    cb.dml(lambda e: e.dma_start(out=ident[:, :], in_=ident_in.ap()[:, :]),
           wr=[B["ident_s"]])
    cb.dml(lambda e: e.dma_start(out=cfm[:, :], in_=cfm_in.ap()[:, :]),
           wr=[B["cfm_s"]])
    for i, src in enumerate((bq_fm, bo_fm, b2_fm, g1_fm, be1_fm, g2_fm)):
        cb.dml(lambda e, i=i, src=src: e.dma_start(
            out=vecs[:, i * 8:(i + 1) * 8], in_=src.ap()[:, :]),
            wr=[B["vecs"]])
    cb.dml(lambda e: e.dma_start(out=vecs[:, 48:80], in_=b1_fm.ap()[:, :]),
           wr=[B["vecs"]])
    cb.dml(lambda e: e.dma_start(out=be2s[:, :], in_=be2_fm.ap()[:, :]),
           wr=[B["be2s"]])
    cb.v(lambda e: e.memset(r6c[:, 0:1], 6.0), wr=[B["r6c"]])
    cb.v(lambda e: e.reciprocal(r6c[:, 0:1], r6c[:, 0:1]), rd=[B["r6c"]],
         wr=[B["r6c"]])
    cb.v(lambda e: e.memset(r6c[:, 1:2], EPS_LN), wr=[B["r6c"]])
    cb.v(lambda e: e.memset(onesr[:, :], 1.0), wr=[B["onesr"]])
    cb.v(lambda e: e.memset(onescol[:, :], 1.0), wr=[B["onescol"]])

    cb.a(lambda e: e.activation(silc[:, :], cfm[:, :], AF.Silu),
         rd=[B["cfm_s"]], wr=[B["silc"]])
    for g in range(12):
        def ldw(e, g=g):
            o = ap3(xb, 0, [[SL, 128], [512, 8], [1, 512]])
            i = ap3(wmod_in, g * 512,
                    [[6 * D, 128], [128 * 6 * D, 8], [1, 512]])
            return e.dma_start(out=o, in_=i)
        cb.dml(ldw, wr=[B["xb"]])

        def mm(e, g=g):
            r = None
            for mc in range(4):
                for kc in range(KC):
                    r = e.matmul(
                        pb[0][:, g * 4 + mc:g * 4 + mc + 1],
                        xb[:, kc * 512 + mc * 128:kc * 512 + (mc + 1) * 128],
                        silc[:, kc:kc + 1],
                        start=(kc == 0), stop=(kc == KC - 1))
            return r
        cb.t(mm, rd=[B["xb"], B["silc"]], wr=[PB[0]])
    cb.v(lambda e: e.tensor_copy(modv[:, 0:48], pb[0][:, 0:48]),
         rd=[PB[0]], wr=[B["modv"]])
    cb.v(lambda e: e.tensor_scalar_add(modv[:, M_CA1:M_CA1 + 8],
                                       modv[:, M_CA:M_CA + 8], 1.0),
         rd=[B["modv"]], wr=[B["modv"]])
    cb.v(lambda e: e.tensor_scalar_add(modv[:, M_CB1:M_CB1 + 8],
                                       modv[:, M_CB:M_CB + 8], 1.0),
         rd=[B["modv"]], wr=[B["modv"]])
    for j, col in enumerate((M_GA, M_GB)):
        cb.v(lambda e, j=j, col=col: e.tensor_scalar(
            sgn8[:, j * 8:(j + 1) * 8], modv[:, col:col + 8], 0.0, 2.0,
            OP.is_ge, OP.mult), rd=[B["modv"]], wr=[B["sgn8"]])
        cb.v(lambda e, j=j: e.tensor_scalar_sub(
            sgn8[:, j * 8:(j + 1) * 8], sgn8[:, j * 8:(j + 1) * 8], 1.0),
            rd=[B["sgn8"]], wr=[B["sgn8"]])
        cb.a(lambda e, col=col: e.activation(
            modv[:, col:col + 8], modv[:, col:col + 8], AF.Abs),
            rd=[B["modv"]], wr=[B["modv"]])
        cb.v(lambda e, col=col: e.tensor_scalar_max(
            modv[:, col:col + 8], modv[:, col:col + 8], 0.25),
            rd=[B["modv"]], wr=[B["modv"]])
        cb.v(lambda e, j=j, col=col: e.tensor_mul(
            modv[:, col:col + 8], modv[:, col:col + 8],
            sgn8[:, j * 8:(j + 1) * 8]),
            rd=[B["modv"], B["sgn8"]], wr=[B["modv"]])

    for j, col in enumerate((M_GA, M_GB, M_CB1, M_SB)):
        def st(e, j=j, col=col):
            o = ap3(garow_d, j * D, [[1, 128], [128, 8]])
            return e.dma_start(out=o, in_=modv[:, col:col + 8],
                               allow_slow_non_contiguous=True)
        cb.dms(st, rd=[B["modv"]], wr=[Bgarow])

    def replicate(dst_col, src_dram, src_off):
        def f(e):
            o = ap3(rep, dst_col, [[4 * D, 128], [1, D]])
            i = ap3(src_dram, src_off, [[0, 128], [1, D]])
            return e.dma_start(out=o, in_=i)
        cb.dml(f, rd=[Bgarow], wr=[B["rep"]])

    replicate(R_BK, bk_row, 0)
    replicate(R_BV, bv_row, 0)

    def _bl(b):
        return list(b) if isinstance(b, (list, tuple)) else [b]

    def load_split(w_dram, wcols, row0, col0, nrows, ncols, hi, lo, hib, lob,
                   dcol0=0):
        hib, lob = _bl(hib), _bl(lob)
        nkc = nrows // 128

        def cast(e):
            o = ap3(hi, dcol0, [[W_COLS, 128], [ncols, nkc], [1, ncols]])
            i = ap3(w_dram, row0 * wcols + col0,
                    [[wcols, 128], [128 * wcols, nkc], [1, ncols]])
            return e.dma_start(out=o, in_=i)
        cb.dms(cast, wr=hib)
        hkc = max(1, nkc // 2)
        nhalf = (nkc + hkc - 1) // hkc
        for hh in range(nhalf):
            def full(e, hh=hh):
                o = ap3(xb, 0, [[SL, 128], [ncols, hkc], [1, ncols]])
                i = ap3(w_dram, (row0 + hh * hkc * 128) * wcols + col0,
                        [[wcols, 128], [128 * wcols, hkc], [1, ncols]])
                return e.dma_start(out=o, in_=i)
            cb.dml(full, wr=[B["xb"]])
            cb.v(lambda e, hh=hh: e.tensor_sub(
                ap3(lo, dcol0 + hh * hkc * ncols,
                    [[W_COLS, 128], [1, hkc * ncols]]),
                ap3(xb, 0, [[SL, 128], [1, hkc * ncols]]),
                ap3(hi, dcol0 + hh * hkc * ncols,
                    [[W_COLS, 128], [1, hkc * ncols]])),
                rd=[B["xb"]] + hib, wr=lob)

    load_split(wq_in, D, 0, 0, D, 1024, wa_hi, wa_lo, B["wa_hi"], B["wa_lo"])

    cb.v(lambda e: e.memset(qb[:, 0:520], 0.0), wr=[B["qb"]])
    cb.v(lambda e: e.memset(kvexp[:, :], 0.0), wr=[B["kvexp"]])

    def ln_stats_tm(src, src_b, sqout, sqout_b, mdst, mdst_b, rdst, rdst_b,
                    col0, eps):
        for c in range(NCT):
            cb.v(lambda e, c=c: e.tensor_reduce(
                minis[:, c:c + 1], src[:, c * D:(c + 1) * D], AX.X, OP.add),
                rd=[src_b], wr=[B["minis"]])
            cb.a(lambda e, c=c: e.activation(
                sqout[:, c * D:(c + 1) * D], src[:, c * D:(c + 1) * D],
                AF.Square, accum_out=minis[:, 4 + c:5 + c]),
                rd=[src_b], wr=[sqout_b, B["minis"]])
        cb.v(lambda e: e.tensor_scalar_mul(
            mdst[:, col0:col0 + NCT], minis[:, 0:NCT], 1.0 / D),
            rd=[B["minis"]], wr=[mdst_b])
        cb.v(lambda e: e.tensor_mul(
            minis[:, 8:8 + NCT], mdst[:, col0:col0 + NCT],
            mdst[:, col0:col0 + NCT]), rd=[mdst_b], wr=[B["minis"]])
        cb.v(lambda e: e.scalar_tensor_tensor(
            minis[:, 12:12 + NCT], minis[:, 4:4 + NCT], 1.0 / D,
            minis[:, 8:8 + NCT], OP.mult, OP.subtract),
            rd=[B["minis"]], wr=[B["minis"]])
        cb.a(lambda e: e.activation(
            minis[:, 16:16 + NCT], minis[:, 12:12 + NCT], AF.Sqrt,
            bias=r6c[:, 1:2]),
            rd=[B["minis"], B["r6c"]], wr=[B["minis"]])
        cb.v(lambda e: e.reciprocal(
            rdst[:, col0:col0 + NCT], minis[:, 16:16 + NCT]),
            rd=[B["minis"]], wr=[rdst_b])

    def ln_apply_tm(src, src_b, dst, dst_b, mdst, mdst_b, rdst, rdst_b, col0,
                    rw):
        for c in range(NCT):
            def f(e, c=c):
                r_b = ap3(rdst, col0 + c, [[rw, 128], [0, D]])
                return e.scalar_tensor_tensor(
                    dst[:, c * D:(c + 1) * D], src[:, c * D:(c + 1) * D],
                    mdst[:, col0 + c:col0 + c + 1], r_b, OP.subtract, OP.mult)
            cb.v(f, rd=[src_b, mdst_b, rdst_b], wr=[dst_b])

    def tr_tm_to_fm(src, src_b, dsthi, dsthi_b, dstlo, dstlo_b, banks,
                    affine=None, aff_tmp=None, aff_tmp_b=None, dcol0=0,
                    extra_wr_hi=(), extra_wr_lo=()):
        for f in range(KC):
            bk = banks[f % 2]
            pbk = pb[bk]

            def tr(e, f=f, pbk=pbk):
                r = None
                for c in range(NCT):
                    r = e.transpose(
                        pbk[:, c * 128:(c + 1) * 128],
                        src[:, c * D + f * 128:c * D + (f + 1) * 128],
                        ident[:, :])
                return r
            cb.t(tr, rd=[src_b, B["ident_s"]], wr=[PB[bk]])
            if affine is not None:
                acol, scol = affine

                def ev(e, f=f, pbk=pbk, acol=acol, scol=scol):
                    sa_b = ap3(modv, scol + f, [[64, 128], [0, SLAB]])
                    return e.scalar_tensor_tensor(
                        aff_tmp[:, f * SLAB:(f + 1) * SLAB], pbk[:, 0:SLAB],
                        modv[:, acol + f:acol + f + 1], sa_b, OP.mult, OP.add)
                cb.v(ev, rd=[PB[bk], B["modv"]], wr=[aff_tmp_b])
                cb.a(lambda e, f=f: e.activation(
                    dsthi[:, dcol0 + f * SLAB:dcol0 + (f + 1) * SLAB],
                    aff_tmp[:, f * SLAB:(f + 1) * SLAB], AF.Identity),
                    rd=[aff_tmp_b], wr=[dsthi_b])
                cb.v(lambda e, f=f: e.tensor_sub(
                    dstlo[:, dcol0 + f * SLAB:dcol0 + (f + 1) * SLAB],
                    aff_tmp[:, f * SLAB:(f + 1) * SLAB],
                    dsthi[:, dcol0 + f * SLAB:dcol0 + (f + 1) * SLAB]),
                    rd=[aff_tmp_b, dsthi_b], wr=[dstlo_b])
            else:
                cb.a(lambda e, f=f, pbk=pbk: e.activation(
                    dsthi[:, dcol0 + f * SLAB:dcol0 + (f + 1) * SLAB],
                    pbk[:, 0:SLAB],
                    AF.Identity), rd=[PB[bk]],
                    wr=[dsthi_b] + list(extra_wr_hi))
                cb.v(lambda e, f=f, pbk=pbk: e.tensor_sub(
                    dstlo[:, dcol0 + f * SLAB:dcol0 + (f + 1) * SLAB],
                    pbk[:, 0:SLAB],
                    dsthi[:, dcol0 + f * SLAB:dcol0 + (f + 1) * SLAB]),
                    rd=[PB[bk], dsthi_b], wr=[dstlo_b] + list(extra_wr_lo))

    def mm3_fm(bank, lhs_hi, lhs_lo, lhsb, rhs_hi, rhs_lo, rhsb, fcol, wcol0,
               wcols, nkc=KC, rcol0=0, rstep=SLAB):
        pbk = pb[bank]

        def mm(e):
            r = None
            n = 0
            for (wsrc, xsrc) in ((lhs_hi, rhs_hi), (lhs_hi, rhs_lo),
                                 (lhs_lo, rhs_hi)):
                for kc in range(nkc):
                    r = e.matmul(
                        pbk[:, 0:SLAB],
                        wsrc[:, wcol0 + kc * wcols + fcol * 128:
                             wcol0 + kc * wcols + (fcol + 1) * 128],
                        xsrc[:, rcol0 + kc * rstep:rcol0 + kc * rstep + SLAB],
                        start=(n == 0), stop=(n == 3 * nkc - 1))
                    n += 1
            return r
        rds = []
        for b in (*lhsb, *rhsb):
            rds.extend(_bl(b))
        return cb.t(mm, rd=rds, wr=[PB[bank]])

    def fm_stats(srcbuf, src_b, tmp, tmp_b, eps):
        def f1(e):
            r = None
            for kc in range(KC):
                r = e.matmul(pb[7][0:1, 0:SLAB], onescol[:, :],
                             srcbuf[:, kc * SLAB:(kc + 1) * SLAB],
                             start=(kc == 0), stop=(kc == KC - 1))
            return r
        cb.t(f1, rd=[src_b, B["onescol"]], wr=[PB[7]])
        cb.a(lambda e: e.activation(tmp[:, :], srcbuf[:, :], AF.Square),
             rd=[src_b], wr=[tmp_b])

        def f2(e):
            r = None
            for kc in range(KC):
                r = e.matmul(pb[6][0:1, 0:SLAB], onescol[:, :],
                             tmp[:, kc * SLAB:(kc + 1) * SLAB],
                             start=(kc == 0), stop=(kc == KC - 1))
            return r
        cb.t(f2, rd=[tmp_b, B["onescol"]], wr=[PB[6]])
        cb.v(lambda e: e.tensor_scalar_mul(ub[0:1, 0:SLAB],
                                           pb[7][0:1, 0:SLAB], 1.0 / D),
             rd=[PB[7]], wr=[B["ub"]])
        cb.v(lambda e: e.tensor_mul(t1s[0:1, 0:SLAB], ub[0:1, 0:SLAB],
                                    ub[0:1, 0:SLAB]),
             rd=[B["ub"]], wr=[B["t1s"]])
        cb.v(lambda e: e.scalar_tensor_tensor(
            t2s[0:1, 0:SLAB], pb[6][0:1, 0:SLAB], 1.0 / D, t1s[0:1, 0:SLAB],
            OP.mult, OP.subtract), rd=[PB[6], B["t1s"]], wr=[B["t2s"]])
        cb.a(lambda e: e.activation(t1s[0:1, 0:SLAB], t2s[0:1, 0:SLAB],
                                    AF.Sqrt, bias=r6c[0:1, 1:2]),
             rd=[B["t2s"], B["r6c"]], wr=[B["t1s"]])
        cb.v(lambda e: e.reciprocal(t2s[0:1, 0:SLAB], t1s[0:1, 0:SLAB]),
             rd=[B["t1s"]], wr=[B["t2s"]])
        cb.v(lambda e: e.tensor_mul(t1s[0:1, 0:SLAB], ub[0:1, 0:SLAB],
                                    t2s[0:1, 0:SLAB]),
             rd=[B["ub"], B["t2s"]], wr=[B["t1s"]])

    def fm_apply(srcbuf, src_b, gcol, bvec, bvec_b, bvw, bcol0, dst, dst_b,
                 tmp, tmp_b):
        cb.t(lambda e: e.matmul(pb[7][:, 0:SLAB], onesr[:, :],
                                t2s[0:1, 0:SLAB], start=True, stop=True),
             rd=[B["t2s"], B["onesr"]], wr=[PB[7]])
        cb.t(lambda e: e.matmul(pb[6][:, 0:SLAB], onesr[:, :],
                                t1s[0:1, 0:SLAB], start=True, stop=True),
             rd=[B["t1s"], B["onesr"]], wr=[PB[6]])
        for f in range(KC):
            cb.v(lambda e, f=f: e.tensor_mul(
                tmp[:, f * SLAB:(f + 1) * SLAB],
                srcbuf[:, f * SLAB:(f + 1) * SLAB], pb[7][:, 0:SLAB]),
                rd=[src_b, PB[7]], wr=[tmp_b])
            cb.v(lambda e, f=f: e.tensor_sub(
                tmp[:, f * SLAB:(f + 1) * SLAB],
                tmp[:, f * SLAB:(f + 1) * SLAB], pb[6][:, 0:SLAB]),
                rd=[tmp_b, PB[6]], wr=[tmp_b])

            def f3(e, f=f):
                b_b = ap3(bvec, bcol0 + f, [[bvw, 128], [0, SLAB]])
                return e.scalar_tensor_tensor(
                    dst[:, f * SLAB:(f + 1) * SLAB],
                    tmp[:, f * SLAB:(f + 1) * SLAB],
                    vcol(gcol + f), b_b, OP.mult, OP.add)
            cb.v(f3, rd=[tmp_b, B["vecs"], bvec_b], wr=[dst_b])

    def fm_ln(srcbuf, src_b, gcol, bvec, bvec_b, bvw, bcol0, dst, dst_b, eps,
              tmp, tmp_b):
        fm_stats(srcbuf, src_b, tmp, tmp_b, eps)
        fm_apply(srcbuf, src_b, gcol, bvec, bvec_b, bvw, bcol0, dst, dst_b,
                 tmp, tmp_b)

    def quant_tm(src, src_b, sw, dst, dst_b, dw, col0, F,
                 s1, s1b, s1w, c1, s2, s2b, s2w, c2, s3, s3b, s3w, c3,
                 sc_off=0, pool=False):
        V = cb.p if pool else cb.v
        NB = F // 16
        sA = ap3(src, col0, [[sw, 128], [16, NB], [1, 16]])
        cb.v(lambda e: e.tensor_reduce(ub[:, sc_off:sc_off + NB], sA, AX.X,
                                       OP.max, apply_absolute_value=True),
             rd=[src_b], wr=[B["ub"]])
        cb.v(lambda e: e.tensor_scalar(ub[:, 256 + sc_off:256 + sc_off + NB],
                                       ub[:, sc_off:sc_off + NB],
                                       r6c[:, 0:1], 1e-12, OP.mult, OP.max),
             rd=[B["ub"], B["r6c"]], wr=[B["ub"]])
        cb.v(lambda e: e.reciprocal(ub[:, sc_off:sc_off + NB],
                                    ub[:, 256 + sc_off:256 + sc_off + NB]),
             rd=[B["ub"]], wr=[B["ub"]])
        vf = ap3(s1, c1, [[s1w, 128], [1, F]])
        v3 = ap3(s1, c1, [[s1w, 128], [16, NB], [1, 16]])
        if pool:
            V(lambda e: e.tensor_single_scalar(
                vf, ap3(src, col0, [[sw, 128], [1, F]]), 0.0, OP.abs_max),
              rd=[src_b], wr=[s1b])
        else:
            cb.a(lambda e: e.activation(
                vf, ap3(src, col0, [[sw, 128], [1, F]]), AF.Abs),
                rd=[src_b], wr=[s1b])
        rsb = ap3(ub, sc_off, [[SLAB, 128], [1, NB], [0, 16]])
        V(lambda e: e.tensor_tensor(v3, v3, rsb, OP.mult),
          rd=[s1b, B["ub"]], wr=[s1b])
        tf = ap3(s2, c2, [[s2w, 128], [1, F]])
        kf = ap3(s3, c3, [[s3w, 128], [1, F]])
        V(lambda e: e.tensor_single_scalar(tf, vf, 0.25, OP.is_gt),
          rd=[s1b], wr=[s2b])
        for thr in (0.75, 1.25, 1.75):
            V(lambda e, thr=thr: e.scalar_tensor_tensor(
                tf, vf, thr, tf, OP.is_gt, OP.add), rd=[s1b, s2b], wr=[s2b])
        V(lambda e: e.tensor_single_scalar(kf, vf, 2.5, OP.is_gt),
          rd=[s1b], wr=[s3b])
        for thr in (3.5, 5.0, 5.0):
            V(lambda e, thr=thr: e.scalar_tensor_tensor(
                kf, vf, thr, kf, OP.is_gt, OP.add), rd=[s1b, s3b], wr=[s3b])
        V(lambda e: e.scalar_tensor_tensor(tf, tf, 0.5, kf, OP.mult, OP.add),
          rd=[s2b, s3b], wr=[s2b])
        if pool:
            V(lambda e: e.tensor_single_scalar(
                kf, ap3(src, col0, [[sw, 128], [1, F]]), 0.0, OP.is_lt),
              rd=[src_b], wr=[s3b])
            V(lambda e: e.tensor_single_scalar(
                vf, ap3(src, col0, [[sw, 128], [1, F]]), 0.0, OP.is_gt),
              rd=[src_b], wr=[s1b])
            V(lambda e: e.tensor_tensor(vf, vf, kf, OP.subtract),
              rd=[s1b, s3b], wr=[s1b])
        else:
            cb.a(lambda e: e.activation(
                vf, ap3(src, col0, [[sw, 128], [1, F]]), AF.Sign),
                rd=[src_b], wr=[s1b])
        V(lambda e: e.tensor_tensor(tf, tf, vf, OP.mult),
          rd=[s2b, s1b], wr=[s2b])
        t3 = ap3(s2, c2, [[s2w, 128], [16, NB], [1, 16]])
        d3 = ap3(dst, col0, [[dw, 128], [16, NB], [1, 16]])
        ssb = ap3(ub, 256 + sc_off, [[SLAB, 128], [1, NB], [0, 16]])
        V(lambda e: e.tensor_tensor(d3, t3, ssb, OP.mult),
          rd=[s2b, B["ub"]], wr=[dst_b])

    def prep_fp4_weight(w_dram, wcols, i0, isz, along_rows, mat, stripe,
                        thunks):
        if along_rows:
            nrc, ncols = isz // 128, wcols
            off = i0 * wcols
        else:
            nrc, ncols = D // 128, isz
            off = i0

        def ld(e):
            o = ap3(x1b, 0, [[SL, 128], [ncols, nrc], [1, ncols]])
            i = ap3(w_dram, off, [[wcols, 128], [128 * wcols, nrc],
                                  [1, ncols]])
            return e.dma_start(out=o, in_=i)
        thunks.append(lambda: cb.dml(ld, wr=[B["x1b"]]))
        total = nrc * ncols
        for p0 in range(0, total, 1024):
            pf = min(1024, total - p0)
            thunks.append(lambda p0=p0, pf=pf: quant_tm(
                x1b, B["x1b"], SL, x1b, B["x1b"], SL, p0, pf,
                a2b, B["a2b"], SL, 0, a2b, B["a2b"], SL, 1024,
                a2b, B["a2b"], SL, 2048, sc_off=64))
        nkc = ncols // 128
        for kc in range(nkc):
            for grp in range((nrc + 3) // 4):
                bank = 5 + grp % 2
                pbk = pb[bank]
                cnt = min(4, nrc - grp * 4)

                def tr(e, kc=kc, grp=grp, cnt=cnt, pbk=pbk):
                    r = None
                    for j in range(cnt):
                        rc = grp * 4 + j
                        r = e.transpose(
                            pbk[:, j * 128:(j + 1) * 128],
                            x1b[:, rc * ncols + kc * 128:
                                rc * ncols + (kc + 1) * 128],
                            ident[:, :])
                    return r

                def group(tr=tr, bank=bank, pbk=pbk, cnt=cnt,
                          kc=kc, grp=grp, mat=mat, stripe=stripe, nrc=nrc):
                    cb.t(tr, rd=[B["x1b"], B["ident_s"]], wr=[PB[bank]])
                    cb.v(lambda e, cnt=cnt, pbk=pbk: e.tensor_copy(
                        a2b[:, 3072:3072 + cnt * 128], pbk[:, 0:cnt * 128]),
                        rd=[PB[bank]], wr=[B["a2b"]])
                    dcol = kc * (nrc * 128) + grp * 512

                    def stw(e, mat=mat, stripe=stripe, dcol=dcol, cnt=cnt):
                        o = ap3(w4_d, (mat * 6 + stripe) * 128 * 4096 + dcol,
                                [[4096, 128], [1, cnt * 128]])
                        return e.dma_start(out=o,
                                           in_=a2b[:, 3072:3072 + cnt * 128])
                    cb.dms(stw, rd=[B["a2b"]], wr=[Bw4])
                thunks.append(group)

    STRIPES4 = [(i, min(512, FI - i)) for i in range(0, FI, 512)]
    prep_thunks = []
    for st4, (i0_, isz_) in enumerate(STRIPES4):
        prep_fp4_weight(wg_in, D, i0_, isz_, True, 0, st4, prep_thunks)
        prep_fp4_weight(wu_in, D, i0_, isz_, True, 1, st4, prep_thunks)
        prep_fp4_weight(wd_in, FI, i0_, isz_, False, 2, st4, prep_thunks)

    def drain_prep(n):
        for _ in range(n):
            if prep_thunks:
                prep_thunks.pop(0)()

    for s in range(NS):
        def ldx(e, s=s):
            o = ap3(xb, 0, [[SL, 128], [D, NCT], [1, D]])
            i = ap3(xin, s * SLAB * D, [[D, 128], [128 * D, NCT], [1, D]])
            return e.dma_start(out=o, in_=i)
        cb.dml(ldx, wr=[B["xb"]])
        ln_stats_tm(xb, B["xb"], tb, B["tb"], m1b, B["m1b"], r1b, B["r1b"],
                    s * NCT, EPS_LN)
        ln_apply_tm(xb, B["xb"], tb, B["tb"], m1b, B["m1b"], r1b, B["r1b"],
                    s * NCT, 32)
        tr_tm_to_fm(tb, B["tb"], hhi, B["hhi"], hlo, B["hlo"], (1, 2),
                    affine=(M_CA1, M_SA), aff_tmp=kb, aff_tmp_b=B["kb"])
        cb.dms(lambda e, s=s: e.dma_start(out=h_hi_d.ap()[s], in_=hhi[:, :]),
               rd=[B["hhi"]], wr=[SP["h"]])
        cb.dms(lambda e, s=s: e.dma_start(out=h_lo_d.ap()[s], in_=hlo[:, :]),
               rd=[B["hlo"]], wr=[SP["h"]])
        for f in range(KC):
            bank = 3 + f % 2
            mm3_fm(bank, wa_hi, wa_lo, (B["wa_hi"], B["wa_lo"]),
                   hhi, hlo, (B["hhi"], B["hlo"]), f, 0, 1024)
            cb.a(lambda e, f=f, bank=bank: e.activation(
                ub[:, :], pb[bank][:, 0:SLAB], AF.Identity,
                bias=vcol(C_BQ + f)), rd=[PB[bank], B["vecs"]], wr=[B["ub"]])
            cb.v(lambda e: e.tensor_scalar_min(t1s[:, :], ub[:, :], 0.0),
                 rd=[B["ub"]], wr=[B["t1s"]])
            cb.a(lambda e: e.activation(t2s[:, :], t1s[:, :], AF.Exp),
                 rd=[B["t1s"]], wr=[B["t2s"]])
            cb.v(lambda e, f=f: e.scalar_tensor_tensor(
                qb[:, f * SLAB:(f + 1) * SLAB], ub[:, :], 0.0, t2s[:, :],
                OP.max, OP.add), rd=[B["ub"], B["t2s"]], wr=[B["qb"]])
        cb.dms(lambda e, s=s: e.dma_start(out=q_d.ap()[s], in_=qb[:, :]),
               rd=[B["qb"]], wr=[SP["q"]])
        drain_prep(20)

    if stop_after == "a1":
        return _finish_queues(cb, nc)
    load_split(wk_in, D, 0, 0, D, 1024, wa_hi, wa_lo, B["wa_hi"], B["wa_lo"])
    load_split(wv_in, D, 0, 0, D, 1024, wb_hi, wb_lo, WBH[0], WBL[0])
    for s in range(NS):
        cb.dml(lambda e, s=s: e.dma_start(out=hhi[:, :], in_=h_hi_d.ap()[s]),
               rd=[SP["h"]], wr=[B["hhi"]])
        cb.dml(lambda e, s=s: e.dma_start(out=hlo[:, :], in_=h_lo_d.ap()[s]),
               rd=[SP["h"]], wr=[B["hlo"]])
        for proj in ("k", "v"):
            whi_, wlo_ = (wa_hi, wa_lo) if proj == "k" else (wb_hi, wb_lo)
            whib = B["wa_hi"] if proj == "k" else WBH[0]
            wlob = B["wa_lo"] if proj == "k" else WBL[0]
            for c in range(NCT):
                for nh in range(2):
                    bank = 3 + (c * 2 + nh) % 2
                    pbk = pb[bank]

                    def mm(e, c=c, nh=nh, pbk=pbk, whi_=whi_, wlo_=wlo_):
                        r = None
                        n = 0
                        for (wsrc, xsrc) in ((whi_, hhi), (wlo_, hhi),
                                             (whi_, hlo)):
                            for kc in range(KC):
                                r = e.matmul(
                                    pbk[:, 0:512],
                                    xsrc[:, kc * SLAB + c * 128:
                                         kc * SLAB + (c + 1) * 128],
                                    wsrc[:, kc * 1024 + nh * 512:
                                         kc * 1024 + (nh + 1) * 512],
                                    start=(n == 0), stop=(n == 3 * KC - 1))
                                n += 1
                        return r
                    cb.t(mm, rd=[whib, wlob, B["hhi"], B["hlo"]],
                         wr=[PB[bank]])
                    if proj == "k":
                        cb.v(lambda e, nh=nh, pbk=pbk: e.tensor_add(
                            ub[:, :], pbk[:, 0:512],
                            rep[:, R_BK + nh * 512:R_BK + (nh + 1) * 512]),
                            rd=[PB[bank], B["rep"]], wr=[B["ub"]])
                        cb.v(lambda e: e.tensor_scalar_min(
                            t1s[:, :], ub[:, :], 0.0),
                            rd=[B["ub"]], wr=[B["t1s"]])
                        cb.a(lambda e: e.activation(t2s[:, :], t1s[:, :],
                                                    AF.Exp),
                             rd=[B["t1s"]], wr=[B["t2s"]])
                        cb.v(lambda e, c=c, nh=nh: e.scalar_tensor_tensor(
                            kb[:, c * D + nh * 512:c * D + (nh + 1) * 512],
                            ub[:, :], 0.0, t2s[:, :], OP.max, OP.add),
                            rd=[B["ub"], B["t2s"]], wr=[B["kb"]])
                    else:
                        cb.v(lambda e, c=c, nh=nh, pbk=pbk: e.tensor_add(
                            tb[:, c * D + nh * 512:c * D + (nh + 1) * 512],
                            pbk[:, 0:512],
                            rep[:, R_BV + nh * 512:R_BV + (nh + 1) * 512]),
                            rd=[PB[bank], B["rep"]], wr=[B["tb"]])
        for c in range(NCT):
            bank = 5 + c % 2
            pbk = pb[bank]
            p7off = (c % 2) * 65

            def mmkv(e, c=c, pbk=pbk, p7off=p7off):
                r = None
                for hh in range(H):
                    hp, half = hh // 2, hh % 2
                    dst = (pbk[half * 64:(half + 1) * 64,
                               hp * 65:hp * 65 + 64] if hp < 7 else
                           pb[7][half * 64:(half + 1) * 64,
                                 p7off:p7off + 64])
                    r = e.matmul(
                        dst,
                        kb[:, c * D + hh * 64:c * D + (hh + 1) * 64],
                        tb[:, c * D + hh * 64:c * D + (hh + 1) * 64],
                        start=True, stop=True)
                for hh in range(H):
                    hp, half = hh // 2, hh % 2
                    dst = (pbk[half * 64:(half + 1) * 64,
                               hp * 65 + 64:hp * 65 + 65] if hp < 7 else
                           pb[7][half * 64:(half + 1) * 64,
                                 p7off + 64:p7off + 65])
                    r = e.matmul(
                        dst,
                        kb[:, c * D + hh * 64:c * D + (hh + 1) * 64],
                        onescol[:, :],
                        start=True, stop=True)
                return r
            cb.t(mmkv, rd=[B["kb"], B["tb"], B["onescol"]],
                 wr=[PB[bank], PB[7]])
            cb.v(lambda e, pbk=pbk: e.tensor_add(
                qb[:, 0:455], qb[:, 0:455], pbk[:, 0:455]),
                rd=[PB[bank], B["qb"]], wr=[B["qb"]])
            cb.v(lambda e, p7off=p7off: e.tensor_add(
                qb[:, 455:520], qb[:, 455:520],
                pb[7][:, p7off:p7off + 65]),
                rd=[PB[7], B["qb"]], wr=[B["qb"]])
        drain_prep(4)

    if stop_after == "a2":
        return _finish_queues(cb, nc)
    drain_prep(len(prep_thunks))

    cb.dms(lambda e: e.dma_start(out=cc_in.ap()[:, :], in_=qb[:, 0:520]),
           rd=[B["qb"]], wr=[Bcc])

    if skip_collective:
        cb.dms(lambda e: e.dma_start(out=cc_out.ap()[:, :],
                                     in_=cc_in.ap()[:, :]),
               rd=[Bcc], wr=[Bcc])
    else:
        def ccf(e):
            return e.collective_compute(
                "AllReduce", OP.add,
                replica_groups=[[0, 1, 2, 3], [4, 5, 6, 7]],
                ins=[cc_in.ap().opt()], outs=[cc_out.ap().opt()])
        cb._emit("gpsimd", ccf, [Bcc], [Bcc], 1, False, semkey="cc")
    cb.dml(lambda e: e.dma_start(out=qb[:, 0:520], in_=cc_out.ap()[:, :]),
           rd=[Bcc], wr=[B["qb"]])

    def expand_kv(half):
        def f(e):
            o = ap3(kvexp, half * (64 * 8 * 130 + 65),
                    [[8 * 130, 64], [130, 8], [1, 65]])
            i = ap3(qb, half * 64 * SL, [[SL, 64], [65, 8], [1, 65]])
            return e.dma_start(out=o, in_=i)
        cb.dms(f, rd=[B["qb"]], wr=[B["kvexp"]])
    expand_kv(0)
    expand_kv(1)

    for j, rc in enumerate((R_GA, R_GB, R_CB1, R_SB)):
        replicate(rc, garow_d, j * D)

    if stop_after == "cc":
        return _finish_queues(cb, nc)
    load_split(wo_in, D, 0, 0, D, 1024, wa_hi, wa_lo, B["wa_hi"], B["wa_lo"])
    x1hi = wb_hi
    x1lo = wb_lo

    def bc1_A(s, dst, dst_b):
        cb.dml(lambda e, s=s: e.dma_start(out=qb[:, :], in_=q_d.ap()[s]),
               rd=[SP["q"]], wr=[B["qb"]])
        cb.dml(lambda e, s=s: e.dma_start(out=hhi[:, :], in_=h_hi_d.ap()[s]),
               rd=[SP["h"]], wr=[B["hhi"]])
        cb.dml(lambda e, s=s: e.dma_start(out=hlo[:, :], in_=h_lo_d.ap()[s]),
               rd=[SP["h"]], wr=[B["hlo"]])
        for c in range(NCT):
            def mmo(e, c=c):
                r = None
                for hp in range(8):
                    b_i = hp // 3
                    off = (hp % 3) * 130
                    r = e.matmul(
                        pb[b_i][:, off:off + 130],
                        qb[:, hp * SLAB + c * 128:hp * SLAB + (c + 1) * 128],
                        kvexp[:, hp * 130:(hp + 1) * 130],
                        start=True, stop=True)
                return r
            cb.t(mmo, rd=[B["qb"], B["kvexp"]], wr=[PB[0], PB[1], PB[2]])
            for b_i, nh in ((0, 6), (1, 6), (2, 4)):
                cb.v(lambda e, b_i=b_i, nh=nh: e.tensor_scalar_add(
                    zb[:, b_i * 6:b_i * 6 + nh],
                    ap3(pb[b_i], 64, [[512, 128], [65, nh]]), EPS_Z),
                    rd=[PB[b_i]], wr=[B["zb"]])
            cb.v(lambda e: e.reciprocal(zb[:, 32:48], zb[:, 0:16]),
                 rd=[B["zb"]], wr=[B["zb"]])
            for b_i, nh in ((0, 6), (1, 6), (2, 4)):
                def f(e, c=c, b_i=b_i, nh=nh):
                    o = ap3(xb, c * D + b_i * 6 * 64,
                            [[SL, 128], [64, nh], [1, 64]])
                    i = ap3(pb[b_i], 0, [[512, 128], [65, nh], [1, 64]])
                    i2 = ap3(zb, 32 + b_i * 6, [[64, 128], [1, nh], [0, 64]])
                    return e.tensor_tensor(o, i, i2, OP.mult)
                cb.v(f, rd=[PB[b_i], B["zb"]], wr=[B["xb"]])
        tr_tm_to_fm(xb, B["xb"], wb_hi, WBH[2], wb_lo, WBL[2],
                    (3, 4), dcol0=4096)
        for f in range(KC):
            bank = 5 + f % 2
            mm3_fm(bank, wa_hi, wa_lo, (B["wa_hi"], B["wa_lo"]),
                   wb_hi, wb_lo, (WBH[2], WBL[2]), f, 0, 1024,
                   rcol0=4096)
            cb.a(lambda e, f=f, bank=bank: e.activation(
                xb[:, 0:SLAB], pb[bank][:, 0:SLAB],
                AF.Identity, bias=vcol(C_BO + f)),
                rd=[PB[bank], B["vecs"]], wr=[B["xb"]])
            cb.v(lambda e, f=f: e.tensor_add(
                xb[:, SLAB:2 * SLAB], xb[:, 0:SLAB],
                hhi[:, f * SLAB:(f + 1) * SLAB]),
                rd=[B["xb"], B["hhi"]], wr=[B["xb"]])
            cb.v(lambda e, f=f: e.tensor_add(
                dst[:, f * SLAB:(f + 1) * SLAB], xb[:, SLAB:2 * SLAB],
                hlo[:, f * SLAB:(f + 1) * SLAB]),
                rd=[B["xb"], B["hlo"]], wr=[dst_b])

    def bc1_P(s, srcbuf, src_b):
        fm_apply(srcbuf, src_b, C_G1, vecs, B["vecs"], 80, C_BE1,
                 x1b, B["x1b"], tb, B["tb"])
        cb.a(lambda e: e.activation(x1hi[:, 0:SL], x1b[:, :], AF.Identity),
             rd=[B["x1b"]], wr=[WBH[0], WBH[1]])
        cb.v(lambda e: e.tensor_sub(x1lo[:, 0:SL], x1b[:, :],
                                    x1hi[:, 0:SL]),
             rd=[B["x1b"], WBH[0], WBH[1]], wr=[WBL[0], WBL[1]])
        cb.dms(lambda e, s=s: e.dma_start(out=x1_hi_d.ap()[s],
                                          in_=x1hi[:, 0:SL]),
               rd=[WBH[0], WBH[1]], wr=[SP["x1"]])
        cb.dms(lambda e, s=s: e.dma_start(out=x1_lo_d.ap()[s],
                                          in_=x1lo[:, 0:SL]),
               rd=[WBL[0], WBL[1]], wr=[SP["x1"]])

    BC1_PIPE = True
    if BC1_PIPE:
        prev = None
        for s in range(NS):
            dst, dst_b = (a2b, B["a2b"]) if s % 2 == 0 else (kb, B["kb"])
            bc1_A(s, dst, dst_b)
            if prev is not None:
                bc1_P(prev[0], prev[1], prev[2])
            fm_stats(dst, dst_b, tb, B["tb"], EPS_LN)
            prev = (s, dst, dst_b)
        bc1_P(prev[0], prev[1], prev[2])
    else:
        for s in range(NS):
            dst, dst_b = (a2b, B["a2b"]) if s % 2 == 0 else (kb, B["kb"])
            bc1_A(s, dst, dst_b)
            fm_stats(dst, dst_b, tb, B["tb"], EPS_LN)
            bc1_P(s, dst, dst_b)

    if stop_after == "bc1":
        return _finish_queues(cb, nc)

    cb.dml(lambda e: e.dma_start(out=tb[0:1, 0:D], in_=g2_row.ap()[:, :]),
           wr=[B["tb"]])
    cb.dml(lambda e: e.dma_start(out=tb[0:1, D:2 * D],
                                 in_=be2_row.ap()[:, :]), wr=[B["tb"]])
    for j in range(2):
        cb.t(lambda e, j=j: e.matmul(
            pb[4][:, 0:SLAB], onesr[:, :],
            tb[0:1, D + j * SLAB:D + (j + 1) * SLAB],
            start=True, stop=True), rd=[B["tb"], B["onesr"]], wr=[PB[4]])
        cb.v(lambda e, j=j: e.tensor_mul(
            kvexp[:, j * SLAB:(j + 1) * SLAB], pb[4][:, 0:SLAB],
            rep[:, R_GA + j * SLAB:R_GA + (j + 1) * SLAB]),
            rd=[PB[4], B["rep"]], wr=[B["kvexp"]])
    for j in range(2):
        cb.t(lambda e, j=j: e.matmul(
            pb[5][:, 0:SLAB], onesr[:, :], tb[0:1, j * SLAB:(j + 1) * SLAB],
            start=True, stop=True), rd=[B["tb"], B["onesr"]], wr=[PB[5]])
        cb.v(lambda e, j=j: e.tensor_mul(
            rep[:, R_GA + j * SLAB:R_GA + (j + 1) * SLAB],
            rep[:, R_GA + j * SLAB:R_GA + (j + 1) * SLAB],
            pb[5][:, 0:SLAB]), rd=[PB[5], B["rep"]], wr=[B["rep"]])

    def w12_off(st, part, col):
        return (st * 2 + part) * 128 * W_COLS + col

    def emit_wload(s, st):
        if s == 0:
            first = st == 0
            load_split(w1_in, FF, 0, st * 512, D, 512, wa_hi, wa_lo,
                       [WAH[0]] + ([B["wa_hi"]] if first else []),
                       [WAL[0]] + ([B["wa_lo"]] if first else []), dcol0=0)
            load_split(w2_in, D, st * 512, 0, 512, 1024, wa_hi, wa_lo,
                       [WAH[1]] + ([B["wa_hi"]] if first else []),
                       [WAL[1]] + ([B["wa_lo"]] if first else []),
                       dcol0=4096)
            for part, src, srcb in ((0, wa_hi, [WAH[0]]),
                                    (1, wa_lo, [WAL[0]])):
                def sp1(e, st=st, part=part, src=src):
                    o = ap3(w12_d, w12_off(st, part, 0),
                            [[W_COLS, 128], [1, 4096]])
                    return e.dma_start(out=o, in_=src[:, 0:4096])
                cb.dms(sp1, rd=srcb, wr=[Bw12[st]])
            for part, src, srcb in ((0, wa_hi, [WAH[1]]),
                                    (1, wa_lo, [WAL[1]])):
                def sp2(e, st=st, part=part, src=src):
                    o = ap3(w12_d, w12_off(st, part, 4096),
                            [[W_COLS, 128], [1, 4096]])
                    return e.dma_start(out=o, in_=src[:, 4096:8192])
                cb.dms(sp2, rd=srcb, wr=[Bw12[st]])
        else:
            for part, dst, dstb in ((0, wa_hi, [WAH[0]]),
                                    (1, wa_lo, [WAL[0]])):
                def ld1(e, st=st, part=part, dst=dst):
                    i = ap3(w12_d, w12_off(st, part, 0),
                            [[W_COLS, 128], [1, 4096]])
                    return e.dma_start(out=dst[:, 0:4096], in_=i)
                cb.dml(ld1, rd=[Bw12[st]], wr=dstb)
            for part, dst, dstb in ((0, wa_hi, [WAH[1]]),
                                    (1, wa_lo, [WAL[1]])):
                def ld2(e, st=st, part=part, dst=dst):
                    i = ap3(w12_d, w12_off(st, part, 4096),
                            [[W_COLS, 128], [1, 4096]])
                    return e.dma_start(out=dst[:, 4096:8192], in_=i)
                cb.dml(ld2, rd=[Bw12[st]], wr=dstb)

    def slab_bufs(s):
        p = s % 2
        Y, BY = (qb, B["qb"]) if p == 0 else (a2b, B["a2b"])
        if p == 0:
            return Y, BY, hhi, hlo, [B["hhi"]], [B["hlo"]]
        return Y, BY, wb_hi, wb_lo, [WBH[0], WBH[1]], [WBL[0], WBL[1]]

    def emit_stripe(s, st):
        Y, BY, xh, xl, Bxh, Bxl = slab_bufs(s)
        for gf in range(4):
            bank = gf % 2
            mm3_fm(bank, wa_hi, wa_lo, ([WAH[0]], [WAL[0]]), xh, xl,
                   (Bxh, Bxl), gf, 0, 512, nkc=8)
            cb.a(lambda e, gf=gf, bank=bank, st=st: e.activation(
                tb[:, gf * SLAB:(gf + 1) * SLAB], pb[bank][:, 0:SLAB],
                AF.Gelu, bias=vcol(C_B1 + st * 4 + gf)),
                rd=[PB[bank], B["vecs"]], wr=[B["tb"]])
            cb.a(lambda e, gf=gf: e.activation(
                wb_hi[:, 4096 + gf * SLAB:4096 + (gf + 1) * SLAB],
                tb[:, gf * SLAB:(gf + 1) * SLAB], AF.Identity),
                rd=[B["tb"]], wr=[WBH[2]])
            cb.v(lambda e, gf=gf: e.tensor_sub(
                wb_lo[:, 4096 + gf * SLAB:4096 + (gf + 1) * SLAB],
                tb[:, gf * SLAB:(gf + 1) * SLAB],
                wb_hi[:, 4096 + gf * SLAB:4096 + (gf + 1) * SLAB]),
                rd=[B["tb"], WBH[2]], wr=[WBL[2]])
        for yo in range(KC):
            bank = (2, 3, 6, 7)[yo % 4]
            mm3_fm(bank, wa_hi, wa_lo, ([WAH[1]], [WAL[1]]),
                   wb_hi, wb_lo, (WBH[2], WBL[2]), yo, 4096, 1024,
                   nkc=4, rcol0=4096, rstep=SLAB)
            if st == 0:
                cb.a(lambda e, yo=yo, bank=bank, Y=Y, BY=BY: e.activation(
                    Y[:, yo * SLAB:(yo + 1) * SLAB], pb[bank][:, 0:SLAB],
                    AF.Identity, bias=vcol(C_B2 + yo)),
                    rd=[PB[bank], B["vecs"]], wr=[BY])
            else:
                cb.v(lambda e, yo=yo, bank=bank, Y=Y, BY=BY: e.tensor_add(
                    Y[:, yo * SLAB:(yo + 1) * SLAB], pb[bank][:, 0:SLAB],
                    Y[:, yo * SLAB:(yo + 1) * SLAB]),
                    rd=[PB[bank], BY], wr=[BY])

    def bc3_piece(s, pid):
        Y, BY, xh, xl, Bxh, Bxl = slab_bufs(s)
        if pid == 0:
            cb.v(lambda e, Y=Y, xh=xh: e.tensor_add(x1b[:, :], Y[:, :],
                                                    xh[:, 0:SL]),
                 rd=[BY] + Bxh, wr=[B["x1b"]])
            cb.v(lambda e, xl=xl: e.tensor_add(x1b[:, :], x1b[:, :],
                                               xl[:, 0:SL]),
                 rd=[B["x1b"]] + Bxl, wr=[B["x1b"]])
        elif pid == 1:
            for f in range(KC):
                bk = 4 + f % 2
                pbk = pb[bk]

                def tr(e, f=f, pbk=pbk):
                    r = None
                    for c in range(NCT):
                        r = e.transpose(
                            pbk[:, c * 128:(c + 1) * 128],
                            x1b[:, f * SLAB + c * 128:
                                f * SLAB + (c + 1) * 128],
                            ident[:, :])
                    return r
                cb.t(tr, rd=[B["x1b"], B["ident_s"]], wr=[PB[bk]])

                def ev(e, f=f, pbk=pbk):
                    o = ap3(kb, f * 128, [[SL, 128], [D, NCT], [1, 128]])
                    i = ap3(pbk, 0, [[512, 128], [128, NCT], [1, 128]])
                    return e.activation(o, i, AF.Identity)
                cb.a(ev, rd=[PB[bk]], wr=[B["kb"]])
        elif pid == 2:
            ln_stats_tm(kb, B["kb"], x1b, B["x1b"], m4b, B["m4b"],
                        r4b, B["r4b"], 0, EPS_LN)
        elif pid == 3:
            ln_apply_tm(kb, B["kb"], kb, B["kb"], m4b, B["m4b"],
                        r4b, B["r4b"], 0, 4)

            def gm(e):
                k3 = ap3(kb, 0, [[SL, 128], [D, NCT], [1, D]])
                r3 = ap3(rep, R_GA, [[4 * D, 128], [0, NCT], [1, D]])
                return e.tensor_tensor(k3, k3, r3, OP.mult)
            cb.v(gm, rd=[B["kb"], B["rep"]], wr=[B["kb"]])
        elif pid == 4:
            def gab(e):
                k3 = ap3(kb, 0, [[SL, 128], [D, NCT], [1, D]])
                g3 = ap3(kvexp, 0, [[8 * 130, 128], [0, NCT], [1, D]])
                return e.tensor_tensor(k3, k3, g3, OP.add)
            cb.v(gab, rd=[B["kb"], B["kvexp"]], wr=[B["kb"]])
            cb.v(lambda e: e.tensor_add(kb[:, :], kb[:, :], xb[:, :]),
                 rd=[B["kb"], B["xb"]], wr=[B["kb"]])
            cb.dms(lambda e, s=s: e.dma_start(out=x2_d.ap()[s],
                                              in_=kb[:, :]),
                   rd=[B["kb"]], wr=[SP["x2"]])
        elif pid == 5:
            ln_stats_tm(kb, B["kb"], x1b, B["x1b"], m1b, B["m1b"],
                        r1b, B["r1b"], 0, EPS_LN)
        elif pid == 6:
            ln_apply_tm(kb, B["kb"], kb, B["kb"], m1b, B["m1b"],
                        r1b, B["r1b"], 0, 32)

            def cm(e):
                k3 = ap3(kb, 0, [[SL, 128], [D, NCT], [1, D]])
                r3 = ap3(rep, R_CB1, [[4 * D, 128], [0, NCT], [1, D]])
                return e.tensor_tensor(k3, k3, r3, OP.mult)
            cb.v(cm, rd=[B["kb"], B["rep"]], wr=[B["kb"]])

            def sm(e):
                k3 = ap3(kb, 0, [[SL, 128], [D, NCT], [1, D]])
                r3 = ap3(rep, R_SB, [[4 * D, 128], [0, NCT], [1, D]])
                return e.tensor_tensor(k3, k3, r3, OP.add)
            cb.v(sm, rd=[B["kb"], B["rep"]], wr=[B["kb"]])
        elif pid == 7:
            sA = ap3(kb, 0, [[SL, 128], [16, 256], [1, 16]])
            cb.v(lambda e, sA=sA: e.tensor_reduce(
                ub[:, 0:256], sA, AX.X, OP.max, apply_absolute_value=True),
                rd=[B["kb"]], wr=[B["ub"]])
            cb.v(lambda e: e.tensor_scalar(ub[:, 256:512], ub[:, 0:256],
                                           r6c[:, 0:1], 1e-12, OP.mult,
                                           OP.max),
                 rd=[B["ub"], B["r6c"]], wr=[B["ub"]])
            cb.v(lambda e: e.reciprocal(ub[:, 0:256], ub[:, 256:512]),
                 rd=[B["ub"]], wr=[B["ub"]])
            cb.a(lambda e, Y=Y: e.activation(Y[:, :], kb[:, :], AF.Abs),
                 rd=[B["kb"]], wr=[BY])
            v3 = ap3(Y, 0, [[SL, 128], [16, 256], [1, 16]])
            rsb = ap3(ub, 0, [[SLAB, 128], [1, 256], [0, 16]])
            cb.v(lambda e, v3=v3, rsb=rsb: e.tensor_tensor(v3, v3, rsb,
                                                           OP.mult),
                 rd=[BY, B["ub"]], wr=[BY])
        elif pid == 8:
            vf = ap3(Y, 0, [[SL, 128], [1, SL]])
            tf = ap3(x1b, 0, [[SL, 128], [1, SL]])
            cb.v(lambda e, vf=vf, tf=tf: e.tensor_single_scalar(
                tf, vf, 0.25, OP.is_gt), rd=[BY], wr=[B["x1b"]])
            for thr in (0.75, 1.25, 1.75):
                cb.v(lambda e, vf=vf, tf=tf, thr=thr: e.scalar_tensor_tensor(
                    tf, vf, thr, tf, OP.is_gt, OP.add),
                    rd=[BY, B["x1b"]], wr=[B["x1b"]])
        elif pid == 9:
            vf = ap3(Y, 0, [[SL, 128], [1, SL]])
            kf = ap3(xb, 0, [[SL, 128], [1, SL]])
            cb.v(lambda e, vf=vf, kf=kf: e.tensor_single_scalar(
                kf, vf, 2.5, OP.is_gt), rd=[BY], wr=[B["xb"]])
            for thr in (3.5, 5.0, 5.0):
                cb.v(lambda e, vf=vf, kf=kf, thr=thr: e.scalar_tensor_tensor(
                    kf, vf, thr, kf, OP.is_gt, OP.add),
                    rd=[BY, B["xb"]], wr=[B["xb"]])
        elif pid == 10:
            vf = ap3(Y, 0, [[SL, 128], [1, SL]])
            tf = ap3(x1b, 0, [[SL, 128], [1, SL]])
            kf = ap3(xb, 0, [[SL, 128], [1, SL]])
            cb.v(lambda e, tf=tf, kf=kf: e.scalar_tensor_tensor(
                tf, tf, 0.5, kf, OP.mult, OP.add),
                rd=[B["x1b"], B["xb"]], wr=[B["x1b"]])
            cb.a(lambda e, Y=Y: e.activation(Y[:, :], kb[:, :], AF.Sign),
                 rd=[B["kb"]], wr=[BY])
            cb.v(lambda e, tf=tf, vf=vf: e.tensor_tensor(tf, tf, vf,
                                                         OP.mult),
                 rd=[B["x1b"], BY], wr=[B["x1b"]])
            t3 = ap3(x1b, 0, [[SL, 128], [16, 256], [1, 16]])
            d3 = ap3(kb, 0, [[SL, 128], [16, 256], [1, 16]])
            ssb = ap3(ub, 256, [[SLAB, 128], [1, 256], [0, 16]])
            cb.v(lambda e, t3=t3, d3=d3, ssb=ssb: e.tensor_tensor(
                d3, t3, ssb, OP.mult), rd=[B["x1b"], B["ub"]], wr=[B["kb"]])
        elif pid == 11:
            tr_tm_to_fm(kb, B["kb"], wa_hi, WAH[1], wa_lo, WAL[1], (4, 5),
                        dcol0=4096)
            cb.dms(lambda e, s=s: e.dma_start(out=xq_hi_d.ap()[s],
                                              in_=wa_hi[:, 4096:4096 + SL]),
                   rd=[WAH[1]], wr=[SP["xq"]])
            cb.dms(lambda e, s=s: e.dma_start(out=xq_lo_d.ap()[s],
                                              in_=wa_lo[:, 4096:4096 + SL]),
                   rd=[WAL[1]], wr=[SP["xq"]])

    PIECE_MAP = {0: (0, 1, 2), 1: (3,), 2: (4,), 3: (5, 6), 4: (7,),
                 5: (8,), 6: (9, 10), 7: (11,)}

    cb.dml(lambda e: e.dma_start(out=hhi[:, :], in_=x1_hi_d.ap()[0]),
           rd=[SP["x1"]], wr=[B["hhi"]])
    cb.dml(lambda e: e.dma_start(out=hlo[:, :], in_=x1_lo_d.ap()[0]),
           rd=[SP["x1"]], wr=[B["hlo"]])
    emit_wload(0, 0)
    for s in range(NS):
        for st in range(8):
            emit_stripe(s, st)
            if s >= 1:
                for pid in PIECE_MAP[st]:
                    bc3_piece(s - 1, pid)
            if st < 7:
                emit_wload(s, st + 1)
            elif s + 1 < NS:
                emit_wload(s + 1, 0)
            if st == 1 and s + 1 < NS:
                _, _, nh, nl, nhb, nlb = slab_bufs(s + 1)
                cb.dml(lambda e, s=s, nh=nh: e.dma_start(
                    out=nh[:, 0:SL], in_=x1_hi_d.ap()[s + 1]),
                    rd=[SP["x1"]], wr=nhb)
                cb.dml(lambda e, s=s, nl=nl: e.dma_start(
                    out=nl[:, 0:SL], in_=x1_lo_d.ap()[s + 1]),
                    rd=[SP["x1"]], wr=nlb)
            if st == 6:
                def ldx2(e, s=s):
                    o = ap3(xb, 0, [[SL, 128], [D, NCT], [1, D]])
                    i = ap3(xin, s * SLAB * D,
                            [[D, 128], [128 * D, NCT], [1, D]])
                    return e.dma_start(out=o, in_=i)
                cb.dml(ldx2, wr=[B["xb"]])
    for pid in range(12):
        bc3_piece(NS - 1, pid)

    if stop_after in ("bc2", "bc3"):
        return _finish_queues(cb, nc)

    STRIPES = STRIPES4
    NST4 = len(STRIPES)

    def w4_bufs(mat):
        if mat == 0:
            return wa_hi, wa_lo, 0, [WAH[0]], [WAL[0]]
        if mat == 1:
            return wa_hi, wa_lo, 4096, [WAH[1]], [WAL[1]]
        return wb_hi, wb_lo, 0, [WBH[0], WBH[1]], [WBL[0], WBL[1]]

    def emit_w4load_one(s, sti, mat):
        i0, isz = STRIPES[sti]
        nic = isz // 128
        ncols = nic * 1024 if mat == 2 else KC * isz
        hi_t, lo_t, dc0, hib, lob = w4_bufs(mat)
        first = s == 0 and sti == 0 and mat == 0
        wh = hib + ([B["wa_hi"]] if first else [])
        wl = lob + ([B["wa_lo"]] if first else [])
        if s == 0:
            def ldc(e, mat=mat, dc0=dc0, ncols=ncols, sti=sti, hi_t=hi_t):
                o = ap3(hi_t, dc0, [[W_COLS, 128], [1, ncols]])
                i = ap3(w4_d, (mat * 6 + sti) * 128 * 4096,
                        [[4096, 128], [1, ncols]])
                return e.dma_start(out=o, in_=i)
            cb.dms(ldc, rd=[Bw4], wr=wh)

            def ldf(e, mat=mat, ncols=ncols, sti=sti):
                o = ap3(xb, 0, [[SL, 128], [1, ncols]])
                i = ap3(w4_d, (mat * 6 + sti) * 128 * 4096,
                        [[4096, 128], [1, ncols]])
                return e.dma_start(out=o, in_=i)
            cb.dml(ldf, rd=[Bw4], wr=[B["xb"]])
            cb.v(lambda e, dc0=dc0, ncols=ncols, hi_t=hi_t, lo_t=lo_t:
                 e.tensor_sub(
                     ap3(lo_t, dc0, [[W_COLS, 128], [1, ncols]]),
                     ap3(xb, 0, [[SL, 128], [1, ncols]]),
                     ap3(hi_t, dc0, [[W_COLS, 128], [1, ncols]])),
                 rd=[B["xb"]] + hib, wr=wl)
            for part, src, srcb in ((0, hi_t, hib), (1, lo_t, lob)):
                def sp(e, mat=mat, sti=sti, part=part, src=src,
                       dc0=dc0, ncols=ncols):
                    o = ap3(w4b_d,
                            ((mat * 6 + sti) * 2 + part) * 128 * 4096,
                            [[4096, 128], [1, ncols]])
                    return e.dma_start(
                        out=o, in_=ap3(src, dc0,
                                       [[W_COLS, 128], [1, ncols]]))
                cb.dms(sp, rd=srcb, wr=[Bw4b[mat * 6 + sti]])
        else:
            for part, dst, dstb in ((0, hi_t, wh), (1, lo_t, wl)):
                def ldb(e, mat=mat, sti=sti, part=part, dst=dst,
                        dc0=dc0, ncols=ncols):
                    o = ap3(dst, dc0, [[W_COLS, 128], [1, ncols]])
                    i = ap3(w4b_d,
                            ((mat * 6 + sti) * 2 + part) * 128 * 4096,
                            [[4096, 128], [1, ncols]])
                    return e.dma_start(out=o, in_=i)
                cb.dms(ldb, rd=[Bw4b[mat * 6 + sti]], wr=dstb)

    def h_slot(sti):
        return ((a2b, 0, B["a2b"]), (a2b, 2048, B["a2b"]),
                (qb, 0, B["qb"]))[sti % 3]

    def bc4_p1(s, sti):
        i0, isz = STRIPES[sti]
        hb, ho, hbb = h_slot(sti)
        for c in range(NCT):
            def mmg(e, c=c, isz=isz):
                r = None
                n = 0
                for (wsrc, xsrc) in ((wa_hi, hhi), (wa_lo, hhi),
                                     (wa_hi, hlo)):
                    for kc in range(KC):
                        r = e.matmul(
                            pb[c][:, 0:isz],
                            xsrc[:, kc * SLAB + c * 128:
                                 kc * SLAB + (c + 1) * 128],
                            wsrc[:, kc * isz:(kc + 1) * isz],
                            start=(n == 0), stop=(n == 3 * KC - 1))
                        n += 1
                return r
            cb.t(mmg, rd=[WAH[0], WAL[0], B["hhi"], B["hlo"]], wr=[PB[c]])
        for c in range(NCT):
            def mmu(e, c=c, isz=isz):
                r = None
                n = 0
                for (wsrc, xsrc) in ((wa_hi, hhi), (wa_lo, hhi),
                                     (wa_hi, hlo)):
                    for kc in range(KC):
                        r = e.matmul(
                            pb[4 + c][:, 0:isz],
                            xsrc[:, kc * SLAB + c * 128:
                                 kc * SLAB + (c + 1) * 128],
                            wsrc[:, 4096 + kc * isz:4096 + (kc + 1) * isz],
                            start=(n == 0), stop=(n == 3 * KC - 1))
                        n += 1
                return r
            cb.t(mmu, rd=[WAH[1], WAL[1], B["hhi"], B["hlo"]],
                 wr=[PB[4 + c]])
        for c in range(NCT):
            cb.a(lambda e, c=c, isz=isz: e.activation(
                kb[:, 2048 + c * 512:2048 + c * 512 + isz],
                pb[c][:, 0:isz], AF.Silu),
                rd=[PB[c]], wr=[KB1])
        for c in range(NCT):
            cb.v(lambda e, c=c, isz=isz, hb=hb, ho=ho: e.tensor_mul(
                hb[:, ho + c * isz:ho + (c + 1) * isz],
                kb[:, 2048 + c * 512:2048 + c * 512 + isz],
                pb[4 + c][:, 0:isz]),
                rd=[KB1, PB[4 + c]], wr=[hbb])

    def bc4_p1q(s, sti):
        i0, isz = STRIPES[sti]
        hb, ho, hbb = h_slot(sti)
        quant_tm(hb, hbb, SL, hb, hbb, SL, ho, NCT * isz,
                 tb, B["tb"], SL, 0, tb, B["tb"], SL, 2048,
                 kb, KB0, SL, 0, sc_off=0)

    def bc4_p2(s, sti):
        i0, isz = STRIPES[sti]
        nic = isz // 128
        last = sti == NST4 - 1
        hb, ho, hbb = h_slot(sti)
        for c in range(NCT):
            pbk = pb[c % 2]

            def trh(e, c=c, nic=nic, pbk=pbk, hb=hb, ho=ho, isz=isz):
                r = None
                for ic in range(nic):
                    r = e.transpose(
                        pbk[:, ic * 128:(ic + 1) * 128],
                        hb[:, ho + c * isz + ic * 128:
                           ho + c * isz + (ic + 1) * 128],
                        ident[:, :])
                return r
            cb.t(trh, rd=[hbb, B["ident_s"]], wr=[PB[c % 2]])
            hqc = 4096 + c * 512
            cb.a(lambda e, hqc=hqc, nic=nic, pbk=pbk: e.activation(
                wb_hi[:, hqc:hqc + nic * 128], pbk[:, 0:nic * 128],
                AF.Identity), rd=[PB[c % 2]], wr=[WBH[2]])
            cb.v(lambda e, hqc=hqc, nic=nic, pbk=pbk: e.tensor_sub(
                wb_lo[:, hqc:hqc + nic * 128], pbk[:, 0:nic * 128],
                wb_hi[:, hqc:hqc + nic * 128]),
                rd=[PB[c % 2], WBH[2]], wr=[WBL[2]])
            for no in range(2):
                bank = 2 + no

                def mmd(e, c=c, no=no, bank=bank, nic=nic, hqc=hqc):
                    r = None
                    n = 0
                    for (wsrc, xsrc) in ((wb_hi, wb_hi), (wb_lo, wb_hi),
                                         (wb_hi, wb_lo)):
                        for ic in range(nic):
                            r = e.matmul(
                                pb[bank][:, 0:512],
                                xsrc[:, hqc + ic * 128:
                                     hqc + (ic + 1) * 128],
                                wsrc[:, ic * 1024 + no * 512:
                                     ic * 1024 + (no + 1) * 512],
                                start=(n == 0), stop=(n == 3 * nic - 1))
                            n += 1
                    return r
                cb.t(mmd, rd=[WBH[0], WBH[1], WBL[0], WBL[1],
                              WBH[2], WBL[2]], wr=[PB[bank]])
                if sti == 0:
                    cb.v(lambda e, c=c, no=no, bank=bank: e.tensor_copy(
                        x1b[:, c * D + no * 512:c * D + (no + 1) * 512],
                        pb[bank][:, 0:512]),
                        rd=[PB[bank]], wr=[B["x1b"]])
                elif not last:
                    cb.v(lambda e, c=c, no=no, bank=bank: e.tensor_add(
                        x1b[:, c * D + no * 512:c * D + (no + 1) * 512],
                        pb[bank][:, 0:512],
                        x1b[:, c * D + no * 512:c * D + (no + 1) * 512]),
                        rd=[PB[bank], B["x1b"]], wr=[B["x1b"]])
                else:
                    cb.v(lambda e, c=c, no=no, bank=bank: e.tensor_add(
                        t1s[:, :], pb[bank][:, 0:512],
                        x1b[:, c * D + no * 512:c * D + (no + 1) * 512]),
                        rd=[PB[bank], B["x1b"]], wr=[B["t1s"]])
                    cb.v(lambda e, no=no: e.tensor_mul(
                        t2s[:, :], t1s[:, :],
                        rep[:, R_GB + no * 512:R_GB + (no + 1) * 512]),
                        rd=[B["t1s"], B["rep"]], wr=[B["t2s"]])
                    cb.v(lambda e, c=c, no=no: e.tensor_add(
                        x1b[:, c * D + no * 512:c * D + (no + 1) * 512],
                        t2s[:, :],
                        xb[:, c * D + no * 512:c * D + (no + 1) * 512]),
                        rd=[B["t2s"], B["xb"]], wr=[B["x1b"]])
        if last:
            def stout(e, s=s):
                oo = ap3(out, s * SLAB * D,
                         [[D, 128], [128 * D, NCT], [1, D]])
                i = ap3(x1b, 0, [[SL, 128], [D, NCT], [1, D]])
                return e.dma_start(out=oo, in_=i)
            cb.dms(stout, rd=[B["x1b"]], wr=[SP["out"]])

    def xq_load(s):
        cb.dml(lambda e, s=s: e.dma_start(out=hhi[:, :],
                                          in_=xq_hi_d.ap()[s]),
               rd=[SP["xq"]], wr=[B["hhi"]])
        cb.dml(lambda e, s=s: e.dma_start(out=hlo[:, :],
                                          in_=xq_lo_d.ap()[s]),
               rd=[SP["xq"]], wr=[B["hlo"]])

    jobs = [(s, sti) for s in range(NS) for sti in range(NST4)]
    cb.v(lambda e: e.memset(kb[:, 0:1], 0.0), wr=[KB0, KB1, B["kb"]])
    for mat in range(3):
        emit_w4load_one(0, 0, mat)
    xq_load(0)
    bc4_p1(*jobs[0])
    bc4_p1q(*jobs[0])
    for mat in range(2):
        emit_w4load_one(*jobs[1], mat)
    bc4_p1(*jobs[1])
    bc4_p1q(*jobs[1])
    for idx, cur in enumerate(jobs):
        n1 = jobs[idx + 1] if idx + 1 < len(jobs) else None
        n2 = jobs[idx + 2] if idx + 2 < len(jobs) else None
        if n2 is not None:
            if n2[1] == 0:
                xq_load(n2[0])
            for mat in range(2):
                emit_w4load_one(*n2, mat)
            bc4_p1(*n2)
        if cur[1] == NST4 - 1:
            cb.dml(lambda e, s=cur[0]: e.dma_start(out=xb[:, :],
                                                   in_=x2_d.ap()[s]),
                   rd=[SP["x2"]], wr=[B["xb"]])
        bc4_p2(*cur)
        if n2 is not None:
            bc4_p1q(*n2)
        if n1 is not None:
            emit_w4load_one(*n1, 2)

    return _finish_queues(cb, nc)

def _finish_queues(cb, nc):
    fin = [(cb.sem[f"gpsimd_d{j}"], cb.cnt[f"gpsimd_d{j}"])
           for j in range(NPOOL) if cb.cnt[f"gpsimd_d{j}"]]

    def _final(e, fin=fin):
        for s, v in fin:
            e.wait_ge(s, v)
    cb.q["gpsimd"].append(_final)

    from contextlib import ExitStack
    with ExitStack() as ctx:
        block = ctx.enter_context(nc.Block())

        @block.sync
        def _(sync):
            for g in cb.q["sync"]:
                g(sync)

        @block.scalar
        def _(scalar):
            for g in cb.q["scalar"]:
                g(scalar)

        @block.vector
        def _(vector):
            for g in cb.q["vector"]:
                g(vector)

        @block.tensor
        def _(tensor):
            for g in cb.q["tensor"]:
                g(tensor)

        @block.gpsimd
        def _(gpsimd):
            for g in cb.q["gpsimd"]:
                g(gpsimd)

    return nc


def make_in_maps(inputs):
    x = np.ascontiguousarray(np.asarray(inputs["L0_attn_src"],
                                        dtype=np.float32))
    c = np.asarray(inputs["L0_c"], dtype=np.float32)
    ident = np.eye(128, dtype=np.float32)

    def fm(v, k):
        return np.ascontiguousarray(
            np.asarray(v, dtype=np.float32).reshape(k, 128).T)

    def cc(v):
        return np.ascontiguousarray(np.asarray(v, dtype=np.float32))

    common = {
        "wmod": cc(inputs["L0_wmod"]),
        "wq": cc(inputs["L0_attn_Wq"]), "wk": cc(inputs["L0_attn_Wk"]),
        "wv": cc(inputs["L0_attn_Wv"]), "wo": cc(inputs["L0_attn_Wo"]),
        "w1": cc(inputs["L0_attn_W1"]), "w2": cc(inputs["L0_attn_W2"]),
        "wg": cc(inputs["L0_ffn_w_gate"]), "wu": cc(inputs["L0_ffn_w_up"]),
        "wd": cc(inputs["L0_ffn_w_down"]),
        "bq_fm": fm(inputs["L0_attn_bq"], 8),
        "bk_row": cc(np.asarray(inputs["L0_attn_bk"]).reshape(1, D)),
        "bv_row": cc(np.asarray(inputs["L0_attn_bv"]).reshape(1, D)),
        "bo_fm": fm(inputs["L0_attn_bo"], 8),
        "b1_fm": fm(inputs["L0_attn_b1"], 32),
        "b2_fm": fm(inputs["L0_attn_b2"], 8),
        "g1_fm": fm(inputs["L0_attn_g1"], 8),
        "be1_fm": fm(inputs["L0_attn_be1"], 8),
        "g2_fm": fm(inputs["L0_attn_g2"], 8),
        "be2_fm": fm(inputs["L0_attn_be2"], 8),
        "g2_row": cc(np.asarray(inputs["L0_attn_g2"]).reshape(1, D)),
        "be2_row": cc(np.asarray(inputs["L0_attn_be2"]).reshape(1, D)),
        "ident": ident,
    }
    in_maps = []
    for core in range(NCORES):
        b = core // 4
        t0 = (core % 4) * T
        m = dict(common)
        m["xin"] = np.ascontiguousarray(x[b, t0:t0 + T])
        m["cfm"] = fm(c[b], 8)
        in_maps.append(m)
    return in_maps


def kernel(**inputs):
    nc = build_program()
    in_maps = make_in_maps(inputs)
    res = run_bass_kernel_spmd(nc, in_maps, core_ids=list(range(NCORES)))
    full = np.empty((2, 16384, D), dtype=np.float32)
    for core in range(NCORES):
        b = core // 4
        t0 = (core % 4) * T
        full[b, t0:t0 + T] = res.results[core]["out"]
    return full
